# version 09
# baseline (speedup 1.0000x reference)
import numpy as np
from contextlib import ExitStack
import concourse.bass as bass
import concourse.mybir as mybir
from concourse.bass_utils import run_bass_kernel_spmd

F32 = mybir.dt.float32
BF16 = mybir.dt.bfloat16
AF = mybir.ActivationFunctionType
ALU = mybir.AluOpType

NCORES = 8
TC = 1024
TH = TC + 3
D = 2048
DCH = 16
H = 32
P = 64
NS = 128
INTER = 8192
ICH = 64
L = 128
NCH = 8
PROJ = 12320
XS0, Z0, B0r, C0r, DT0 = 0, 16, 32, 64, 96

DEBUG_PHASES = 99
DEBUG_TAPS = ()
DEBUG_NO_EPI = False
DEBUG_2C_LVL = 9
DEBUG_BF_STAGES = 4
DEBUG_UG_LVL = 9


def _sylvester(n):
    h = np.array([[1.0]], dtype=np.float32)
    while h.shape[0] < n:
        h = np.block([[h, h], [h, -h]])
    return np.ascontiguousarray(h.astype(np.float32))


def _host_prep(inputs):
    x = np.asarray(inputs["L0_attn_x"], dtype=np.float32)[0]
    xT = np.ascontiguousarray(x.T)
    xpad = np.concatenate([np.zeros((D, 3), np.float32), xT], axis=1)

    WinT = np.asarray(inputs["L0_attn_Win"], np.float32).T
    WoutT = np.asarray(inputs["L0_attn_Wout"], np.float32).T
    WupT = np.asarray(inputs["L0_ffn_W_up"], np.float32).T
    WgtT = np.asarray(inputs["L0_ffn_W_gate"], np.float32).T
    Wp1 = np.ascontiguousarray(
        WinT[:, :PROJ - 32].reshape(4, 4, 128, 48, 256)
        .transpose(3, 0, 2, 1, 4).reshape(48 * 4 * 128, 1024))
    WdtP = np.ascontiguousarray(
        WinT[:, PROJ - 32:].reshape(16, 128, 32).transpose(1, 0, 2).reshape(128, 512))
    Wp2 = np.ascontiguousarray(
        WoutT.reshape(4, 4, 128, 8, 256).transpose(3, 0, 2, 1, 4).reshape(4096, 1024))
    WupP = np.ascontiguousarray(
        WupT.reshape(16, 128, 64, 128).transpose(2, 1, 0, 3).reshape(INTER, D))
    WgtP = np.ascontiguousarray(
        WgtT.reshape(16, 128, 64, 128).transpose(2, 1, 0, 3).reshape(INTER, D))
    WdT = np.asarray(inputs["L0_ffn_W_down"], np.float32).T
    orig = (np.arange(64)[None, :] * 128 + np.arange(128)[:, None]).reshape(-1)
    WdP = np.ascontiguousarray(WdT[orig])

    cwx = np.asarray(inputs["L0_attn_conv_w_x"], np.float32).reshape(D, 4)
    cwx_t = np.ascontiguousarray(cwx.reshape(16, 128, 4).transpose(1, 0, 2).reshape(128, 64))
    cwb = np.asarray(inputs["L0_attn_conv_w_b"], np.float32).reshape(4096, 4)
    cwb_t = np.ascontiguousarray(cwb.reshape(32, 128, 4).transpose(1, 0, 2).reshape(128, 128))
    cwc = np.asarray(inputs["L0_attn_conv_w_c"], np.float32).reshape(4096, 4)
    cwc_t = np.ascontiguousarray(cwc.reshape(32, 128, 4).transpose(1, 0, 2).reshape(128, 128))

    A_log = np.ascontiguousarray(np.asarray(inputs["L0_attn_A_log"], np.float32).reshape(H, 1))
    dt_bias = np.ascontiguousarray(np.asarray(inputs["L0_attn_dt_bias"], np.float32).reshape(H, 1))

    suug_t = np.ascontiguousarray(np.asarray(inputs["L0_ffn_SU_ug"], np.float32).reshape(16, 128).T)
    sudp_t = np.ascontiguousarray(np.asarray(inputs["L0_ffn_SU_dp"], np.float32).reshape(64, 128).T)
    wsug = np.asarray(inputs["L0_ffn_Wscale_ug"], np.float32)
    wsup_t = np.ascontiguousarray(wsug[:INTER].reshape(64, 128).T)
    wsgt_t = np.ascontiguousarray(wsug[INTER:].reshape(64, 128).T)
    wsdp_t = np.ascontiguousarray(
        np.asarray(inputs["L0_ffn_Wscale_dp"], np.float32).reshape(16, 128).T)

    bo = np.zeros((128, 16 * 32), np.float32)
    sel = np.zeros((32, 16 * 128), np.float32)
    for j in range(16):
        for p in range(128):
            h = 2 * j + (1 if p >= 64 else 0)
            bo[p, j * 32 + h] = 1.0
            sel[h, j * 128 + p] = 1.0

    shared = {
        "Wp1": Wp1, "WdtP": WdtP, "Wp2": Wp2, "WupP": WupP, "WgtP": WgtP, "WdP": WdP,
        "cwx_t": cwx_t, "cwb_t": cwb_t, "cwc_t": cwc_t,
        "A_log": A_log, "dt_bias": dt_bias,
        "suug_t": suug_t, "sudp_t": sudp_t,
        "wsup_t": wsup_t, "wsgt_t": wsgt_t, "wsdp_t": wsdp_t,
        "H128": _sylvester(128), "H64": _sylvester(64),
        "H64blk": np.ascontiguousarray(np.kron(np.eye(2, dtype=np.float32), _sylvester(64))),
        "IDENT": np.eye(128, dtype=np.float32),
        "TRIU": np.ascontiguousarray(np.triu(np.ones((128, 128), np.float32))),
        "BO": np.ascontiguousarray(bo), "SEL": np.ascontiguousarray(sel),
    }
    in_maps = []
    for k in range(NCORES):
        m = dict(shared)
        m["xin"] = np.ascontiguousarray(xpad[:, TC * k: TC * k + TH])
        MkM = np.zeros((8, 8), np.float32)
        for j in range(8):
            for i in range(8):
                if j < i <= k - 1:
                    MkM[j, i] = 1.0
        m["MkT"] = np.ascontiguousarray(MkM.T)
        cm = np.zeros((1, 256), np.float32)
        for j in range(8):
            if j < k:
                cm[0, j * 32: j * 32 + 32] = 1.0
        m["coremask"] = cm
        in_maps.append(m)
    return in_maps


class Sem:
    def __init__(self, handle):
        self.h = handle
        self.n = 0

    def inc(self, k=1):
        self.n += k
        return self.n


def _build(phases=99, taps=()):
    nc = bass.Bass("TRN2", target_bir_lowering=False)
    taps = list(taps)
    tap_outs = {}

    din = {}
    for name, shape in [
        ("xin", [D, TH]), ("Wp1", [48 * 4 * 128, 1024]), ("WdtP", [128, 512]),
        ("Wp2", [4096, 1024]),
        ("WupP", [INTER, D]), ("WgtP", [INTER, D]), ("WdP", [INTER, D]),
        ("cwx_t", [128, 64]), ("cwb_t", [128, 128]), ("cwc_t", [128, 128]),
        ("A_log", [H, 1]), ("dt_bias", [H, 1]),
        ("suug_t", [128, 16]), ("sudp_t", [128, 64]),
        ("wsup_t", [128, 64]), ("wsgt_t", [128, 64]), ("wsdp_t", [128, 16]),
        ("H128", [128, 128]), ("H64", [64, 64]), ("H64blk", [128, 128]),
        ("IDENT", [128, 128]),
        ("TRIU", [128, 128]), ("BO", [128, 512]), ("SEL", [32, 2048]),
        ("MkT", [8, 8]), ("coremask", [1, 256]),
    ]:
        din[name] = nc.dram_tensor(name, shape, F32, kind="ExternalInput").ap()

    out = nc.dram_tensor("out", [D, TC], F32, kind="ExternalOutput").ap()

    z_dram = nc.dram_tensor("z_dram", [D, TC], BF16).ap()
    bc_dram = nc.dram_tensor("bc_dram", [H * 128, TC], BF16).ap()
    cc_dram = nc.dram_tensor("cc_dram", [H * 128, TC], BF16).ap()
    y_dram = nc.dram_tensor("y_dram", [D, TC], BF16).ap()
    s_dram = nc.dram_tensor("s_dram", [D, TC], BF16).ap()
    wtil_dram = nc.dram_tensor("wtil_dram", [INTER, D], BF16).ap()
    ldb_dram = nc.dram_tensor("ldb_dram", [NCH * 32, 128], F32).ap()
    ag_in_h = nc.dram_tensor("ag_in_h", [128, H * P], F32).ap()
    ag_in_s = nc.dram_tensor("ag_in_s", [32, 1], F32).ap()
    ag_out_h = nc.dram_tensor("ag_out_h", [NCORES * 128, H * P], F32,
                              addr_space="Shared").ap()
    ag_out_s = nc.dram_tensor("ag_out_s", [NCORES * 32, 1], F32,
                              addr_space="Shared").ap()
    db_dram = nc.dram_tensor("db_dram", [8, 32], F32).ap()
    pc_dram = nc.dram_tensor("pc_dram", [H, 8], F32).ap()

    EM = {k: [] for k in "tvsgy"}

    def em(eng, fn):
        EM[eng].append(fn)

    with ExitStack() as st:
        SB = lambda n, s, d: st.enter_context(nc.sbuf_tensor(n, s, d))
        SEM = lambda n: Sem(st.enter_context(nc.semaphore(n)))

        dW = SEM("dW")
        dS = SEM("dS")
        dC = SEM("dC")
        tM = SEM("tM")
        vM = SEM("vM")
        sM = SEM("sM")
        cS = SEM("cS")

        cwxS = SB("cwxS", [128, 64], F32)
        cwbS = SB("cwbS", [128, 128], F32)
        cwcS = SB("cwcS", [128, 128], F32)
        Acol = SB("Acol", [H, 1], F32)
        dtb = SB("dtb", [H, 1], F32)
        suugS = SB("suugS", [128, 16], F32)
        sudpS = SB("sudpS", [128, 64], F32)
        wsupS = SB("wsupS", [128, 64], F32)
        wsgtS = SB("wsgtS", [128, 64], F32)
        wsdpS = SB("wsdpS", [128, 16], F32)
        upscale = SB("upscale", [128, 64], F32)
        gtscale = SB("gtscale", [128, 64], F32)
        dnscale = SB("dnscale", [128, 16], F32)
        h128b = SB("h128b", [128, 128], BF16)
        h128su = SB("h128su", [128, 16 * 128], BF16)
        h64b = SB("h64b", [64, 64], BF16)
        h64blkb = SB("h64blkb", [128, 128], BF16)
        idb = SB("idb", [128, 128], BF16)
        idf = SB("idf", [128, 128], F32)
        ones1b = SB("ones1b", [1, 128], BF16)
        ones1f = SB("ones1f", [1, 128], F32)
        onescolb = SB("onescolb", [128, 1], BF16)
        zeros32 = SB("zeros32", [32, 128], F32)
        epsln = SB("epsln", [1, 1], F32)
        epsrms = SB("epsrms", [32, 1], F32)

        PB = [st.enter_context(nc.psum_tensor(f"pb{i}", [128, 512], F32)) for i in range(8)]
        PBb = [PB[i].ap().bitcast(BF16) for i in range(8)]

        pxn = ExitStack()
        xn = pxn.enter_context(nc.sbuf_tensor("xn", [128, DCH * TH], BF16))
        pk2 = ExitStack()
        xcT = pk2.enter_context(nc.sbuf_tensor("xcT", [128, NCH * 16 * 128], BF16))
        snaps = pk2.enter_context(nc.sbuf_tensor("snaps", [128, NCH * 2048], BF16))
        hfull = pk2.enter_context(nc.sbuf_tensor("hfull", [128, 2048], F32))
        dtT_all = pk2.enter_context(nc.sbuf_tensor("dtT_all", [128, NCH * 32], F32))
        ldT_all = pk2.enter_context(nc.sbuf_tensor("ldT_all", [128, NCH * 32], F32))
        ldTneg_all = pk2.enter_context(nc.sbuf_tensor("ldTneg_all", [128, NCH * 32], F32))
        sdtT = pk2.enter_context(nc.sbuf_tensor("sdtT", [32, 8], F32))
        pxc = ExitStack()
        xc = pxc.enter_context(nc.sbuf_tensor("xc", [128, 16 * TC], BF16))
        dtt = pxc.enter_context(nc.sbuf_tensor("dtt", [32, TC], F32))

        p0 = ExitStack()
        lnrow = p0.enter_context(nc.sbuf_tensor("lnrow", [1, 6 * TH], F32))
        xfs = p0.enter_context(nc.sbuf_tensor("xfs", [128, 2 * TH], F32))
        xbs = p0.enter_context(nc.sbuf_tensor("xbs", [128, 2 * TH], BF16))
        xsq = p0.enter_context(nc.sbuf_tensor("xsq", [128, 2 * TH], BF16))

        const_loads = [
            (cwxS, "cwx_t"), (cwbS, "cwb_t"), (cwcS, "cwc_t"), (dtb, "dt_bias"),
            (suugS, "suug_t"), (sudpS, "sudp_t"), (wsupS, "wsup_t"),
            (wsgtS, "wsgt_t"), (wsdpS, "wsdp_t"), (h128b, "H128"), (h64b, "H64"),
            (h64blkb, "H64blk"), (idb, "IDENT"), (idf, "IDENT"), (Acol, "A_log"),
        ]
        for dst, srcn in const_loads:
            em('g', lambda g, d=dst, s=din[srcn]: g.dma_start(out=d[:], in_=s).then_inc(dW.h, 16))
        dW.inc(16 * len(const_loads))
        n_const = dW.n

        em('s', lambda s: s.wait_ge(dW.h, n_const))
        em('s', lambda s: s.activation(Acol[:], Acol[:], AF.Exp))
        em('s', lambda s: s.activation(Acol[:], Acol[:], AF.Copy, scale=-1.0).then_inc(sM.h, 1))
        sA = sM.inc()

        em('v', lambda v: v.wait_ge(dW.h, n_const))
        em('v', lambda v: v.memset(ones1b[:], 1.0))
        em('v', lambda v: v.memset(ones1f[:], 1.0))
        em('v', lambda v: v.memset(onescolb[:], 1.0))
        em('v', lambda v: v.memset(zeros32[:], 0.0))
        em('v', lambda v: v.memset(epsln[:], 1e-5))
        em('v', lambda v: v.memset(epsrms[:], 1e-6))
        for c in range(16):
            em('v', lambda v, c=c: v.tensor_scalar(
                h128su[:, c * 128:(c + 1) * 128], h128b[:], suugS[:, c:c + 1], None, ALU.mult))
        em('v', lambda v: v.tensor_mul(upscale[:], wsupS[:], sudpS[:]))
        em('v', lambda v: v.tensor_scalar(upscale[:], upscale[:], 1.0 / np.sqrt(2048.0), None, ALU.mult))
        em('v', lambda v: v.tensor_scalar(gtscale[:], wsgtS[:], 1.0 / np.sqrt(2048.0), None, ALU.mult))
        em('v', lambda v: v.tensor_scalar(dnscale[:], wsdpS[:], 1.0 / np.sqrt(8192.0), None,
                                          ALU.mult).then_inc(vM.h, 1))
        vConsts = vM.inc()

        SL = [(0, 343), (343, 342), (685, 342)]
        xb_free = [0, 0]
        for j in range(DCH):
            hb = j % 2
            ld_wait = max(xb_free[hb], 0)
            em('g', lambda g, j=j, hb=hb, w=ld_wait: (
                g.wait_ge(tM.h, w) if w else None,
                g.dma_start(out=xbs[:, hb * TH:hb * TH + TH],
                            in_=din["xin"].rearrange("(c p) t -> c p t", p=128)[j]
                            ).then_inc(dW.h, 16)))
            nl = dW.inc(16)
            em('s', lambda s, hb=hb, nl=nl, w=xb_free[hb]: (
                s.wait_ge(dW.h, nl),
                s.wait_ge(tM.h, w) if w else None,
                s.activation(xsq[:, hb * TH:hb * TH + TH], xbs[:, hb * TH:hb * TH + TH],
                             AF.Square).then_inc(sM.h, 1)))
            nsq = sM.inc()
            em('t', lambda t, nl=nl, nsq=nsq: (t.wait_ge(dW.h, nl), t.wait_ge(sM.h, nsq)))
            for si, (off, w) in enumerate(SL):
                last = (si == len(SL) - 1)
                em('t', lambda t, j=j, hb=hb, si=si, off=off, w=w, last=last: (
                    t.matmul(PB[si].ap()[0:1, 0:w], onescolb[:],
                             xbs[:, hb * TH + off: hb * TH + off + w],
                             start=(j == 0), stop=(j == DCH - 1)),
                    t.matmul(PB[3 + si].ap()[0:1, 0:w], onescolb[:],
                             xsq[:, hb * TH + off: hb * TH + off + w],
                             start=(j == 0), stop=(j == DCH - 1)
                             ).then_inc(tM.h, 1) if last else
                    t.matmul(PB[3 + si].ap()[0:1, 0:w], onescolb[:],
                             xsq[:, hb * TH + off: hb * TH + off + w],
                             start=(j == 0), stop=(j == DCH - 1))))
            xb_free[hb] = tM.inc()
        stats_done = tM.n

        em('v', lambda v: v.wait_ge(tM.h, stats_done))
        for si, (off, w) in enumerate(SL):
            last = (si == len(SL) - 1)
            em('v', lambda v, si=si, off=off, w=w, last=last: (
                v.tensor_scalar(lnrow[0:1, off:off + w], PB[si].ap()[0:1, 0:w],
                                1.0 / 2048.0, None, ALU.mult),
                v.tensor_mul(lnrow[0:1, 2 * TH + off:2 * TH + off + w],
                             lnrow[0:1, off:off + w], lnrow[0:1, off:off + w]),
                (v.scalar_tensor_tensor(lnrow[0:1, 3 * TH + off:3 * TH + off + w],
                                        PB[3 + si].ap()[0:1, 0:w], 1.0 / 2048.0,
                                        lnrow[0:1, 2 * TH + off:2 * TH + off + w],
                                        ALU.mult, ALU.subtract).then_inc(vM.h, 1)
                 if last else
                 v.scalar_tensor_tensor(lnrow[0:1, 3 * TH + off:3 * TH + off + w],
                                        PB[3 + si].ap()[0:1, 0:w], 1.0 / 2048.0,
                                        lnrow[0:1, 2 * TH + off:2 * TH + off + w],
                                        ALU.mult, ALU.subtract))))
        vVar = vM.inc()
        em('s', lambda s: s.wait_ge(vM.h, vVar))
        em('s', lambda s: (
            s.activation(lnrow[0:1, 2 * TH:3 * TH], lnrow[0:1, 3 * TH:4 * TH],
                         AF.Ln, bias=epsln[0:1, 0:1]),
            s.activation(lnrow[0:1, TH:2 * TH], lnrow[0:1, 2 * TH:3 * TH],
                         AF.Exp, scale=-0.5).then_inc(sM.h, 1)))
        vRstd = sM.inc()

        em('t', lambda t: t.wait_ge(sM.h, vRstd))
        for si, (off, w) in enumerate(SL):
            last = (si == len(SL) - 1)
            em('t', lambda t, si=si, off=off, w=w, last=last: (
                t.matmul(PB[si].ap()[:, 0:w], ones1f[:],
                         lnrow[0:1, off:off + w], start=True, stop=True),
                (t.matmul(PB[3 + si].ap()[:, 0:w], ones1f[:],
                          lnrow[0:1, TH + off:TH + off + w], start=True, stop=True
                          ).then_inc(tM.h, 1)
                 if last else
                 t.matmul(PB[3 + si].ap()[:, 0:w], ones1f[:],
                          lnrow[0:1, TH + off:TH + off + w], start=True, stop=True))))
        tBcast = tM.inc()

        xf_free = [0, 0]
        em('v', lambda v: v.wait_ge(tM.h, tBcast))
        for j in range(DCH):
            hb = j % 2
            em('y', lambda y, j=j, hb=hb, w=xf_free[hb]: (
                y.wait_ge(vM.h, w) if w else None,
                y.dma_start(out=xfs[:, hb * TH:hb * TH + TH],
                            in_=din["xin"].rearrange("(c p) t -> c p t", p=128)[j]
                            ).then_inc(dS.h, 16)))
            nl = dS.inc(16)
            em('v', lambda v, nl=nl: v.wait_ge(dS.h, nl))
            for si, (off, w) in enumerate(SL):
                last = (si == len(SL) - 1)
                em('v', lambda v, j=j, hb=hb, si=si, off=off, w=w, last=last: (
                    v.tensor_sub(xfs[:, hb * TH + off:hb * TH + off + w],
                                 xfs[:, hb * TH + off:hb * TH + off + w],
                                 PB[si].ap()[:, 0:w]),
                    (v.tensor_tensor(xn[:, j * TH + off:j * TH + off + w],
                                     xfs[:, hb * TH + off:hb * TH + off + w],
                                     PB[3 + si].ap()[:, 0:w], ALU.mult).then_inc(vM.h, 1)
                     if last else
                     v.tensor_tensor(xn[:, j * TH + off:j * TH + off + w],
                                     xfs[:, hb * TH + off:hb * TH + off + w],
                                     PB[3 + si].ap()[:, 0:w], ALU.mult))))
            xf_free[hb] = vM.inc()
        xn_done_v = vM.n

        p0.close()
        p1 = ExitStack()
        wbuf = p1.enter_context(nc.sbuf_tensor("wbuf", [128, 2 * 4096], BF16))
        wdt = p1.enter_context(nc.sbuf_tensor("wdt", [128, 512], BF16))
        pre = p1.enter_context(nc.sbuf_tensor("pre", [128, 4 * TH], BF16))
        accb = p1.enter_context(nc.sbuf_tensor("accb", [128, 2 * TC], F32))
        spb = p1.enter_context(nc.sbuf_tensor("spb", [128, 3 * TC], BF16))
        dtw = p1.enter_context(nc.sbuf_tensor("dtw", [32, TC], F32))

        mlist = (list(range(XS0, XS0 + 16)) + [DT0] + list(range(B0r, B0r + 32))
                 + list(range(C0r, C0r + 32)) + list(range(Z0, Z0 + 16)))

        bank_free = [0, 0, 0, 0]
        slice_ctr = 0
        wbuf_free = [0, 0]
        pre_free = [0, 0, 0, 0]
        accb_free = [0, 0]
        spb_free = [0, 0, 0]
        pjc = 0
        spill_ctr = 0
        first_w_dma = True
        dt_evac_done = 0

        groups = []
        i = 0
        while i < len(mlist):
            mj = mlist[i]
            if mj == DT0:
                groups.append(("dt", [mj]))
                i += 1
            else:
                groups.append(("n", [mlist[i], mlist[i + 1]]))
                i += 2

        em('t', lambda t, n=xn_done_v: t.wait_ge(vM.h, n))
        gi = 0
        for kind, ms in groups:
            half = gi % 2
            waits = []
            if first_w_dma:
                waits.append(('v', xn_done_v))
            if wbuf_free[half]:
                waits.append(('t', wbuf_free[half]))
            wv = list(waits)
            if kind == "dt":
                em('g', lambda g, wv=wv: (
                    [g.wait_ge({'v': vM, 't': tM}[e].h, n) for e, n in wv],
                    g.dma_start(out=wdt[:],
                                in_=din["WdtP"][:, :]).then_inc(dW.h, 16)))
                dW.inc(16)
            else:
                rb0 = (ms[0] // 2) * 4
                for kg in range(4):
                    wvk = wv if kg == 0 else []
                    em('g', lambda g, kg=kg, rb0=rb0, half=half, wvk=wvk: (
                        [g.wait_ge({'v': vM, 't': tM}[e].h, n) for e, n in wvk],
                        g.dma_start(
                            out=wbuf[:, half * 4096 + kg * 1024: half * 4096 + (kg + 1) * 1024],
                            in_=din["Wp1"][(rb0 + kg) * 128:(rb0 + kg + 1) * 128, :]
                            ).then_inc(dW.h, 16)))
                    dW.inc(16)
            first_w_dma = False
            nload = dW.n

            for mj in ms:
                is_conv = (XS0 <= mj < 16) or (B0r <= mj < 96)
                if kind == "dt":
                    slices = [(3, 512, 0), (515, 512, 512)]
                elif is_conv:
                    slices = [(0, 343, 0), (343, 342, 343), (685, 342, 685)]
                else:
                    slices = [(3, 512, 0), (515, 512, 512)]
                ev_vals = []
                for (off, w, dcol) in slices:
                    b = slice_ctr % 4
                    slice_ctr += 1
                    em('t', lambda t, nload=nload, bf=bank_free[b]: (
                        t.wait_ge(dW.h, nload),
                        t.wait_ge(sM.h, bf) if bf else None))
                    for k in range(16):
                        if kind == "dt":
                            lhsT = lambda k=k: wdt[:, k * 32:(k + 1) * 32]
                            M = 32
                        else:
                            jj = ms.index(mj)
                            kg, k2 = divmod(k, 4)
                            base = half * 4096 + kg * 1024 + k2 * 256 + jj * 128
                            lhsT = lambda base=base: wbuf[:, base:base + 128]
                            M = 128
                        em('t', lambda t, k=k, lhsT=lhsT, M=M, b=b, off=off, w=w: (
                            t.matmul(PB[b].ap()[0:M, 0:w], lhsT(),
                                     xn[:, k * TH + off:k * TH + off + w],
                                     start=(k == 0), stop=(k == 15)
                                     ).then_inc(tM.h, 1) if k == 15 else
                            t.matmul(PB[b].ap()[0:M, 0:w], lhsT(),
                                     xn[:, k * TH + off:k * TH + off + w],
                                     start=(k == 0), stop=(k == 15))))
                    ntm = tM.inc()
                    if kind == "dt":
                        em('s', lambda s, ntm=ntm, b=b, w=w, dcol=dcol: (
                            s.wait_ge(tM.h, ntm),
                            s.activation(dtt[0:32, dcol:dcol + w], PB[b].ap()[0:32, 0:w],
                                         AF.Copy).then_inc(sM.h, 1)))
                    elif is_conv:
                        slot = pjc % 4
                        em('s', lambda s, ntm=ntm, b=b, w=w, off=off, slot=slot,
                               pf=pre_free[slot]: (
                            s.wait_ge(tM.h, ntm),
                            s.wait_ge(vM.h, pf) if pf else None,
                            s.activation(pre[:, slot * TH + off:slot * TH + off + w],
                                         PB[b].ap()[:, 0:w], AF.Copy).then_inc(sM.h, 1)))
                    else:
                        slot = spill_ctr % 3
                        em('s', lambda s, ntm=ntm, b=b, w=w, dcol=dcol, slot=slot,
                               sf=spb_free[slot]: (
                            s.wait_ge(tM.h, ntm),
                            s.wait_ge(dS.h, sf) if sf else None,
                            s.activation(spb[:, slot * TC + dcol:slot * TC + dcol + w],
                                         PB[b].ap()[:, 0:w], AF.Copy).then_inc(sM.h, 1)))
                    bank_free[b] = sM.inc()
                    ev_vals.append(bank_free[b])
                if kind == "dt":
                    dt_evac_done = ev_vals[-1]
                if is_conv:
                    slot = pjc % 4
                    ah = pjc % 2
                    if mj < 16:
                        cw, cwcol0 = cwxS, mj * 4
                        dst = lambda mj=mj: xc[:, mj * TC:(mj + 1) * TC]
                        spill = None
                    elif B0r <= mj < C0r:
                        cw, cwcol0 = cwbS, (mj - B0r) * 4
                        sslot = spill_ctr % 3
                        dst = lambda sslot=sslot: spb[:, sslot * TC:(sslot + 1) * TC]
                        spill = ("bc", mj - B0r, sslot)
                    else:
                        cw, cwcol0 = cwcS, (mj - C0r) * 4
                        sslot = spill_ctr % 3
                        dst = lambda sslot=sslot: spb[:, sslot * TC:(sslot + 1) * TC]
                        spill = ("cc", mj - C0r, sslot)
                    em('v', lambda v, ev=ev_vals[-1], af=accb_free[ah], slot=slot, ah=ah,
                           cw=cw, c0=cwcol0: (
                        v.wait_ge(sM.h, ev),
                        v.wait_ge(sM.h, af) if af else None,
                        v.tensor_scalar(accb[:, ah * TC:(ah + 1) * TC],
                                        pre[:, slot * TH + 3:slot * TH + 3 + TC],
                                        cw[:, c0 + 3:c0 + 4], None, ALU.mult),
                        v.scalar_tensor_tensor(accb[:, ah * TC:(ah + 1) * TC],
                                               pre[:, slot * TH + 2:slot * TH + 2 + TC],
                                               cw[:, c0 + 2:c0 + 3],
                                               accb[:, ah * TC:(ah + 1) * TC],
                                               ALU.mult, ALU.add),
                        v.scalar_tensor_tensor(accb[:, ah * TC:(ah + 1) * TC],
                                               pre[:, slot * TH + 1:slot * TH + 1 + TC],
                                               cw[:, c0 + 1:c0 + 2],
                                               accb[:, ah * TC:(ah + 1) * TC],
                                               ALU.mult, ALU.add),
                        v.scalar_tensor_tensor(accb[:, ah * TC:(ah + 1) * TC],
                                               pre[:, slot * TH:slot * TH + TC],
                                               cw[:, c0:c0 + 1],
                                               accb[:, ah * TC:(ah + 1) * TC],
                                               ALU.mult, ALU.add).then_inc(vM.h, 1)))
                    pre_free[slot] = vM.inc()
                    if spill is None:
                        em('s', lambda s, vv=pre_free[slot], ah=ah, dst=dst: (
                            s.wait_ge(vM.h, vv),
                            s.activation(dst(), accb[:, ah * TC:(ah + 1) * TC],
                                         AF.Silu).then_inc(sM.h, 1)))
                        accb_free[ah] = sM.inc()
                    else:
                        skind, hh, sslot = spill
                        sdram = cc_dram if skind == "cc" else bc_dram
                        em('s', lambda s, vv=pre_free[slot], ah=ah, dst=dst,
                               sf=spb_free[sslot]: (
                            s.wait_ge(vM.h, vv),
                            s.wait_ge(dS.h, sf) if sf else None,
                            s.activation(dst(), accb[:, ah * TC:(ah + 1) * TC],
                                         AF.Silu).then_inc(sM.h, 1)))
                        accb_free[ah] = sM.inc()
                        em('y', lambda y, sv=accb_free[ah], hh=hh, sslot=sslot, sdram=sdram: (
                            y.wait_ge(sM.h, sv),
                            y.dma_start(out=sdram.rearrange("(h p) t -> h p t", p=128)[hh],
                                        in_=spb[:, sslot * TC:(sslot + 1) * TC]
                                        ).then_inc(dS.h, 16)))
                        spb_free[sslot] = dS.inc(16)
                        spill_ctr += 1
                    pjc += 1
                elif kind == "dt":
                    pass
                else:
                    sslot = spill_ctr % 3
                    em('y', lambda y, sv=ev_vals[-1], hh=mj - Z0, sslot=sslot: (
                        y.wait_ge(sM.h, sv),
                        y.dma_start(out=z_dram.rearrange("(h p) t -> h p t", p=128)[hh],
                                    in_=spb[:, sslot * TC:(sslot + 1) * TC]
                                    ).then_inc(dS.h, 16)))
                    spb_free[sslot] = dS.inc(16)
                    spill_ctr += 1
            if kind != "dt":
                wbuf_free[half] = tM.n
            gi += 1

        em('v', lambda v, n=dt_evac_done: (
            v.wait_ge(sM.h, n),
            v.tensor_scalar(dtt[0:32, 0:TC], dtt[0:32, 0:TC], dtb[:], None,
                            ALU.add).then_inc(vM.h, 1)))
        vDtB = vM.inc()
        em('s', lambda s, n=vDtB: (
            s.wait_ge(vM.h, n),
            s.activation(dtw[0:32, 0:TC], dtt[0:32, 0:TC], AF.Abs),
            s.activation(dtw[0:32, 0:TC], dtw[0:32, 0:TC], AF.Exp, scale=-1.0),
            s.activation(dtw[0:32, 0:TC], dtw[0:32, 0:TC], AF.Ln,
                         bias=1.0).then_inc(sM.h, 1)))
        sDt1 = sM.inc()
        em('v', lambda v: (
            v.wait_ge(sM.h, sDt1),
            v.tensor_relu(dtt[0:32, 0:TC], dtt[0:32, 0:TC]),
            v.tensor_add(dtt[0:32, 0:TC], dtt[0:32, 0:TC],
                         dtw[0:32, 0:TC]).then_inc(vM.h, 1)))
        vDt = vM.inc()

        p1w = ExitStack()
        wdb = p1w.enter_context(nc.sbuf_tensor("wdb", [128, 2 * D], BF16))
        wtb = p1w.enter_context(nc.sbuf_tensor("wtb", [128, D], BF16))
        wdb_free = [0, 0]
        wtb_free = 0
        ev_v = ev_s = 0
        for r in range(64):
            hb = r % 2
            em('g', lambda g, r=r, hb=hb, wf=wdb_free[hb]: (
                g.wait_ge(tM.h, wf) if wf else None,
                g.dma_start(out=wdb[:, hb * D:(hb + 1) * D],
                            in_=din["WdP"][r * 128:(r + 1) * 128, :]).then_inc(dW.h, 16)))
            nl = dW.inc(16)
            em('t', lambda t, nl=nl, ev=ev_v, es=ev_s: (
                t.wait_ge(dW.h, nl),
                t.wait_ge(vM.h, ev) if ev else None,
                t.wait_ge(sM.h, es) if es else None))
            for q in range(4):
                em('t', lambda t, hb=hb, q=q: (
                    t.matmul(PB[4 + q].ap()[:, 0:512], h64blkb[:],
                             wdb[:, hb * D + q * 512:hb * D + (q + 1) * 512],
                             start=True, stop=True).then_inc(tM.h, 1)
                    if q == 3 else
                    t.matmul(PB[4 + q].ap()[:, 0:512], h64blkb[:],
                             wdb[:, hb * D + q * 512:hb * D + (q + 1) * 512],
                             start=True, stop=True)))
            wdb_free[hb] = tM.inc()
            em('v', lambda v, ntm=wdb_free[hb], wf=wtb_free: (
                v.wait_ge(tM.h, ntm),
                v.wait_ge(dC.h, wf) if wf else None,
                v.tensor_copy(wtb[:, 0:512], PB[4].ap()[:, 0:512]),
                v.tensor_copy(wtb[:, 512:1024], PB[5].ap()[:, 0:512]).then_inc(vM.h, 1)))
            ev_v = vM.inc()
            em('s', lambda s, ntm=wdb_free[hb], wf=wtb_free: (
                s.wait_ge(tM.h, ntm),
                s.wait_ge(dC.h, wf) if wf else None,
                s.activation(wtb[:, 1024:1536], PB[6].ap()[:, 0:512], AF.Copy),
                s.activation(wtb[:, 1536:2048], PB[7].ap()[:, 0:512],
                             AF.Copy).then_inc(sM.h, 1)))
            ev_s = sM.inc()
            em('s', lambda s, vv=ev_v, r=r: (
                s.wait_ge(vM.h, vv),
                s.dma_start(out=wtil_dram.rearrange("(c p) g -> c p g", p=128)[:, 2 * r, :],
                            in_=wtb[0:64, :]).then_inc(dC.h, 16),
                s.dma_start(out=wtil_dram.rearrange("(c p) g -> c p g", p=128)[:, 2 * r + 1, :],
                            in_=wtb[64:128, :]).then_inc(dC.h, 16)))
            wtb_free = dC.inc(32)
        wtil_done_d = dC.n

        p1w.close()
        p1.close()
        p1_spills_d = dS.n
        p2a = ExitStack()
        bcbuf = p2a.enter_context(nc.sbuf_tensor("bcbuf", [128, 2 * 4096], BF16))
        bcTall = p2a.enter_context(nc.sbuf_tensor("bcTall", [128, 4096], BF16))
        ld32 = p2a.enter_context(nc.sbuf_tensor("ld32", [32, 128], F32))
        dta = p2a.enter_context(nc.sbuf_tensor("dta", [32, 128], F32))
        w32pre = p2a.enter_context(nc.sbuf_tensor("w32pre", [32, 128], F32))
        e32t = p2a.enter_context(nc.sbuf_tensor("e32t", [32, 128], F32))
        W32 = p2a.enter_context(nc.sbuf_tensor("W32", [128, 32], F32))
        lamb = p2a.enter_context(nc.sbuf_tensor("lamb", [128, 32], F32))
        lamRowS = p2a.enter_context(nc.sbuf_tensor("lamRowS", [1, 32], F32))
        lamCol = p2a.enter_context(nc.sbuf_tensor("lamCol", [32, 1], F32))
        dtxw = p2a.enter_context(nc.sbuf_tensor("dtxw", [128, 2 * 64], BF16))

        em('v', lambda v: v.memset(hfull[:], 0.0).then_inc(vM.h, 1))
        hfull_z = vM.inc()

        bcb_free = [0, 0]
        pb4_free = 0
        pb5_free = 0
        pbb_free = [0, 0]
        dh_free = [0, 0, 0, 0]
        ld32_free = 0
        dtxw_free = [0, 0]
        e_done = hfull_z
        ldb_val = []

        for c in range(NCH):
            hb = c % 2
            em('y', lambda y, c=c, hb=hb, bf=bcb_free[hb], pd=p1_spills_d: (
                y.wait_ge(dS.h, pd),
                y.wait_ge(tM.h, bf) if bf else None,
                y.dma_start(out=bcbuf[:, hb * 4096:(hb + 1) * 4096]
                            .rearrange("n (h t) -> n h t", h=32),
                            in_=bc_dram.rearrange("(h n) t -> n h t", n=128)
                            [:, :, c * 128:(c + 1) * 128]).then_inc(dS.h, 16)))
            nbc = dS.inc(16)

            em('v', lambda v, c=c, lf=ld32_free: (
                v.wait_ge(dS.h, lf) if lf else None,
                v.tensor_scalar(dta[:], dtt[0:32, c * 128:(c + 1) * 128], Acol[:],
                                None, ALU.mult),
                v.tensor_tensor_scan(ld32[:], dta[:], zeros32[:], 0.0, ALU.add, ALU.add),
                v.reduce_sum(sdtT[0:32, c:c + 1], dtt[0:32, c * 128:(c + 1) * 128],
                             axis=mybir.AxisListType.X),
                v.tensor_scalar(e32t[:], ld32[:], ld32[0:32, 127:128], None,
                                ALU.subtract).then_inc(vM.h, 1)))
            v1 = vM.inc()
            em('s', lambda s, n=v1: (
                s.wait_ge(vM.h, n),
                s.activation(e32t[:], e32t[:], AF.Exp, scale=-1.0),
                s.activation(lamCol[:], ld32[0:32, 127:128], AF.Exp).then_inc(sM.h, 1)))
            s1 = sM.inc()
            em('v', lambda v, n=s1, c=c: (
                v.wait_ge(sM.h, n),
                v.tensor_tensor(w32pre[:], e32t[:],
                                dtt[0:32, c * 128:(c + 1) * 128],
                                ALU.mult).then_inc(vM.h, 1)))
            v2 = vM.inc()
            em('t', lambda t, n=v2, pf=pb4_free, c=c: (
                t.wait_ge(vM.h, n),
                t.wait_ge(vM.h, pf) if pf else None,
                t.transpose(PB[4].ap()[:, 0:32], dtt[0:32, c * 128:(c + 1) * 128],
                            idf[0:32, 0:32]),
                t.transpose(PB[4].ap()[:, 32:64], ld32[:], idf[0:32, 0:32]),
                t.transpose(PB[4].ap()[:, 64:96], w32pre[:], idf[0:32, 0:32]),
                t.transpose(PB[4].ap()[0:1, 96:128], lamCol[:],
                            idf[0:32, 0:32]).then_inc(tM.h, 1)))
            t1 = tM.inc()
            em('v', lambda v, n=t1, c=c: (
                v.wait_ge(tM.h, n),
                v.tensor_copy(dtT_all[:, c * 32:(c + 1) * 32], PB[4].ap()[:, 0:32]),
                v.tensor_copy(ldT_all[:, c * 32:(c + 1) * 32], PB[4].ap()[:, 32:64]),
                v.tensor_copy(W32[:], PB[4].ap()[:, 64:96]),
                v.tensor_copy(lamRowS[:], PB[4].ap()[0:1, 96:128]),
                v.tensor_scalar(ldTneg_all[:, c * 32:(c + 1) * 32],
                                ldT_all[:, c * 32:(c + 1) * 32], -1.0, None,
                                ALU.mult).then_inc(vM.h, 1)))
            pb4_free = v3 = vM.inc()
            em('t', lambda t, n=v3, pf=pb5_free: (
                t.wait_ge(vM.h, n),
                t.wait_ge(sM.h, pf) if pf else None,
                t.matmul(PB[5].ap()[:, 0:32], ones1f[:], lamRowS[:],
                         start=True, stop=True).then_inc(tM.h, 1)))
            t2 = tM.inc()
            em('s', lambda s, n=t2: (
                s.wait_ge(tM.h, n),
                s.activation(lamb[:], PB[5].ap()[:, 0:32], AF.Copy).then_inc(sM.h, 1)))
            pb5_free = s2 = sM.inc()
            em('y', lambda y, n=v1, c=c: (
                y.wait_ge(vM.h, n),
                y.dma_start(out=ldb_dram[32 * c:32 * (c + 1), :],
                            in_=ld32[:]).then_inc(dS.h, 16)))
            ld32_free = dS.inc(16)
            ldb_val.append(ld32_free)
            em('s', lambda s, n=e_done, c=c: (
                s.wait_ge(vM.h, n),
                s.activation(snaps[:, c * 2048:(c + 1) * 2048], hfull[:],
                             AF.Copy).then_inc(sM.h, 1)))
            s3 = sM.inc()
            for rd in range(2):
                bb = 6 + rd
                em('t', lambda t, pf=pbb_free[rd]: (
                    t.wait_ge(vM.h, pf) if pf else None))
                for i in range(8):
                    j = rd * 8 + i
                    em('t', lambda t, c=c, j=j, i=i, bb=bb: (
                        t.transpose(PBb[bb][:, i * 128:(i + 1) * 128],
                                    xc[:, j * TC + c * 128:j * TC + (c + 1) * 128],
                                    idb[:]).then_inc(tM.h, 1) if i == 7 else
                        t.transpose(PBb[bb][:, i * 128:(i + 1) * 128],
                                    xc[:, j * TC + c * 128:j * TC + (c + 1) * 128],
                                    idb[:])))
                ntp = tM.inc()
                em('v', lambda v, n=ntp, c=c, rd=rd, bb=bb: (
                    v.wait_ge(tM.h, n),
                    v.tensor_copy(xcT[:, (c * 16 + rd * 8) * 128:(c * 16 + rd * 8 + 8) * 128],
                                  PBb[bb][:, 0:1024]).then_inc(vM.h, 1)))
                pbb_free[rd] = vM.inc()
            for rd in range(4):
                bb = 6 + rd % 2
                em('t', lambda t, pf=pbb_free[rd % 2], nbc=nbc: (
                    t.wait_ge(dS.h, nbc),
                    t.wait_ge(vM.h, pf) if pf else None))
                for i in range(8):
                    h = rd * 8 + i
                    em('t', lambda t, hb=hb, h=h, i=i, bb=bb: (
                        t.transpose(PBb[bb][:, i * 128:(i + 1) * 128],
                                    bcbuf[:, hb * 4096 + h * 128:hb * 4096 + (h + 1) * 128],
                                    idb[:]).then_inc(tM.h, 1) if i == 7 else
                        t.transpose(PBb[bb][:, i * 128:(i + 1) * 128],
                                    bcbuf[:, hb * 4096 + h * 128:hb * 4096 + (h + 1) * 128],
                                    idb[:])))
                ntp = tM.inc()
                em('v', lambda v, n=ntp, rd=rd, bb=bb: (
                    v.wait_ge(tM.h, n),
                    v.tensor_copy(bcTall[:, rd * 1024:(rd + 1) * 1024],
                                  PBb[bb][:, 0:1024]).then_inc(vM.h, 1)))
                pbb_free[rd % 2] = vM.inc()
            bcT_done = vM.n
            bcb_free[hb] = tM.n
            for g in range(4):
                bk = g % 4
                for i in range(8):
                    h = g * 8 + i
                    dh = h % 2
                    em('v', lambda v, c=c, h=h, dh=dh, df=dtxw_free[dh]: (
                        v.wait_ge(tM.h, df) if df else None,
                        v.tensor_scalar(dtxw[:, dh * 64:(dh + 1) * 64],
                                        xcT[:, (c * 16 + h // 2) * 128 + (h % 2) * 64:
                                            (c * 16 + h // 2) * 128 + (h % 2) * 64 + 64],
                                        W32[:, h:h + 1], None, ALU.mult).then_inc(vM.h, 1)))
                    nv = vM.inc()
                    em('t', lambda t, n=nv, h=h, dh=dh, bk=bk, i=i, dhf=dh_free[bk]: (
                        t.wait_ge(vM.h, n),
                        t.wait_ge(vM.h, dhf) if (i == 0 and dhf) else None,
                        t.matmul(PB[bk].ap()[:, (h % 8) * 64:(h % 8 + 1) * 64],
                                 bcTall[:, h * 128:(h + 1) * 128],
                                 dtxw[:, dh * 64:(dh + 1) * 64],
                                 start=True, stop=True).then_inc(tM.h, 1)))
                    dtxw_free[dh] = tM.inc()
                tdh = tM.n
                em('v', lambda v, n=tdh, s3=s3, g=g, bk=bk: (
                    v.wait_ge(tM.h, n),
                    v.wait_ge(sM.h, s3),
                    *[v.scalar_tensor_tensor(
                        hfull[:, (g * 8 + i2) * 64:(g * 8 + i2 + 1) * 64],
                        hfull[:, (g * 8 + i2) * 64:(g * 8 + i2 + 1) * 64],
                        lamb[:, g * 8 + i2:g * 8 + i2 + 1],
                        PB[bk].ap()[:, i2 * 64:(i2 + 1) * 64],
                        ALU.mult, ALU.add) for i2 in range(7)],
                    v.scalar_tensor_tensor(
                        hfull[:, (g * 8 + 7) * 64:(g * 8 + 8) * 64],
                        hfull[:, (g * 8 + 7) * 64:(g * 8 + 8) * 64],
                        lamb[:, g * 8 + 7:g * 8 + 8],
                        PB[bk].ap()[:, 7 * 64:8 * 64],
                        ALU.mult, ALU.add).then_inc(vM.h, 1)))
                dh_free[bk] = vM.inc()
            e_done = vM.n

        if phases < 3:
            _finish_dev(nc, em, EM, taps, tap_outs, locals())
            p2a.close()
            pxc.close()
            pk2.close()
            pxn.close()
            _run_blocks(st, nc, EM)
            return nc, tap_outs

        p2a.close()
        pxc.close()
        p2c = ExitStack()
        P2C = lambda n, s, d: p2c.enter_context(nc.sbuf_tensor(n, s, d))
        triuS = P2C("triuS", [128, 128], BF16)
        boS = P2C("boS", [128, 512], BF16)
        selS = P2C("selS", [32, 2048], BF16)
        mktS = P2C("mktS", [8, 8], F32)
        cmaskS = P2C("cmaskS", [1, 256], F32)
        sdtc = P2C("sdtc", [32, 1], F32)
        PS8 = P2C("PS8", [32, 8], F32)
        pse = P2C("pse", [32, 8], F32)
        sdtg = P2C("sdtg", [8, 32], F32)
        D8 = P2C("D8", [8, 32], F32)
        eflat = P2C("eflat", [1, 256], F32)
        pcf = P2C("pcf", [1, 256], F32)
        EbS = P2C("EbS", [128, 256], F32)
        PbS = P2C("PbS", [128, 256], F32)
        hin = P2C("hin", [128, 2048], F32)
        gHb = P2C("gHb", [128, 2048], BF16)
        ldflat = P2C("ldflat", [1, 4096], F32)
        ccbuf = P2C("ccbuf", [128, 2 * 4096], BF16)
        bcb2 = P2C("bcb2", [128, 4096], BF16)
        zbuf = P2C("zbuf", [128, 2048], BF16)
        ybf = P2C("ybf", [128, 2048], BF16)
        ysq = P2C("ysq", [128, 2048], BF16)
        yg = P2C("yg", [128, 2048], BF16)
        mtb = P2C("mtb", [128, 2 * 128], BF16)
        ebcb = P2C("ebcb", [128, 2 * 128], BF16)
        s2b = P2C("s2b", [128, 2 * 128], BF16)
        dtxb = P2C("dtxb", [128, 2 * 64], BF16)
        ccsb = P2C("ccsb", [128, 2 * 128], BF16)
        rstdf = P2C("rstdf", [32, 128], F32)
        rstdf2 = P2C("rstdf2", [32, 128], F32)
        arow = P2C("arow", [1, 32], F32)
        rstdb = P2C("rstdb", [32, 128], BF16)

        t2a, v2a, s2a, d2a = tM.n, vM.n, sM.n, dS.n
        first = True
        for dst, srcn in [(triuS, "TRIU"), (boS, "BO"), (selS, "SEL"),
                          (mktS, "MkT"), (cmaskS, "coremask")]:
            em('g', lambda g, d=dst, s=din[srcn], f=first: (
                (g.wait_ge(tM.h, t2a), g.wait_ge(vM.h, v2a),
                 g.wait_ge(sM.h, s2a), g.wait_ge(dS.h, d2a)) if f else None,
                g.dma_start(out=d[:], in_=s).then_inc(dW.h, 16)))
            dW.inc(16)
            first = False
        c2_consts = dW.n

        em('v', lambda v: (
            v.reduce_sum(sdtc[:], sdtT[0:32, 0:8], axis=mybir.AxisListType.X),
            v.memset(PS8[0:32, 0:1], 0.0),
            *[v.tensor_add(PS8[0:32, cc2:cc2 + 1], PS8[0:32, cc2 - 1:cc2],
                           sdtT[0:32, cc2 - 1:cc2]) for cc2 in range(1, 8)],
            v.tensor_scalar(PS8[:], PS8[:], Acol[:], None, ALU.mult).then_inc(vM.h, 1)))
        v2b = vM.inc()
        em('s', lambda s, n=v2b: (
            s.wait_ge(vM.h, n),
            s.activation(pse[:], PS8[:], AF.Exp).then_inc(sM.h, 1)))
        s2b_ = sM.inc()
        em('y', lambda y, n=s2b_: (
            y.wait_ge(sM.h, n),
            y.dma_start(out=ag_in_s, in_=sdtc[:]).then_inc(dS.h, 16),
            y.dma_start(out=pc_dram, in_=pse[:]).then_inc(dS.h, 16)))
        dS.inc(32)
        ag_s_d = dS.n
        em('g', lambda g, n=e_done: (
            g.wait_ge(vM.h, n),
            g.dma_start(out=ag_in_h, in_=hfull[:]).then_inc(dW.h, 16)))
        ag_h_d = dW.inc(16)
        em('g', lambda g, n1=ag_s_d, n2=ag_h_d: (
            g.wait_ge(dS.h, n1),
            g.wait_ge(dW.h, n2),
            g.collective_compute(
                "AllGather", ALU.bypass,
                replica_groups=[list(range(NCORES))],
                ins=[ag_in_h], outs=[ag_out_h]).then_inc(cS.h),
            g.collective_compute(
                "AllGather", ALU.bypass,
                replica_groups=[list(range(NCORES))],
                ins=[ag_in_s], outs=[ag_out_s]).then_inc(cS.h)))
        cS.inc(2)
        em('y', lambda y: (
            y.wait_ge(cS.h, 2),
            y.dma_start(out=sdtg[:],
                        in_=ag_out_s.rearrange("(r h) c -> r (h c)", h=32)
                        ).then_inc(dS.h, 16),
            y.dma_start(out=pcf[:],
                        in_=pc_dram.rearrange("a b -> (a b)").unsqueeze(0)
                        ).then_inc(dS.h, 16)))
        dS.inc(32)
        sdtg_d = dS.n
        em('t', lambda t, n=sdtg_d: (
            t.wait_ge(dS.h, n),
            t.matmul(PB[6].ap()[0:8, 0:32], mktS[:], sdtg[:],
                     start=True, stop=True).then_inc(tM.h, 1)))
        tD = tM.inc()
        em('s', lambda s, n=tD, nc2=c2_consts: (
            s.wait_ge(tM.h, n),
            s.wait_ge(dW.h, nc2),
            s.activation(D8[:], PB[6].ap()[0:8, 0:32], AF.Copy).then_inc(sM.h, 1)))
        sD8 = sM.inc()
        em('s', lambda s, n=sD8: (
            s.wait_ge(sM.h, n),
            s.dma_start(out=db_dram, in_=D8[:]).then_inc(dC.h, 16)))
        dD8 = dC.inc(16)
        em('y', lambda y, n=dD8: (
            y.wait_ge(dC.h, n),
            y.dma_start(out=eflat[:],
                        in_=db_dram.rearrange("a b -> (a b)").unsqueeze(0)
                        ).then_inc(dS.h, 16)))
        eflat_d = dS.inc(16)
        em('t', lambda t: t.transpose(PB[5].ap()[0:1, 0:32], Acol[:],
                                      idf[0:32, 0:32]).then_inc(tM.h, 1))
        tAr = tM.inc()
        em('v', lambda v, n=tAr, n2=eflat_d: (
            v.wait_ge(tM.h, n),
            v.wait_ge(dS.h, n2),
            v.tensor_copy(arow[:], PB[5].ap()[0:1, 0:32]),
            *[v.tensor_mul(eflat[0:1, 32 * jr:32 * (jr + 1)],
                           eflat[0:1, 32 * jr:32 * (jr + 1)], arow[:])
              for jr in range(7)],
            v.tensor_mul(eflat[0:1, 224:256], eflat[0:1, 224:256],
                         arow[:]).then_inc(vM.h, 1)))
        vAe = vM.inc()
        em('s', lambda s, n=vAe: (
            s.wait_ge(vM.h, n),
            s.activation(eflat[:], eflat[:], AF.Exp).then_inc(sM.h, 1)))
        sEf = sM.inc()
        em('v', lambda v, n=sEf: (
            v.wait_ge(sM.h, n),
            v.tensor_mul(eflat[:], eflat[:], cmaskS[:]).then_inc(vM.h, 1)))
        vEf = vM.inc()
        em('t', lambda t, n=vEf, n2=sdtg_d: (
            t.wait_ge(vM.h, n),
            t.wait_ge(dS.h, n2),
            t.matmul(PB[6].ap()[:, 0:256], ones1f[:], eflat[:], start=True, stop=True),
            t.matmul(PB[7].ap()[:, 0:256], ones1f[:], pcf[:],
                     start=True, stop=True).then_inc(tM.h, 1)))
        tEb = tM.inc()
        em('v', lambda v, n=tEb: (
            v.wait_ge(tM.h, n),
            v.tensor_copy(EbS[:], PB[6].ap()[:, 0:256]),
            v.tensor_copy(PbS[:], PB[7].ap()[:, 0:256]),
            v.memset(hin[:], 0.0).then_inc(vM.h, 1)))
        vEb = vM.inc()
        ghb_free = vEb
        for r in range(NCORES):
            em('g', lambda g, r=r, gf=ghb_free: (
                g.wait_ge(vM.h, gf),
                g.dma_start(out=gHb[:], in_=ag_out_h[128 * r:128 * (r + 1), :]
                            ).then_inc(dW.h, 16)))
            ngh = dW.inc(16)
            em('v', lambda v, n=ngh, r=r: (
                v.wait_ge(dW.h, n),
                *[v.scalar_tensor_tensor(
                    hin[:, h2 * 64:(h2 + 1) * 64], gHb[:, h2 * 64:(h2 + 1) * 64],
                    EbS[:, r * 32 + h2:r * 32 + h2 + 1],
                    hin[:, h2 * 64:(h2 + 1) * 64], ALU.mult, ALU.add)
                    for h2 in range(31)],
                v.scalar_tensor_tensor(
                    hin[:, 31 * 64:32 * 64], gHb[:, 31 * 64:32 * 64],
                    EbS[:, r * 32 + 31:r * 32 + 32],
                    hin[:, 31 * 64:32 * 64], ALU.mult, ALU.add).then_inc(vM.h, 1)))
            ghb_free = vM.inc()
        for c in range(NCH):
            em('v', lambda v, c=c: (
                *[v.scalar_tensor_tensor(
                    snaps[:, c * 2048 + h2 * 64:c * 2048 + (h2 + 1) * 64],
                    hin[:, h2 * 64:(h2 + 1) * 64],
                    PbS[:, h2 * 8 + c:h2 * 8 + c + 1],
                    snaps[:, c * 2048 + h2 * 64:c * 2048 + (h2 + 1) * 64],
                    ALU.mult, ALU.add) for h2 in range(31)],
                v.scalar_tensor_tensor(
                    snaps[:, c * 2048 + 31 * 64:c * 2048 + 32 * 64],
                    hin[:, 31 * 64:32 * 64],
                    PbS[:, 31 * 8 + c:31 * 8 + c + 1],
                    snaps[:, c * 2048 + 31 * 64:c * 2048 + 32 * 64],
                    ALU.mult, ALU.add).then_inc(vM.h, 1)))
            vM.inc()
        corr_done = vM.n

        if phases < 4:
            _finish_dev(nc, em, EM, taps, tap_outs, locals())
            p2c.close()
            pk2.close()
            pxn.close()
            _run_blocks(st, nc, EM)
            return nc, tap_outs

        chunk_done = [{'t': 0, 'v': 0, 's': 0}, {'t': 0, 'v': 0, 's': 0}]
        bcb2_free = 0
        zbuf_free = 0
        ldflat_free = 0
        mp_free = [(None, 0), (None, 0)]
        sa_free = [0, 0]
        y_bank_free = [0, 0, 0, 0]
        rot_free = [0, 0]
        yg_spill = 0
        prev_trms = prev_ygv = 0

        for c in range(NCH):
            hb = c % 2
            cdp = dict(chunk_done[hb])
            em('y', lambda y, c=c, hb=hb, cdp=cdp, cd=corr_done, bf=bcb2_free,
                   zf=zbuf_free, lf=ldflat_free: (
                y.wait_ge(vM.h, cd) if c == 0 else None,
                y.wait_ge(tM.h, cdp['t']) if cdp['t'] else None,
                y.wait_ge(vM.h, cdp['v']) if cdp['v'] else None,
                y.wait_ge(sM.h, cdp['s']) if cdp['s'] else None,
                y.dma_start(out=ccbuf[:, hb * 4096:(hb + 1) * 4096]
                            .rearrange("n (h t) -> n h t", h=32),
                            in_=cc_dram.rearrange("(h n) t -> n h t", n=128)
                            [:, :, c * 128:(c + 1) * 128]).then_inc(dS.h, 16),
                y.wait_ge(tM.h, bf) if bf else None,
                y.dma_start(out=bcb2[:, 0:4096]
                            .rearrange("n (h t) -> n h t", h=32),
                            in_=bc_dram.rearrange("(h n) t -> n h t", n=128)
                            [:, :, c * 128:(c + 1) * 128]).then_inc(dS.h, 16),
                y.wait_ge(vM.h, zf) if zf else None,
                y.dma_start(out=zbuf[:, 0:2048]
                            .rearrange("p (j t) -> p j t", j=16),
                            in_=z_dram.rearrange("(j p) t -> p j t", p=128)
                            [:, :, c * 128:(c + 1) * 128]).then_inc(dS.h, 16),
                y.wait_ge(tM.h, lf) if lf else None,
                y.dma_start(out=ldflat[:],
                            in_=ldb_dram[32 * c:32 * (c + 1), :]
                            .rearrange("a b -> (a b)").unsqueeze(0)
                            ).then_inc(dS.h, 16)))
            dS.inc(64)
            nld = dS.n

            for h in range(32):
                if DEBUG_2C_LVL < 2:
                    break
                rot = h % 2
                bk = h // 8
                pj = h // 2
                em('t', lambda t, h=h, rot=rot, n=nld, mf=mp_free[rot]: (
                    t.wait_ge(dS.h, n) if h == 0 else None,
                    t.wait_ge({'s': sM, 'v': vM}[mf[0]].h, mf[1]) if mf[0] else None,
                    t.matmul(PB[6 + rot].ap()[:, 0:128], ones1f[:],
                             ldflat[0:1, h * 128:(h + 1) * 128],
                             start=True, stop=True).then_inc(tM.h, 1)))
                nmp = tM.inc()
                em('s', lambda s, n=nmp, c=c, h=h, rot=rot: (
                    s.wait_ge(tM.h, n),
                    (s.activation(mtb[:, rot * 128:(rot + 1) * 128],
                                  PB[6 + rot].ap()[:, 0:128], AF.Exp,
                                  bias=ldTneg_all[:, c * 32 + h:c * 32 + h + 1])
                     if DEBUG_2C_LVL >= 2.5 else
                     s.activation(mtb[:, rot * 128:(rot + 1) * 128],
                                  PB[6 + rot].ap()[:, 0:128], AF.Copy)),
                    (s.activation(ebcb[:, rot * 128:(rot + 1) * 128],
                                  PB[6 + rot].ap()[:, 0:128],
                                  AF.Exp) if DEBUG_2C_LVL >= 2.7 else
                     s.activation(ebcb[:, rot * 128:(rot + 1) * 128],
                                  PB[6 + rot].ap()[:, 0:128],
                                  AF.Copy)).then_inc(sM.h, 1)))
                mp_free[rot] = ('s', sM.inc())
                em('v', lambda v, n=mp_free[rot][1], rot=rot: (
                    v.wait_ge(sM.h, n),
                    v.tensor_mul(mtb[:, rot * 128:(rot + 1) * 128],
                                 mtb[:, rot * 128:(rot + 1) * 128],
                                 triuS[:]).then_inc(vM.h, 1)))
                nmt2 = vM.inc()
                if DEBUG_2C_LVL < 3:
                    continue
                em('t', lambda t, h=h, hb=hb, rot=rot, af=sa_free[rot], n=nld: (
                    t.wait_ge(dS.h, n) if h == 0 else None,
                    (t.wait_ge({'s': sM, 'v': vM}[af[0]].h, af[1]) if isinstance(af, tuple)
                     else t.wait_ge(vM.h, af)) if af else None,
                    t.matmul(PB[4 + rot].ap()[:, 0:128],
                             bcb2[:, h * 128:(h + 1) * 128],
                             ccbuf[:, hb * 4096 + h * 128:hb * 4096 + (h + 1) * 128],
                             start=True, stop=True).then_inc(tM.h, 1)))
                nsa = tM.inc()
                em('v', lambda v, n=nsa, n2=nmt2, rot=rot, c=c, h=h, hb=hb: (
                    v.wait_ge(tM.h, n),
                    v.tensor_tensor(s2b[:, rot * 128:(rot + 1) * 128],
                                    PB[4 + rot].ap()[:, 0:128],
                                    mtb[:, rot * 128:(rot + 1) * 128], ALU.mult),
                    v.tensor_scalar(dtxb[:, rot * 64:(rot + 1) * 64],
                                    xcT[:, (c * 16 + h // 2) * 128 + (h % 2) * 64:
                                        (c * 16 + h // 2) * 128 + (h % 2) * 64 + 64],
                                    dtT_all[:, c * 32 + h:c * 32 + h + 1],
                                    None, ALU.mult),
                    v.tensor_tensor(ccsb[:, rot * 128:(rot + 1) * 128],
                                    ccbuf[:, hb * 4096 + h * 128:hb * 4096 + (h + 1) * 128],
                                    ebcb[:, rot * 128:(rot + 1) * 128],
                                    ALU.mult).then_inc(vM.h, 1)))
                sa_free[rot] = vM.inc()
                if DEBUG_2C_LVL < 4:
                    continue
                em('t', lambda t, n=sa_free[rot], h=h, c=c, rot=rot, bk=bk, pj=pj,
                       yf=y_bank_free[bk]: (
                    t.wait_ge(vM.h, n),
                    t.wait_ge(sM.h, yf) if (h % 8 == 0 and yf) else None,
                    t.matmul(PB[bk].ap()[64 * (h % 2):64 * (h % 2) + 64,
                                         (pj % 4) * 128:(pj % 4 + 1) * 128],
                             dtxb[:, rot * 64:(rot + 1) * 64],
                             s2b[:, rot * 128:(rot + 1) * 128],
                             start=True, stop=False),
                    t.matmul(PB[bk].ap()[64 * (h % 2):64 * (h % 2) + 64,
                                         (pj % 4) * 128:(pj % 4 + 1) * 128],
                             snaps[:, c * 2048 + h * 64:c * 2048 + (h + 1) * 64],
                             ccsb[:, rot * 128:(rot + 1) * 128],
                             start=False, stop=True).then_inc(tM.h, 1)))
                tM.inc()
                if h % 8 == 7:
                    tyb = tM.n
                    em('s', lambda s, n=tyb, bk=bk, pt=prev_trms, pv=prev_ygv: (
                        s.wait_ge(tM.h, n),
                        s.wait_ge(tM.h, pt) if (bk == 0 and pt) else None,
                        s.wait_ge(vM.h, pv) if (bk == 0 and pv) else None,
                        s.activation(ybf[:, bk * 512:(bk + 1) * 512],
                                     PB[bk].ap()[:, 0:512], AF.Copy),
                        s.activation(ysq[:, bk * 512:(bk + 1) * 512],
                                     PB[bk].ap()[:, 0:512],
                                     AF.Square).then_inc(sM.h, 1)))
                    y_bank_free[bk] = sM.inc()
            ldflat_free = tM.n
            yev = sM.n
            if DEBUG_NO_EPI or DEBUG_2C_LVL < 9:
                prev_trms, prev_ygv = tM.n, vM.n
                chunk_done[hb] = {'t': tM.n, 'v': vM.n, 's': sM.n}
                bcb2_free = tM.n
                zbuf_free = 0
                continue
            em('t', lambda t, n=yev, af0=sa_free[0], af1=sa_free[1],
                   mf0=mp_free[0], mf1=mp_free[1]: (
                t.wait_ge(sM.h, n),
                ((t.wait_ge({'s': sM, 'v': vM}[af0[0]].h, af0[1]) if isinstance(af0, tuple)
                  else t.wait_ge(vM.h, af0)) if af0 else None),
                ((t.wait_ge({'s': sM, 'v': vM}[af1[0]].h, af1[1]) if isinstance(af1, tuple)
                  else t.wait_ge(vM.h, af1)) if af1 else None),
                t.wait_ge({'s': sM, 'v': vM}[mf0[0]].h, mf0[1]) if mf0[0] else None,
                t.wait_ge({'s': sM, 'v': vM}[mf1[0]].h, mf1[1]) if mf1[0] else None))
            for j in range(16):
                em('t', lambda t, j=j, bk=4 + j // 4, c0=(j % 4) * 128: (
                    t.matmul(PB[bk].ap()[0:32, c0:c0 + 128], boS[:, j * 32:(j + 1) * 32],
                             ysq[:, j * 128:(j + 1) * 128],
                             start=True, stop=True).then_inc(tM.h, 1)
                    if j == 15 else
                    t.matmul(PB[bk].ap()[0:32, c0:c0 + 128], boS[:, j * 32:(j + 1) * 32],
                             ysq[:, j * 128:(j + 1) * 128],
                             start=True, stop=True)))
            trms = tM.inc()
            em('v', lambda v, n=trms: (
                v.wait_ge(tM.h, n),
                v.tensor_copy(rstdf[:], PB[4].ap()[0:32, 0:128]),
                *[v.tensor_add(rstdf[:], rstdf[:],
                               PB[4 + jj // 4].ap()[0:32, (jj % 4) * 128:(jj % 4) * 128 + 128])
                  for jj in range(1, 15)],
                v.tensor_add(rstdf[:], rstdf[:],
                             PB[7].ap()[0:32, 3 * 128:4 * 128]).then_inc(vM.h, 1)))
            vsum = vM.inc()
            em('s', lambda s, n=vsum: (
                s.wait_ge(vM.h, n),
                s.activation(rstdf2[:], rstdf[:], AF.Ln,
                             bias=epsrms[:], scale=1.0 / 64.0),
                s.activation(rstdb[:], rstdf2[:], AF.Exp,
                             scale=-0.5).then_inc(sM.h, 1)))
            vrms = srms = sM.inc()
            sa_free[0] = ('v', vsum)
            sa_free[1] = ('v', vsum)
            em('s', lambda s, n=nld, hb=hb: (
                s.wait_ge(dS.h, n),
                s.activation(zbuf[:, 0:2048], zbuf[:, 0:2048],
                             AF.Silu).then_inc(sM.h, 1)))
            zsil = sM.inc()
            for j in range(16):
                rot = j % 2
                em('t', lambda t, j=j, rot=rot, n=vrms, rf=rot_free[rot],
                       mf0=mp_free[0], mf1=mp_free[1]: (
                    t.wait_ge(sM.h, n) if j == 0 else None,
                    t.wait_ge({'s': sM, 'v': vM}[mf0[0]].h, mf0[1])
                    if (j == 0 and mf0[0]) else None,
                    t.wait_ge({'s': sM, 'v': vM}[mf1[0]].h, mf1[1])
                    if (j == 0 and mf1[0]) else None,
                    t.wait_ge(vM.h, rf) if rf else None,
                    t.matmul(PB[6 + rot].ap()[:, 0:128], selS[:, j * 128:(j + 1) * 128],
                             rstdb[:], start=True, stop=True).then_inc(tM.h, 1)))
                nrb = tM.inc()
                em('v', lambda v, n=nrb, j=j, rot=rot, zs=zsil, hb=hb, ys=yg_spill: (
                    v.wait_ge(tM.h, n),
                    v.wait_ge(sM.h, zs) if j == 0 else None,
                    v.wait_ge(dC.h, ys) if (j == 0 and ys) else None,
                    v.tensor_tensor(yg[:, j * 128:(j + 1) * 128],
                                    ybf[:, j * 128:(j + 1) * 128],
                                    PB[6 + rot].ap()[:, 0:128], ALU.mult),
                    v.tensor_tensor(yg[:, j * 128:(j + 1) * 128],
                                    yg[:, j * 128:(j + 1) * 128],
                                    zbuf[:, j * 128:(j + 1) * 128],
                                    ALU.mult).then_inc(vM.h, 1)))
                rot_free[rot] = vM.inc()
            mp_free = [('v', rot_free[0]), ('v', rot_free[1])]
            ygv = vM.n
            em('s', lambda s, n=ygv, c=c, ys=yg_spill: (
                s.wait_ge(vM.h, n),
                s.wait_ge(dC.h, ys) if ys else None,
                s.dma_start(out=y_dram.rearrange("(j p) t -> p j t", p=128)
                            [:, :, c * 128:(c + 1) * 128],
                            in_=yg[:].rearrange("p (j t) -> p j t", j=16)
                            ).then_inc(dC.h, 16)))
            yg_spill = dC.inc(16)
            prev_trms, prev_ygv = trms, ygv
            bcb2_free = tM.n
            zbuf_free = ygv
            chunk_done[hb] = {'t': tM.n, 'v': vM.n, 's': sM.n}

        if phases < 5:
            _finish_dev(nc, em, EM, taps, tap_outs, locals())
            p2c.close()
            pk2.close()
            pxn.close()
            _run_blocks(st, nc, EM)
            return nc, tap_outs

        p2c.close()
        pk2.close()
        p2d = ExitStack()
        wb2 = p2d.enter_context(nc.sbuf_tensor("wb2", [128, 2 * 4096], BF16))
        yb2 = p2d.enter_context(nc.sbuf_tensor("yb2", [128, 16 * 1024], BF16))
        sob = p2d.enter_context(nc.sbuf_tensor("sob", [128, 2 * 1024], BF16))
        t2c, v2c, s2c, d2c = tM.n, vM.n, sM.n, dS.n
        em('y', lambda y, n=yg_spill: (
            y.wait_ge(dC.h, n), y.wait_ge(tM.h, t2c), y.wait_ge(vM.h, v2c),
            y.wait_ge(sM.h, s2c), y.wait_ge(dS.h, d2c)))
        em('g', lambda g, n=yg_spill: (
            g.wait_ge(dC.h, n), g.wait_ge(tM.h, t2c), g.wait_ge(vM.h, v2c),
            g.wait_ge(sM.h, s2c), g.wait_ge(dS.h, d2c)))
        for k in range(16):
            em('y', lambda y, k=k: y.dma_start(
                out=yb2[:, k * 1024:(k + 1) * 1024],
                in_=y_dram.rearrange("(j p) t -> j p t", p=128)[k]).then_inc(dS.h, 16))
            dS.inc(16)
        yb2_d = dS.n
        wb2_free = [0, 0]
        sb_free = [0, 0]
        bank_free2 = [0, 0, 0, 0]
        sl2 = 0
        for g2 in range(8):
            half = g2 % 2
            cols0 = g2 * 256
            for kg in range(4):
                em('g', lambda g, kg=kg, g2=g2, half=half, wf=wb2_free[half]: (
                    g.wait_ge(tM.h, wf) if (kg == 0 and wf) else None,
                    g.dma_start(
                        out=wb2[:, half * 4096 + kg * 1024:half * 4096 + (kg + 1) * 1024],
                        in_=din["Wp2"][(g2 * 4 + kg) * 128:(g2 * 4 + kg + 1) * 128, :]
                        ).then_inc(dW.h, 16)))
                dW.inc(16)
            nload = dW.n
            for mj2 in range(2):
                m = g2 * 2 + mj2
                sh = m % 2
                for tf in range(2):
                    b = sl2 % 4
                    sl2 += 1
                    em('t', lambda t, n=nload, bf=bank_free2[b], ny=yb2_d: (
                        t.wait_ge(dW.h, n),
                        t.wait_ge(dS.h, ny),
                        t.wait_ge(sM.h, bf) if bf else None))
                    for k in range(16):
                        kg, k2 = divmod(k, 4)
                        base = half * 4096 + kg * 1024 + k2 * 256 + mj2 * 128
                        em('t', lambda t, k=k, base=base, b=b, tf=tf: (
                            t.matmul(PB[b].ap()[:, 0:512], wb2[:, base:base + 128],
                                     yb2[:, k * 1024 + tf * 512:k * 1024 + (tf + 1) * 512],
                                     start=(k == 0), stop=(k == 15)).then_inc(tM.h, 1)
                            if k == 15 else
                            t.matmul(PB[b].ap()[:, 0:512], wb2[:, base:base + 128],
                                     yb2[:, k * 1024 + tf * 512:k * 1024 + (tf + 1) * 512],
                                     start=(k == 0), stop=(k == 15))))
                    ntm = tM.inc()
                    em('s', lambda s, n=ntm, b=b, sh=sh, tf=tf, sf=sb_free[sh]: (
                        s.wait_ge(tM.h, n),
                        s.wait_ge(dC.h, sf) if (tf == 0 and sf) else None,
                        s.activation(sob[:, sh * 1024 + tf * 512:sh * 1024 + (tf + 1) * 512],
                                     PB[b].ap()[:, 0:512], AF.Copy).then_inc(sM.h, 1)))
                    bank_free2[b] = sM.inc()
                em('s', lambda s, n=bank_free2[(sl2 - 1) % 4], m=m, sh=sh: (
                    s.wait_ge(sM.h, n),
                    s.dma_start(out=s_dram.rearrange("(j p) t -> j p t", p=128)[m],
                                in_=sob[:, sh * 1024:(sh + 1) * 1024]).then_inc(dC.h, 16)))
                sb_free[sh] = dC.inc(16)
            wb2_free[half] = tM.n
        s_done_d = dC.n

        p2d.close()
        p3a = ExitStack()
        tmp1f = p3a.enter_context(nc.sbuf_tensor("tmp1f", [128, 16 * TC], F32))
        btmp = p3a.enter_context(nc.sbuf_tensor("btmp", [128, TC], F32))
        t2d, v2d, s2d, d2d = tM.n, vM.n, sM.n, dS.n

        f1_bank_free = [0, 0, 0, 0]
        slc = 0
        em('t', lambda t: (t.wait_ge(sM.h, s2d), t.wait_ge(dC.h, s_done_d)))
        for j in range(DCH):
            for tf in range(2):
                b = slc % 4
                slc += 1
                em('t', lambda t, j=j, tf=tf, b=b, bf=f1_bank_free[b]: (
                    t.wait_ge(vM.h, bf) if bf else None,
                    t.matmul(PB[b].ap()[:, 0:512], h128su[:, j * 128:(j + 1) * 128],
                             xn[:, j * TH + 3 + tf * 512:j * TH + 3 + (tf + 1) * 512],
                             start=True, stop=True).then_inc(tM.h, 1)))
                nf = tM.inc()
                em('v', lambda v, n=nf, j=j, tf=tf, b=b, f=(j == 0 and tf == 0): (
                    v.wait_ge(tM.h, n),
                    v.wait_ge(sM.h, s2d) if f else None,
                    v.tensor_copy(tmp1f[:, j * TC + tf * 512:j * TC + (tf + 1) * 512],
                                  PB[b].ap()[:, 0:512]).then_inc(vM.h, 1)))
                f1_bank_free[b] = vM.inc()
        f1_done = vM.n

        if phases < 5.15:
            _finish_dev(nc, em, EM, taps, tap_outs, locals())
            p3a.close()
            pxn.close()
            _run_blocks(st, nc, EM)
            return nc, tap_outs

        for stg in range(DEBUG_BF_STAGES):
            stepj = 1 << stg
            for a in range(DCH):
                if a & stepj:
                    continue
                bpt = a + stepj
                em('v', lambda v, a=a, bpt=bpt: (
                    v.tensor_sub(btmp[:], tmp1f[:, a * TC:(a + 1) * TC],
                                 tmp1f[:, bpt * TC:(bpt + 1) * TC]),
                    v.tensor_add(tmp1f[:, a * TC:(a + 1) * TC],
                                 tmp1f[:, a * TC:(a + 1) * TC],
                                 tmp1f[:, bpt * TC:(bpt + 1) * TC]),
                    v.tensor_copy(tmp1f[:, bpt * TC:(bpt + 1) * TC],
                                  btmp[:]).then_inc(vM.h, 1)))
                vM.inc()
        bf_done = vM.n
        for j in range(DCH):
            em('s', lambda s, j=j, n=bf_done: (
                s.wait_ge(vM.h, n) if j == 0 else None,
                s.activation(xn[:, j * TH:j * TH + TC],
                             tmp1f[:, j * TC:(j + 1) * TC],
                             AF.Copy).then_inc(sM.h, 1)))
            sM.inc()
        fwht1_done = sM.n

        if phases < 5.3:
            _finish_dev(nc, em, EM, taps, tap_outs, locals())
            p3a.close()
            pxn.close()
            _run_blocks(st, nc, EM)
            return nc, tap_outs

        p3a.close()
        p3b = ExitStack()
        x2 = p3b.enter_context(nc.sbuf_tensor("x2", [128, ICH * TC], BF16))
        p3u = ExitStack()
        wbf = p3u.enter_context(nc.sbuf_tensor("wbf", [128, 2 * 4096], BF16))
        gts = p3u.enter_context(nc.sbuf_tensor("gts", [128, 2 * TC], BF16))
        ug_bank_free = [0, 0, 0, 0]
        wbf_free = [0, 0]
        gts_free = [0, 0]
        ug_slc = 0
        for m in range(ICH):
            half = m % 2
            em('g', lambda g, m=m, half=half, wf=wbf_free[half], f=(m == 0): (
                (g.wait_ge(sM.h, fwht1_done), g.wait_ge(dS.h, d2d),
                 g.wait_ge(vM.h, bf_done)) if f else None,
                g.wait_ge(tM.h, wf) if wf else None,
                g.dma_start(out=wbf[:, half * 4096:half * 4096 + 2048],
                            in_=din["WupP"][m * 128:(m + 1) * 128, :]).then_inc(dW.h, 16),
                g.dma_start(out=wbf[:, half * 4096 + 2048:half * 4096 + 4096],
                            in_=din["WgtP"][m * 128:(m + 1) * 128, :]).then_inc(dW.h, 16)))
            nload = dW.inc(32)
            if DEBUG_UG_LVL < 0.8:
                continue
            for tf in range(2):
                b0 = ug_slc % 4
                ug_slc += 1
                em('t', lambda t, n=nload, b0=b0, bf=ug_bank_free[b0], tf=tf, f1=fwht1_done: (
                    (t.wait_ge(dW.h, n), t.wait_ge(sM.h, f1)) if tf == 0 else None,
                    (t.wait_ge({'s': sM, 'v': vM}[bf[0]].h, bf[1]) if isinstance(bf, tuple)
                     else t.wait_ge(vM.h, bf)) if bf else None))
                for k in range(16):
                    em('t', lambda t, k=k, m=m, half=half, tf=tf, b0=b0: (
                        t.matmul(PB[b0].ap()[:, 0:512],
                                 wbf[:, half * 4096 + k * 128:half * 4096 + (k + 1) * 128],
                                 xn[:, k * TH + tf * 512:k * TH + (tf + 1) * 512],
                                 start=(k == 0), stop=(k == 15)).then_inc(tM.h, 1)
                        if k == 15 else
                        t.matmul(PB[b0].ap()[:, 0:512],
                                 wbf[:, half * 4096 + k * 128:half * 4096 + (k + 1) * 128],
                                 xn[:, k * TH + tf * 512:k * TH + (tf + 1) * 512],
                                 start=(k == 0), stop=(k == 15))))
                nup = tM.inc()
                if DEBUG_UG_LVL < 0.9:
                    continue
                em('v', lambda v, n=nup, m=m, tf=tf, b0=b0: (
                    v.wait_ge(tM.h, n),
                    (v.tensor_scalar(x2[:, m * TC + tf * 512:m * TC + (tf + 1) * 512],
                                     PB[b0].ap()[:, 0:512], upscale[:, m:m + 1],
                                     None, ALU.mult) if DEBUG_UG_LVL >= 2 else
                     v.tensor_copy(x2[:, m * TC + tf * 512:m * TC + (tf + 1) * 512],
                                   PB[b0].ap()[:, 0:512])).then_inc(vM.h, 1)))
                ug_bank_free[b0] = ('v', vM.inc())
            for tf in range(2 if DEBUG_UG_LVL >= 0.95 else 0):
                b1 = ug_slc % 4
                ug_slc += 1
                em('t', lambda t, b1=b1, bf=ug_bank_free[b1], tf=tf: (
                    (t.wait_ge({'s': sM, 'v': vM}[bf[0]].h, bf[1]) if isinstance(bf, tuple)
                     else t.wait_ge(vM.h, bf)) if bf else None))
                for k in range(16):
                    em('t', lambda t, k=k, m=m, half=half, tf=tf, b1=b1: (
                        t.matmul(PB[b1].ap()[:, 0:512],
                                 wbf[:, half * 4096 + 2048 + k * 128:
                                     half * 4096 + 2048 + (k + 1) * 128],
                                 xn[:, k * TH + tf * 512:k * TH + (tf + 1) * 512],
                                 start=(k == 0), stop=(k == 15)).then_inc(tM.h, 1)
                        if k == 15 else
                        t.matmul(PB[b1].ap()[:, 0:512],
                                 wbf[:, half * 4096 + 2048 + k * 128:
                                     half * 4096 + 2048 + (k + 1) * 128],
                                 xn[:, k * TH + tf * 512:k * TH + (tf + 1) * 512],
                                 start=(k == 0), stop=(k == 15))))
                ngt = tM.inc()
                em('s', lambda s, n=ngt, m=m, tf=tf, b1=b1, half=half,
                       gf=gts_free[half]: (
                    s.wait_ge(tM.h, n),
                    s.wait_ge(vM.h, gf) if (tf == 0 and gf) else None,
                    (s.activation(gts[:, half * TC + tf * 512:half * TC + (tf + 1) * 512],
                                  PB[b1].ap()[:, 0:512], AF.Silu,
                                  scale=gtscale[:, m:m + 1]) if DEBUG_UG_LVL >= 3 else
                     s.activation(gts[:, half * TC + tf * 512:half * TC + (tf + 1) * 512],
                                  PB[b1].ap()[:, 0:512],
                                  AF.Silu)).then_inc(sM.h, 1)))
                ug_bank_free[b1] = ('s', sM.inc())
            nsil = sM.n
            if DEBUG_UG_LVL >= 0.97:
                em('v', lambda v, n=nsil, m=m, half=half: (
                    v.wait_ge(sM.h, n),
                    v.tensor_tensor(x2[:, m * TC:(m + 1) * TC], x2[:, m * TC:(m + 1) * TC],
                                    gts[:, half * TC:(half + 1) * TC],
                                    ALU.mult).then_inc(vM.h, 1)))
                gts_free[half] = vM.inc()
            wbf_free[half] = tM.n
        ug_done_v = vM.n
        ug_done_s = sM.n
        ug_done_t = tM.n

        if phases < 5.5:
            _finish_dev(nc, em, EM, taps, tap_outs, locals())
            p3u.close()
            p3b.close()
            pxn.close()
            _run_blocks(st, nc, EM)
            return nc, tap_outs

        f2_bank_free = [0, 0, 0, 0]
        slc2 = 0
        em('t', lambda t, n=ug_done_v, n2=ug_done_s: (t.wait_ge(vM.h, n),
                                                      t.wait_ge(sM.h, n2)))
        for i2 in range(ICH):
            for tf in range(2):
                b = slc2 % 4
                slc2 += 1
                em('t', lambda t, i2=i2, tf=tf, b=b, bf=f2_bank_free[b]: (
                    t.wait_ge(sM.h, bf) if bf else None,
                    t.matmul(PB[b].ap()[:, 0:512], h128b[:],
                             x2[:, i2 * TC + tf * 512:i2 * TC + (tf + 1) * 512],
                             start=True, stop=True).then_inc(tM.h, 1)))
                nf = tM.inc()
                em('s', lambda s, n=nf, i2=i2, tf=tf, b=b: (
                    s.wait_ge(tM.h, n),
                    s.activation(x2[:, i2 * TC + tf * 512:i2 * TC + (tf + 1) * 512],
                                 PB[b].ap()[:, 0:512], AF.Copy).then_inc(sM.h, 1)))
                f2_bank_free[b] = sM.inc()
        f2_done = sM.n

        if phases < 5.7:
            _finish_dev(nc, em, EM, taps, tap_outs, locals())
            p3u.close()
            p3b.close()
            pxn.close()
            _run_blocks(st, nc, EM)
            return nc, tap_outs

        p3u.close()
        p3w = ExitStack()
        wdn = p3w.enter_context(nc.sbuf_tensor("wdn", [128, 2 * 8192], BF16))
        wdn_free = [0, 0]
        dn_bank_free = [0, 0, 0, 0]
        slc3 = 0
        for m in range(16):
            half = m % 2
            em('y', lambda y, m=m, half=half, wf=wdn_free[half], f=(m == 0),
                   tU=ug_done_t, vU=ug_done_v: (
                (y.wait_ge(sM.h, f2_done), y.wait_ge(dC.h, wtil_done_d),
                 y.wait_ge(tM.h, tU), y.wait_ge(vM.h, vU)) if f else None,
                y.wait_ge(tM.h, wf) if wf else None,
                *[y.dma_start(out=wdn[:, half * 8192 + kg * 2048:
                                      half * 8192 + (kg + 1) * 2048]
                              .rearrange("p (k c) -> p k c", k=16),
                              in_=wtil_dram[kg * 2048:(kg + 1) * 2048,
                                            m * 128:(m + 1) * 128]
                              .rearrange("(k p) c -> p k c", p=128)
                              ).then_inc(dS.h, 16) for kg in range(4)]))
            nload = dS.inc(64)
            for tf in range(2):
                b = slc3 % 4
                slc3 += 1
                em('t', lambda t, n=nload, b=b, bf=dn_bank_free[b], tf=tf,
                       f2=f2_done, f=(m == 0 and tf == 0): (
                    t.wait_ge(dS.h, n) if tf == 0 else None,
                    t.wait_ge(sM.h, f2) if f else None,
                    t.wait_ge(sM.h, bf) if bf else None))
                for k in range(64):
                    em('t', lambda t, k=k, half=half, tf=tf, b=b: (
                        t.matmul(PB[b].ap()[:, 0:512],
                                 wdn[:, half * 8192 + k * 128:half * 8192 + (k + 1) * 128],
                                 x2[:, k * TC + tf * 512:k * TC + (tf + 1) * 512],
                                 start=(k == 0), stop=(k == 63)).then_inc(tM.h, 1)
                        if k == 63 else
                        t.matmul(PB[b].ap()[:, 0:512],
                                 wdn[:, half * 8192 + k * 128:half * 8192 + (k + 1) * 128],
                                 x2[:, k * TC + tf * 512:k * TC + (tf + 1) * 512],
                                 start=(k == 0), stop=(k == 63))))
                ndn = tM.inc()
                em('s', lambda s, n=ndn, m=m, tf=tf, b=b: (
                    s.wait_ge(tM.h, n),
                    s.activation(xn[:, m * 1024 + tf * 512:m * 1024 + (tf + 1) * 512],
                                 PB[b].ap()[:, 0:512], AF.Silu,
                                 scale=dnscale[:, m:m + 1]).then_inc(sM.h, 1)))
                dn_bank_free[b] = sM.inc()
            wdn_free[half] = tM.n
        g_done = sM.n

        if phases < 5.9:
            _finish_dev(nc, em, EM, taps, tap_outs, locals())
            p3w.close()
            p3b.close()
            pxn.close()
            _run_blocks(st, nc, EM)
            return nc, tap_outs

        dn_t = tM.n
        p3w.close()
        p3b.close()
        p3f = ExitStack()
        xfin = p3f.enter_context(nc.sbuf_tensor("xfin", [128, 2 * TC], F32))
        sfin = p3f.enter_context(nc.sbuf_tensor("sfin", [128, 2 * TC], BF16))
        obuf = p3f.enter_context(nc.sbuf_tensor("obuf", [128, 2 * TC], F32))
        ofin_free = [0, 0]
        xf_free2 = [0, 0]
        for m in range(16):
            half = m % 2
            em('y', lambda y, m=m, half=half, xf=xf_free2[half], f=(m == 0): (
                (y.wait_ge(sM.h, g_done), y.wait_ge(tM.h, dn_t)) if f else None,
                y.wait_ge(vM.h, xf) if xf else None,
                y.dma_start(out=xfin[:, half * TC:(half + 1) * TC],
                            in_=din["xin"].rearrange("(c p) t -> c p t", p=128)
                            [m][:, 3:3 + TC]).then_inc(dS.h, 16),
                y.dma_start(out=sfin[:, half * TC:(half + 1) * TC],
                            in_=s_dram.rearrange("(c p) t -> c p t", p=128)[m]
                            ).then_inc(dS.h, 16)))
            nxl = dS.inc(32)
            em('v', lambda v, n=nxl, m=m, half=half, of=ofin_free[half]: (
                v.wait_ge(dS.h, n),
                v.wait_ge(dS.h, of) if of else None,
                v.tensor_tensor(obuf[:, half * TC:(half + 1) * TC],
                                sfin[:, half * TC:(half + 1) * TC],
                                xn[:, m * 1024:(m + 1) * 1024], ALU.mult),
                v.tensor_add(obuf[:, half * TC:(half + 1) * TC],
                             obuf[:, half * TC:(half + 1) * TC],
                             xfin[:, half * TC:(half + 1) * TC]).then_inc(vM.h, 1)))
            nob = vM.inc()
            xf_free2[half] = nob
            em('y', lambda y, n=nob, m=m, half=half: (
                y.wait_ge(vM.h, n),
                y.dma_start(out=out.rearrange("(c p) t -> c p t", p=128)[m],
                            in_=obuf[:, half * TC:(half + 1) * TC]).then_inc(dS.h, 16)))
            ofin_free[half] = dS.inc(16)
        out_done_d = dS.n
        em('y', lambda y, n=out_done_d: y.wait_ge(dS.h, n))

        _finish_dev(nc, em, EM, taps, tap_outs, locals())
        p3f.close()
        pxn.close()
        _run_blocks(st, nc, EM)
    return nc, tap_outs


def _finish_dev(nc, em, EM, taps, tap_outs, env):
    dW, dS, dC = env["dW"], env["dS"], env["dC"]
    tM, vM, sM = env["tM"], env["vM"], env["sM"]
    fin = [(dW, dW.n), (dS, dS.n), (dC, dC.n), (tM, tM.n), (vM, vM.n), (sM, sM.n)]
    em('g', lambda g, f=fin: [g.wait_ge(s.h, n) for s, n in f if n])
    em('y', lambda y, f=fin: [y.wait_ge(s.h, n) for s, n in f if n])
    srcs = {
        "bc_dram": (env["bc_dram"], [H * 128, TC]),
        "z_dram": (env["z_dram"], [D, TC]),
        "cc_dram": (env["cc_dram"], [H * 128, TC]),
        "y_dram": (env["y_dram"], [D, TC]),
        "s_dram": (env["s_dram"], [D, TC]),
        "wtil_dram": (env["wtil_dram"], [INTER, D]),
    }
    sb_srcs = {"xc": [128, 16 * TC], "dtt": [32, TC], "xn": [128, DCH * TH],
               "hfull": [128, 2048], "snaps": [128, NCH * 2048], "hin": [128, 2048],
               "xcT": [128, NCH * 16 * 128], "sdtT": [32, 8], "ybf": [128, 2048],
               "dtT_all": [128, NCH * 32], "ldT_all": [128, NCH * 32],
               "rstdb": [32, 128], "ysq": [128, 2048], "rstdf": [32, 128]}
    for name in taps:
        if name in srcs:
            srcap, shape = srcs[name]
            t = nc.dram_tensor("tap_" + name, shape, F32, kind="ExternalOutput").ap()
            em('g', lambda g, t=t, srcap=srcap: g.dma_start(out=t, in_=srcap).then_inc(dW.h, 16))
            dW.inc(16)
        elif name in sb_srcs:
            shape = sb_srcs[name]
            t = nc.dram_tensor("tap_" + name, shape, F32, kind="ExternalOutput").ap()
            em('g', lambda g, t=t, srct=env[name], shape=shape: g.dma_start(
                out=t, in_=srct[0:shape[0], 0:shape[1]]).then_inc(dW.h, 16))
            dW.inc(16)
        tap_outs[name] = "tap_" + name
    em('g', lambda g, n=dW.n: g.wait_ge(dW.h, n))


def _run_blocks(st, nc, EM):
    block = st.enter_context(nc.Block())

    @block.tensor
    def _(t):
        for fn in EM['t']:
            fn(t)

    @block.vector
    def _(v):
        for fn in EM['v']:
            fn(v)

    @block.scalar
    def _(s):
        for fn in EM['s']:
            fn(s)

    @block.gpsimd
    def _(g):
        for fn in EM['g']:
            fn(g)

    @block.sync
    def _(y):
        for fn in EM['y']:
            fn(y)


def kernel(**inputs):
    in_maps = _host_prep(inputs)
    nc, _ = _build(phases=DEBUG_PHASES, taps=DEBUG_TAPS)
    res = run_bass_kernel_spmd(nc, in_maps, core_ids=list(range(NCORES)))
    cols = [np.asarray(res.results[k]["out"]) for k in range(NCORES)]
    full = np.concatenate(cols, axis=1)
    return np.ascontiguousarray(full.T).reshape(1, 8192, 2048).astype(np.float32)


# version 10
# speedup vs baseline: 1.0136x; 1.0136x over previous
import numpy as np
from contextlib import ExitStack
import concourse.bass as bass
import concourse.mybir as mybir
from concourse.bass_utils import run_bass_kernel_spmd

F32 = mybir.dt.float32
BF16 = mybir.dt.bfloat16
AF = mybir.ActivationFunctionType
ALU = mybir.AluOpType

NCORES = 8
TC = 1024
TH = TC + 3
D = 2048
DCH = 16
H = 32
P = 64
NS = 128
INTER = 8192
ICH = 64
L = 128
NCH = 8
PROJ = 12320
XS0, Z0, B0r, C0r, DT0 = 0, 16, 32, 64, 96

DEBUG_PHASES = 99
DEBUG_TAPS = ()
DEBUG_NO_EPI = False
DEBUG_2C_LVL = 9
DEBUG_BF_STAGES = 4
DEBUG_UG_LVL = 9


def _sylvester(n):
    h = np.array([[1.0]], dtype=np.float32)
    while h.shape[0] < n:
        h = np.block([[h, h], [h, -h]])
    return np.ascontiguousarray(h.astype(np.float32))


def _host_prep(inputs):
    x = np.asarray(inputs["L0_attn_x"], dtype=np.float32)[0]
    xT = np.ascontiguousarray(x.T)
    xpad = np.concatenate([np.zeros((D, 3), np.float32), xT], axis=1)

    WinT = np.asarray(inputs["L0_attn_Win"], np.float32).T
    WoutT = np.asarray(inputs["L0_attn_Wout"], np.float32).T
    WupT = np.asarray(inputs["L0_ffn_W_up"], np.float32).T
    WgtT = np.asarray(inputs["L0_ffn_W_gate"], np.float32).T
    Wp1 = np.ascontiguousarray(
        WinT[:, :PROJ - 32].reshape(4, 4, 128, 48, 256)
        .transpose(3, 0, 2, 1, 4).reshape(48 * 4 * 128, 1024))
    WdtP = np.ascontiguousarray(
        WinT[:, PROJ - 32:].reshape(16, 128, 32).transpose(1, 0, 2).reshape(128, 512))
    Wp2 = np.ascontiguousarray(
        WoutT.reshape(4, 4, 128, 8, 256).transpose(3, 0, 2, 1, 4).reshape(4096, 1024))
    WupP = np.ascontiguousarray(
        WupT.reshape(16, 128, 64, 128).transpose(2, 1, 0, 3).reshape(INTER, D))
    WgtP = np.ascontiguousarray(
        WgtT.reshape(16, 128, 64, 128).transpose(2, 1, 0, 3).reshape(INTER, D))
    WdT = np.asarray(inputs["L0_ffn_W_down"], np.float32).T
    orig = (np.arange(64)[None, :] * 128 + np.arange(128)[:, None]).reshape(-1)
    WdP = np.ascontiguousarray(WdT[orig])

    cwx = np.asarray(inputs["L0_attn_conv_w_x"], np.float32).reshape(D, 4)
    cwx_t = np.ascontiguousarray(cwx.reshape(16, 128, 4).transpose(1, 0, 2).reshape(128, 64))
    cwb = np.asarray(inputs["L0_attn_conv_w_b"], np.float32).reshape(4096, 4)
    cwb_t = np.ascontiguousarray(cwb.reshape(32, 128, 4).transpose(1, 0, 2).reshape(128, 128))
    cwc = np.asarray(inputs["L0_attn_conv_w_c"], np.float32).reshape(4096, 4)
    cwc_t = np.ascontiguousarray(cwc.reshape(32, 128, 4).transpose(1, 0, 2).reshape(128, 128))

    A_log = np.ascontiguousarray(np.asarray(inputs["L0_attn_A_log"], np.float32).reshape(H, 1))
    dt_bias = np.ascontiguousarray(np.asarray(inputs["L0_attn_dt_bias"], np.float32).reshape(H, 1))

    suug_t = np.ascontiguousarray(np.asarray(inputs["L0_ffn_SU_ug"], np.float32).reshape(16, 128).T)
    sudp_t = np.ascontiguousarray(np.asarray(inputs["L0_ffn_SU_dp"], np.float32).reshape(64, 128).T)
    wsug = np.asarray(inputs["L0_ffn_Wscale_ug"], np.float32)
    wsup_t = np.ascontiguousarray(wsug[:INTER].reshape(64, 128).T)
    wsgt_t = np.ascontiguousarray(wsug[INTER:].reshape(64, 128).T)
    wsdp_t = np.ascontiguousarray(
        np.asarray(inputs["L0_ffn_Wscale_dp"], np.float32).reshape(16, 128).T)

    bo = np.zeros((128, 16 * 32), np.float32)
    sel = np.zeros((32, 16 * 128), np.float32)
    for j in range(16):
        for p in range(128):
            h = 2 * j + (1 if p >= 64 else 0)
            bo[p, j * 32 + h] = 1.0
            sel[h, j * 128 + p] = 1.0

    shared = {
        "Wp1": Wp1, "WdtP": WdtP, "Wp2": Wp2, "WupP": WupP, "WgtP": WgtP, "WdP": WdP,
        "cwx_t": cwx_t, "cwb_t": cwb_t, "cwc_t": cwc_t,
        "A_log": A_log, "dt_bias": dt_bias,
        "suug_t": suug_t, "sudp_t": sudp_t,
        "wsup_t": wsup_t, "wsgt_t": wsgt_t, "wsdp_t": wsdp_t,
        "H128": _sylvester(128), "H64": _sylvester(64),
        "H64blk": np.ascontiguousarray(np.kron(np.eye(2, dtype=np.float32), _sylvester(64))),
        "IDENT": np.eye(128, dtype=np.float32),
        "TRIU": np.ascontiguousarray(np.triu(np.ones((128, 128), np.float32))),
        "BO": np.ascontiguousarray(bo), "SEL": np.ascontiguousarray(sel),
    }
    in_maps = []
    for k in range(NCORES):
        m = dict(shared)
        m["xin"] = np.ascontiguousarray(xpad[:, TC * k: TC * k + TH])
        MkM = np.zeros((8, 8), np.float32)
        for j in range(8):
            for i in range(8):
                if j < i <= k - 1:
                    MkM[j, i] = 1.0
        m["MkT"] = np.ascontiguousarray(MkM.T)
        cm = np.zeros((1, 256), np.float32)
        for j in range(8):
            if j < k:
                cm[0, j * 32: j * 32 + 32] = 1.0
        m["coremask"] = cm
        in_maps.append(m)
    return in_maps


class Sem:
    def __init__(self, handle):
        self.h = handle
        self.n = 0

    def inc(self, k=1):
        self.n += k
        return self.n


def _build(phases=99, taps=()):
    nc = bass.Bass("TRN2", target_bir_lowering=False)
    taps = list(taps)
    tap_outs = {}

    din = {}
    for name, shape in [
        ("xin", [D, TH]), ("Wp1", [48 * 4 * 128, 1024]), ("WdtP", [128, 512]),
        ("Wp2", [4096, 1024]),
        ("WupP", [INTER, D]), ("WgtP", [INTER, D]), ("WdP", [INTER, D]),
        ("cwx_t", [128, 64]), ("cwb_t", [128, 128]), ("cwc_t", [128, 128]),
        ("A_log", [H, 1]), ("dt_bias", [H, 1]),
        ("suug_t", [128, 16]), ("sudp_t", [128, 64]),
        ("wsup_t", [128, 64]), ("wsgt_t", [128, 64]), ("wsdp_t", [128, 16]),
        ("H128", [128, 128]), ("H64", [64, 64]), ("H64blk", [128, 128]),
        ("IDENT", [128, 128]),
        ("TRIU", [128, 128]), ("BO", [128, 512]), ("SEL", [32, 2048]),
        ("MkT", [8, 8]), ("coremask", [1, 256]),
    ]:
        din[name] = nc.dram_tensor(name, shape, F32, kind="ExternalInput").ap()

    out = nc.dram_tensor("out", [D, TC], F32, kind="ExternalOutput").ap()

    z_dram = nc.dram_tensor("z_dram", [D, TC], BF16).ap()
    bc_dram = nc.dram_tensor("bc_dram", [H * 128, TC], BF16).ap()
    cc_dram = nc.dram_tensor("cc_dram", [H * 128, TC], BF16).ap()
    y_dram = nc.dram_tensor("y_dram", [D, TC], BF16).ap()
    s_dram = nc.dram_tensor("s_dram", [D, TC], BF16).ap()
    wtil_dram = nc.dram_tensor("wtil_dram", [INTER, D], BF16).ap()
    ldb_dram = nc.dram_tensor("ldb_dram", [NCH * 32, 128], F32).ap()
    ag_in_h = nc.dram_tensor("ag_in_h", [128, H * P], F32).ap()
    ag_in_s = nc.dram_tensor("ag_in_s", [32, 1], F32).ap()
    ag_out_h = nc.dram_tensor("ag_out_h", [NCORES * 128, H * P], F32,
                              addr_space="Shared").ap()
    ag_out_s = nc.dram_tensor("ag_out_s", [NCORES * 32, 1], F32,
                              addr_space="Shared").ap()
    db_dram = nc.dram_tensor("db_dram", [8, 32], F32).ap()
    pc_dram = nc.dram_tensor("pc_dram", [H, 8], F32).ap()

    EM = {k: [] for k in "tvsgy"}

    def em(eng, fn):
        EM[eng].append(fn)

    with ExitStack() as st:
        SB = lambda n, s, d: st.enter_context(nc.sbuf_tensor(n, s, d))
        SEM = lambda n: Sem(st.enter_context(nc.semaphore(n)))

        dW = SEM("dW")
        dS = SEM("dS")
        dC = SEM("dC")
        tM = SEM("tM")
        vM = SEM("vM")
        sM = SEM("sM")
        cS = SEM("cS")

        cwxS = SB("cwxS", [128, 64], F32)
        cwbS = SB("cwbS", [128, 128], F32)
        cwcS = SB("cwcS", [128, 128], F32)
        Acol = SB("Acol", [H, 1], F32)
        dtb = SB("dtb", [H, 1], F32)
        suugS = SB("suugS", [128, 16], F32)
        sudpS = SB("sudpS", [128, 64], F32)
        wsupS = SB("wsupS", [128, 64], F32)
        wsgtS = SB("wsgtS", [128, 64], F32)
        wsdpS = SB("wsdpS", [128, 16], F32)
        upscale = SB("upscale", [128, 64], F32)
        gtscale = SB("gtscale", [128, 64], F32)
        dnscale = SB("dnscale", [128, 16], F32)
        h128b = SB("h128b", [128, 128], BF16)
        h128su = SB("h128su", [128, 16 * 128], BF16)
        h64b = SB("h64b", [64, 64], BF16)
        h64blkb = SB("h64blkb", [128, 128], BF16)
        idb = SB("idb", [128, 128], BF16)
        idf = SB("idf", [128, 128], F32)
        ones1b = SB("ones1b", [1, 128], BF16)
        ones1f = SB("ones1f", [1, 128], F32)
        onescolb = SB("onescolb", [128, 1], BF16)
        zeros32 = SB("zeros32", [32, 128], F32)
        epsln = SB("epsln", [1, 1], F32)
        epsrms = SB("epsrms", [32, 1], F32)

        PB = [st.enter_context(nc.psum_tensor(f"pb{i}", [128, 512], F32)) for i in range(8)]
        PBb = [PB[i].ap().bitcast(BF16) for i in range(8)]

        pxn = ExitStack()
        xn = pxn.enter_context(nc.sbuf_tensor("xn", [128, DCH * TH], BF16))
        pk2 = ExitStack()
        xcT = pk2.enter_context(nc.sbuf_tensor("xcT", [128, NCH * 16 * 128], BF16))
        snaps = pk2.enter_context(nc.sbuf_tensor("snaps", [128, NCH * 2048], BF16))
        hfull = pk2.enter_context(nc.sbuf_tensor("hfull", [128, 2048], F32))
        dtT_all = pk2.enter_context(nc.sbuf_tensor("dtT_all", [128, NCH * 32], F32))
        ldT_all = pk2.enter_context(nc.sbuf_tensor("ldT_all", [128, NCH * 32], F32))
        ldTneg_all = pk2.enter_context(nc.sbuf_tensor("ldTneg_all", [128, NCH * 32], F32))
        sdtT = pk2.enter_context(nc.sbuf_tensor("sdtT", [32, 8], F32))
        pxc = ExitStack()
        xc = pxc.enter_context(nc.sbuf_tensor("xc", [128, 16 * TC], BF16))
        dtt = pxc.enter_context(nc.sbuf_tensor("dtt", [32, TC], F32))

        p0 = ExitStack()
        lnrow = p0.enter_context(nc.sbuf_tensor("lnrow", [1, 6 * TH], F32))
        xfs = p0.enter_context(nc.sbuf_tensor("xfs", [128, 2 * TH], F32))
        xbs = p0.enter_context(nc.sbuf_tensor("xbs", [128, 2 * TH], BF16))
        xsq = p0.enter_context(nc.sbuf_tensor("xsq", [128, 2 * TH], BF16))

        const_loads = [
            (cwxS, "cwx_t"), (cwbS, "cwb_t"), (cwcS, "cwc_t"), (dtb, "dt_bias"),
            (suugS, "suug_t"), (sudpS, "sudp_t"), (wsupS, "wsup_t"),
            (wsgtS, "wsgt_t"), (wsdpS, "wsdp_t"), (h128b, "H128"), (h64b, "H64"),
            (h64blkb, "H64blk"), (idb, "IDENT"), (idf, "IDENT"), (Acol, "A_log"),
        ]
        for dst, srcn in const_loads:
            em('g', lambda g, d=dst, s=din[srcn]: g.dma_start(out=d[:], in_=s).then_inc(dW.h, 16))
        dW.inc(16 * len(const_loads))
        n_const = dW.n

        em('s', lambda s: s.wait_ge(dW.h, n_const))
        em('s', lambda s: s.activation(Acol[:], Acol[:], AF.Exp))
        em('s', lambda s: s.activation(Acol[:], Acol[:], AF.Copy, scale=-1.0).then_inc(sM.h, 1))
        sA = sM.inc()

        em('v', lambda v: v.wait_ge(dW.h, n_const))
        em('v', lambda v: v.memset(ones1b[:], 1.0))
        em('v', lambda v: v.memset(ones1f[:], 1.0))
        em('v', lambda v: v.memset(onescolb[:], 1.0))
        em('v', lambda v: v.memset(zeros32[:], 0.0))
        em('v', lambda v: v.memset(epsln[:], 1e-5))
        em('v', lambda v: v.memset(epsrms[:], 1e-6))
        for c in range(16):
            em('v', lambda v, c=c: v.tensor_scalar(
                h128su[:, c * 128:(c + 1) * 128], h128b[:], suugS[:, c:c + 1], None, ALU.mult))
        em('v', lambda v: v.tensor_mul(upscale[:], wsupS[:], sudpS[:]))
        em('v', lambda v: v.tensor_scalar(upscale[:], upscale[:], 1.0 / np.sqrt(2048.0), None, ALU.mult))
        em('v', lambda v: v.tensor_scalar(gtscale[:], wsgtS[:], 1.0 / np.sqrt(2048.0), None, ALU.mult))
        em('v', lambda v: v.tensor_scalar(dnscale[:], wsdpS[:], 1.0 / np.sqrt(8192.0), None,
                                          ALU.mult).then_inc(vM.h, 1))
        vConsts = vM.inc()

        SL = [(0, 343), (343, 342), (685, 342)]
        xb_free = [0, 0]
        for j in range(DCH):
            hb = j % 2
            ld_wait = max(xb_free[hb], 0)
            em('g', lambda g, j=j, hb=hb, w=ld_wait: (
                g.wait_ge(tM.h, w) if w else None,
                g.dma_start(out=xbs[:, hb * TH:hb * TH + TH],
                            in_=din["xin"].rearrange("(c p) t -> c p t", p=128)[j]
                            ).then_inc(dW.h, 16)))
            nl = dW.inc(16)
            em('s', lambda s, hb=hb, nl=nl, w=xb_free[hb]: (
                s.wait_ge(dW.h, nl),
                s.wait_ge(tM.h, w) if w else None,
                s.activation(xsq[:, hb * TH:hb * TH + TH], xbs[:, hb * TH:hb * TH + TH],
                             AF.Square).then_inc(sM.h, 1)))
            nsq = sM.inc()
            em('t', lambda t, nl=nl, nsq=nsq: (t.wait_ge(dW.h, nl), t.wait_ge(sM.h, nsq)))
            for si, (off, w) in enumerate(SL):
                last = (si == len(SL) - 1)
                em('t', lambda t, j=j, hb=hb, si=si, off=off, w=w, last=last: (
                    t.matmul(PB[si].ap()[0:1, 0:w], onescolb[:],
                             xbs[:, hb * TH + off: hb * TH + off + w],
                             start=(j == 0), stop=(j == DCH - 1)),
                    t.matmul(PB[3 + si].ap()[0:1, 0:w], onescolb[:],
                             xsq[:, hb * TH + off: hb * TH + off + w],
                             start=(j == 0), stop=(j == DCH - 1)
                             ).then_inc(tM.h, 1) if last else
                    t.matmul(PB[3 + si].ap()[0:1, 0:w], onescolb[:],
                             xsq[:, hb * TH + off: hb * TH + off + w],
                             start=(j == 0), stop=(j == DCH - 1))))
            xb_free[hb] = tM.inc()
        stats_done = tM.n

        em('v', lambda v: v.wait_ge(tM.h, stats_done))
        for si, (off, w) in enumerate(SL):
            last = (si == len(SL) - 1)
            em('v', lambda v, si=si, off=off, w=w, last=last: (
                v.tensor_scalar(lnrow[0:1, off:off + w], PB[si].ap()[0:1, 0:w],
                                1.0 / 2048.0, None, ALU.mult),
                v.tensor_mul(lnrow[0:1, 2 * TH + off:2 * TH + off + w],
                             lnrow[0:1, off:off + w], lnrow[0:1, off:off + w]),
                (v.scalar_tensor_tensor(lnrow[0:1, 3 * TH + off:3 * TH + off + w],
                                        PB[3 + si].ap()[0:1, 0:w], 1.0 / 2048.0,
                                        lnrow[0:1, 2 * TH + off:2 * TH + off + w],
                                        ALU.mult, ALU.subtract).then_inc(vM.h, 1)
                 if last else
                 v.scalar_tensor_tensor(lnrow[0:1, 3 * TH + off:3 * TH + off + w],
                                        PB[3 + si].ap()[0:1, 0:w], 1.0 / 2048.0,
                                        lnrow[0:1, 2 * TH + off:2 * TH + off + w],
                                        ALU.mult, ALU.subtract))))
        vVar = vM.inc()
        em('s', lambda s: s.wait_ge(vM.h, vVar))
        em('s', lambda s: (
            s.activation(lnrow[0:1, 2 * TH:3 * TH], lnrow[0:1, 3 * TH:4 * TH],
                         AF.Ln, bias=epsln[0:1, 0:1]),
            s.activation(lnrow[0:1, TH:2 * TH], lnrow[0:1, 2 * TH:3 * TH],
                         AF.Exp, scale=-0.5).then_inc(sM.h, 1)))
        vRstd = sM.inc()

        em('t', lambda t: t.wait_ge(sM.h, vRstd))
        for si, (off, w) in enumerate(SL):
            last = (si == len(SL) - 1)
            em('t', lambda t, si=si, off=off, w=w, last=last: (
                t.matmul(PB[si].ap()[:, 0:w], ones1f[:],
                         lnrow[0:1, off:off + w], start=True, stop=True),
                (t.matmul(PB[3 + si].ap()[:, 0:w], ones1f[:],
                          lnrow[0:1, TH + off:TH + off + w], start=True, stop=True
                          ).then_inc(tM.h, 1)
                 if last else
                 t.matmul(PB[3 + si].ap()[:, 0:w], ones1f[:],
                          lnrow[0:1, TH + off:TH + off + w], start=True, stop=True))))
        tBcast = tM.inc()

        xf_free = [0, 0]
        em('v', lambda v: v.wait_ge(tM.h, tBcast))
        for j in range(DCH):
            hb = j % 2
            em('y', lambda y, j=j, hb=hb, w=xf_free[hb]: (
                y.wait_ge(vM.h, w) if w else None,
                y.dma_start(out=xfs[:, hb * TH:hb * TH + TH],
                            in_=din["xin"].rearrange("(c p) t -> c p t", p=128)[j]
                            ).then_inc(dS.h, 16)))
            nl = dS.inc(16)
            em('v', lambda v, nl=nl: v.wait_ge(dS.h, nl))
            for si, (off, w) in enumerate(SL):
                last = (si == len(SL) - 1)
                em('v', lambda v, j=j, hb=hb, si=si, off=off, w=w, last=last: (
                    v.tensor_sub(xfs[:, hb * TH + off:hb * TH + off + w],
                                 xfs[:, hb * TH + off:hb * TH + off + w],
                                 PB[si].ap()[:, 0:w]),
                    (v.tensor_tensor(xn[:, j * TH + off:j * TH + off + w],
                                     xfs[:, hb * TH + off:hb * TH + off + w],
                                     PB[3 + si].ap()[:, 0:w], ALU.mult).then_inc(vM.h, 1)
                     if last else
                     v.tensor_tensor(xn[:, j * TH + off:j * TH + off + w],
                                     xfs[:, hb * TH + off:hb * TH + off + w],
                                     PB[3 + si].ap()[:, 0:w], ALU.mult))))
            xf_free[hb] = vM.inc()
        xn_done_v = vM.n

        p0.close()
        p1 = ExitStack()
        wbuf = p1.enter_context(nc.sbuf_tensor("wbuf", [128, 2 * 4096], BF16))
        wdt = p1.enter_context(nc.sbuf_tensor("wdt", [128, 512], BF16))
        pre = p1.enter_context(nc.sbuf_tensor("pre", [128, 4 * TH], BF16))
        accb = p1.enter_context(nc.sbuf_tensor("accb", [128, 2 * TC], F32))
        spb = p1.enter_context(nc.sbuf_tensor("spb", [128, 3 * TC], BF16))
        dtw = p1.enter_context(nc.sbuf_tensor("dtw", [32, TC], F32))

        mlist = (list(range(XS0, XS0 + 16)) + [DT0] + list(range(B0r, B0r + 32))
                 + list(range(C0r, C0r + 32)) + list(range(Z0, Z0 + 16)))

        bank_free = [0, 0, 0, 0]
        slice_ctr = 0
        wbuf_free = [0, 0]
        pre_free = [0, 0, 0, 0]
        accb_free = [0, 0]
        spb_free = [0, 0, 0]
        pjc = 0
        spill_ctr = 0
        first_w_dma = True
        dt_evac_done = 0

        groups = []
        i = 0
        while i < len(mlist):
            mj = mlist[i]
            if mj == DT0:
                groups.append(("dt", [mj]))
                i += 1
            else:
                groups.append(("n", [mlist[i], mlist[i + 1]]))
                i += 2

        em('t', lambda t, n=xn_done_v: t.wait_ge(vM.h, n))
        gi = 0
        for kind, ms in groups:
            half = gi % 2
            waits = []
            if first_w_dma:
                waits.append(('v', xn_done_v))
            if wbuf_free[half]:
                waits.append(('t', wbuf_free[half]))
            wv = list(waits)
            if kind == "dt":
                em('g', lambda g, wv=wv: (
                    [g.wait_ge({'v': vM, 't': tM}[e].h, n) for e, n in wv],
                    g.dma_start(out=wdt[:],
                                in_=din["WdtP"][:, :]).then_inc(dW.h, 16)))
                dW.inc(16)
            else:
                rb0 = (ms[0] // 2) * 4
                for kg in range(4):
                    wvk = wv if kg == 0 else []
                    em('g', lambda g, kg=kg, rb0=rb0, half=half, wvk=wvk: (
                        [g.wait_ge({'v': vM, 't': tM}[e].h, n) for e, n in wvk],
                        g.dma_start(
                            out=wbuf[:, half * 4096 + kg * 1024: half * 4096 + (kg + 1) * 1024],
                            in_=din["Wp1"][(rb0 + kg) * 128:(rb0 + kg + 1) * 128, :]
                            ).then_inc(dW.h, 16)))
                    dW.inc(16)
            first_w_dma = False
            nload = dW.n

            for mj in ms:
                is_conv = (XS0 <= mj < 16) or (B0r <= mj < 96)
                if kind == "dt":
                    slices = [(3, 512, 0), (515, 512, 512)]
                elif is_conv:
                    slices = [(0, 343, 0), (343, 342, 343), (685, 342, 685)]
                else:
                    slices = [(3, 512, 0), (515, 512, 512)]
                ev_vals = []
                for (off, w, dcol) in slices:
                    b = slice_ctr % 4
                    slice_ctr += 1
                    em('t', lambda t, nload=nload, bf=bank_free[b]: (
                        t.wait_ge(dW.h, nload),
                        t.wait_ge(sM.h, bf) if bf else None))
                    for k in range(16):
                        if kind == "dt":
                            lhsT = lambda k=k: wdt[:, k * 32:(k + 1) * 32]
                            M = 32
                        else:
                            jj = ms.index(mj)
                            kg, k2 = divmod(k, 4)
                            base = half * 4096 + kg * 1024 + k2 * 256 + jj * 128
                            lhsT = lambda base=base: wbuf[:, base:base + 128]
                            M = 128
                        em('t', lambda t, k=k, lhsT=lhsT, M=M, b=b, off=off, w=w: (
                            t.matmul(PB[b].ap()[0:M, 0:w], lhsT(),
                                     xn[:, k * TH + off:k * TH + off + w],
                                     start=(k == 0), stop=(k == 15)
                                     ).then_inc(tM.h, 1) if k == 15 else
                            t.matmul(PB[b].ap()[0:M, 0:w], lhsT(),
                                     xn[:, k * TH + off:k * TH + off + w],
                                     start=(k == 0), stop=(k == 15))))
                    ntm = tM.inc()
                    if kind == "dt":
                        em('s', lambda s, ntm=ntm, b=b, w=w, dcol=dcol: (
                            s.wait_ge(tM.h, ntm),
                            s.activation(dtt[0:32, dcol:dcol + w], PB[b].ap()[0:32, 0:w],
                                         AF.Copy).then_inc(sM.h, 1)))
                    elif is_conv:
                        slot = pjc % 4
                        em('s', lambda s, ntm=ntm, b=b, w=w, off=off, slot=slot,
                               pf=pre_free[slot]: (
                            s.wait_ge(tM.h, ntm),
                            s.wait_ge(vM.h, pf) if pf else None,
                            s.activation(pre[:, slot * TH + off:slot * TH + off + w],
                                         PB[b].ap()[:, 0:w], AF.Copy).then_inc(sM.h, 1)))
                    else:
                        slot = spill_ctr % 3
                        em('s', lambda s, ntm=ntm, b=b, w=w, dcol=dcol, slot=slot,
                               sf=spb_free[slot]: (
                            s.wait_ge(tM.h, ntm),
                            s.wait_ge(dS.h, sf) if sf else None,
                            s.activation(spb[:, slot * TC + dcol:slot * TC + dcol + w],
                                         PB[b].ap()[:, 0:w], AF.Copy).then_inc(sM.h, 1)))
                    bank_free[b] = sM.inc()
                    ev_vals.append(bank_free[b])
                if kind == "dt":
                    dt_evac_done = ev_vals[-1]
                if is_conv:
                    slot = pjc % 4
                    ah = pjc % 2
                    if mj < 16:
                        cw, cwcol0 = cwxS, mj * 4
                        dst = lambda mj=mj: xc[:, mj * TC:(mj + 1) * TC]
                        spill = None
                    elif B0r <= mj < C0r:
                        cw, cwcol0 = cwbS, (mj - B0r) * 4
                        sslot = spill_ctr % 3
                        dst = lambda sslot=sslot: spb[:, sslot * TC:(sslot + 1) * TC]
                        spill = ("bc", mj - B0r, sslot)
                    else:
                        cw, cwcol0 = cwcS, (mj - C0r) * 4
                        sslot = spill_ctr % 3
                        dst = lambda sslot=sslot: spb[:, sslot * TC:(sslot + 1) * TC]
                        spill = ("cc", mj - C0r, sslot)
                    em('v', lambda v, ev=ev_vals[-1], af=accb_free[ah], slot=slot, ah=ah,
                           cw=cw, c0=cwcol0: (
                        v.wait_ge(sM.h, ev),
                        v.wait_ge(sM.h, af) if af else None,
                        v.tensor_scalar(accb[:, ah * TC:(ah + 1) * TC],
                                        pre[:, slot * TH + 3:slot * TH + 3 + TC],
                                        cw[:, c0 + 3:c0 + 4], None, ALU.mult),
                        v.scalar_tensor_tensor(accb[:, ah * TC:(ah + 1) * TC],
                                               pre[:, slot * TH + 2:slot * TH + 2 + TC],
                                               cw[:, c0 + 2:c0 + 3],
                                               accb[:, ah * TC:(ah + 1) * TC],
                                               ALU.mult, ALU.add),
                        v.scalar_tensor_tensor(accb[:, ah * TC:(ah + 1) * TC],
                                               pre[:, slot * TH + 1:slot * TH + 1 + TC],
                                               cw[:, c0 + 1:c0 + 2],
                                               accb[:, ah * TC:(ah + 1) * TC],
                                               ALU.mult, ALU.add),
                        v.scalar_tensor_tensor(accb[:, ah * TC:(ah + 1) * TC],
                                               pre[:, slot * TH:slot * TH + TC],
                                               cw[:, c0:c0 + 1],
                                               accb[:, ah * TC:(ah + 1) * TC],
                                               ALU.mult, ALU.add).then_inc(vM.h, 1)))
                    pre_free[slot] = vM.inc()
                    if spill is None:
                        em('s', lambda s, vv=pre_free[slot], ah=ah, dst=dst: (
                            s.wait_ge(vM.h, vv),
                            s.activation(dst(), accb[:, ah * TC:(ah + 1) * TC],
                                         AF.Silu).then_inc(sM.h, 1)))
                        accb_free[ah] = sM.inc()
                    else:
                        skind, hh, sslot = spill
                        sdram = cc_dram if skind == "cc" else bc_dram
                        em('s', lambda s, vv=pre_free[slot], ah=ah, dst=dst,
                               sf=spb_free[sslot]: (
                            s.wait_ge(vM.h, vv),
                            s.wait_ge(dS.h, sf) if sf else None,
                            s.activation(dst(), accb[:, ah * TC:(ah + 1) * TC],
                                         AF.Silu).then_inc(sM.h, 1)))
                        accb_free[ah] = sM.inc()
                        em('y', lambda y, sv=accb_free[ah], hh=hh, sslot=sslot, sdram=sdram: (
                            y.wait_ge(sM.h, sv),
                            y.dma_start(out=sdram.rearrange("(h p) t -> h p t", p=128)[hh],
                                        in_=spb[:, sslot * TC:(sslot + 1) * TC]
                                        ).then_inc(dS.h, 16)))
                        spb_free[sslot] = dS.inc(16)
                        spill_ctr += 1
                    pjc += 1
                elif kind == "dt":
                    pass
                else:
                    sslot = spill_ctr % 3
                    em('y', lambda y, sv=ev_vals[-1], hh=mj - Z0, sslot=sslot: (
                        y.wait_ge(sM.h, sv),
                        y.dma_start(out=z_dram.rearrange("(h p) t -> h p t", p=128)[hh],
                                    in_=spb[:, sslot * TC:(sslot + 1) * TC]
                                    ).then_inc(dS.h, 16)))
                    spb_free[sslot] = dS.inc(16)
                    spill_ctr += 1
            if kind != "dt":
                wbuf_free[half] = tM.n
            gi += 1

        em('v', lambda v, n=dt_evac_done: (
            v.wait_ge(sM.h, n),
            v.tensor_scalar(dtt[0:32, 0:TC], dtt[0:32, 0:TC], dtb[:], None,
                            ALU.add).then_inc(vM.h, 1)))
        vDtB = vM.inc()
        em('s', lambda s, n=vDtB: (
            s.wait_ge(vM.h, n),
            s.activation(dtw[0:32, 0:TC], dtt[0:32, 0:TC], AF.Abs),
            s.activation(dtw[0:32, 0:TC], dtw[0:32, 0:TC], AF.Exp, scale=-1.0),
            s.activation(dtw[0:32, 0:TC], dtw[0:32, 0:TC], AF.Ln,
                         bias=1.0).then_inc(sM.h, 1)))
        sDt1 = sM.inc()
        em('v', lambda v: (
            v.wait_ge(sM.h, sDt1),
            v.tensor_relu(dtt[0:32, 0:TC], dtt[0:32, 0:TC]),
            v.tensor_add(dtt[0:32, 0:TC], dtt[0:32, 0:TC],
                         dtw[0:32, 0:TC]).then_inc(vM.h, 1)))
        vDt = vM.inc()

        p1w = ExitStack()
        wdb = p1w.enter_context(nc.sbuf_tensor("wdb", [128, 2 * D], BF16))
        wtb = p1w.enter_context(nc.sbuf_tensor("wtb", [128, D], BF16))
        wdb_free = [0, 0]
        wtb_free = 0
        ev_v = ev_s = 0
        for r in range(64):
            hb = r % 2
            em('g', lambda g, r=r, hb=hb, wf=wdb_free[hb]: (
                g.wait_ge(tM.h, wf) if wf else None,
                g.dma_start(out=wdb[:, hb * D:(hb + 1) * D],
                            in_=din["WdP"][r * 128:(r + 1) * 128, :]).then_inc(dW.h, 16)))
            nl = dW.inc(16)
            em('t', lambda t, nl=nl, ev=ev_v, es=ev_s: (
                t.wait_ge(dW.h, nl),
                t.wait_ge(vM.h, ev) if ev else None,
                t.wait_ge(sM.h, es) if es else None))
            for q in range(4):
                em('t', lambda t, hb=hb, q=q: (
                    t.matmul(PB[4 + q].ap()[:, 0:512], h64blkb[:],
                             wdb[:, hb * D + q * 512:hb * D + (q + 1) * 512],
                             start=True, stop=True).then_inc(tM.h, 1)
                    if q == 3 else
                    t.matmul(PB[4 + q].ap()[:, 0:512], h64blkb[:],
                             wdb[:, hb * D + q * 512:hb * D + (q + 1) * 512],
                             start=True, stop=True)))
            wdb_free[hb] = tM.inc()
            em('v', lambda v, ntm=wdb_free[hb], wf=wtb_free: (
                v.wait_ge(tM.h, ntm),
                v.wait_ge(dC.h, wf) if wf else None,
                v.tensor_copy(wtb[:, 0:512], PB[4].ap()[:, 0:512]),
                v.tensor_copy(wtb[:, 512:1024], PB[5].ap()[:, 0:512]).then_inc(vM.h, 1)))
            ev_v = vM.inc()
            em('s', lambda s, ntm=wdb_free[hb], wf=wtb_free: (
                s.wait_ge(tM.h, ntm),
                s.wait_ge(dC.h, wf) if wf else None,
                s.activation(wtb[:, 1024:1536], PB[6].ap()[:, 0:512], AF.Copy),
                s.activation(wtb[:, 1536:2048], PB[7].ap()[:, 0:512],
                             AF.Copy).then_inc(sM.h, 1)))
            ev_s = sM.inc()
            em('s', lambda s, vv=ev_v, r=r: (
                s.wait_ge(vM.h, vv),
                s.dma_start(out=wtil_dram.rearrange("(c p) g -> c p g", p=128)[:, 2 * r, :],
                            in_=wtb[0:64, :]).then_inc(dC.h, 16),
                s.dma_start(out=wtil_dram.rearrange("(c p) g -> c p g", p=128)[:, 2 * r + 1, :],
                            in_=wtb[64:128, :]).then_inc(dC.h, 16)))
            wtb_free = dC.inc(32)
        wtil_done_d = dC.n

        p1w.close()
        p1.close()
        p1_spills_d = dS.n
        p2a = ExitStack()
        bcbuf = p2a.enter_context(nc.sbuf_tensor("bcbuf", [128, 2 * 4096], BF16))
        bcTall = p2a.enter_context(nc.sbuf_tensor("bcTall", [128, 4096], BF16))
        ld32 = p2a.enter_context(nc.sbuf_tensor("ld32", [32, 128], F32))
        dta = p2a.enter_context(nc.sbuf_tensor("dta", [32, 128], F32))
        w32pre = p2a.enter_context(nc.sbuf_tensor("w32pre", [32, 128], F32))
        e32t = p2a.enter_context(nc.sbuf_tensor("e32t", [32, 128], F32))
        W32 = p2a.enter_context(nc.sbuf_tensor("W32", [128, 32], F32))
        lamb = p2a.enter_context(nc.sbuf_tensor("lamb", [128, 32], F32))
        lamRowS = p2a.enter_context(nc.sbuf_tensor("lamRowS", [1, 32], F32))
        lamCol = p2a.enter_context(nc.sbuf_tensor("lamCol", [32, 1], F32))
        dtxw = p2a.enter_context(nc.sbuf_tensor("dtxw", [128, 2 * 64], BF16))

        em('v', lambda v: v.memset(hfull[:], 0.0).then_inc(vM.h, 1))
        hfull_z = vM.inc()

        bcb_free = [0, 0]
        pb4_free = 0
        pb5_free = 0
        pbb_free = [0, 0]
        dh_free = [0, 0, 0, 0]
        ld32_free = 0
        dtxw_free = [0, 0]
        e_done = hfull_z
        ldb_val = []

        for c in range(NCH):
            hb = c % 2
            em('y', lambda y, c=c, hb=hb, bf=bcb_free[hb], pd=p1_spills_d: (
                y.wait_ge(dS.h, pd),
                y.wait_ge(tM.h, bf) if bf else None,
                y.dma_start(out=bcbuf[:, hb * 4096:(hb + 1) * 4096]
                            .rearrange("n (h t) -> n h t", h=32),
                            in_=bc_dram.rearrange("(h n) t -> n h t", n=128)
                            [:, :, c * 128:(c + 1) * 128]).then_inc(dS.h, 16)))
            nbc = dS.inc(16)

            em('v', lambda v, c=c, lf=ld32_free: (
                v.wait_ge(dS.h, lf) if lf else None,
                v.tensor_scalar(dta[:], dtt[0:32, c * 128:(c + 1) * 128], Acol[:],
                                None, ALU.mult),
                v.tensor_tensor_scan(ld32[:], dta[:], zeros32[:], 0.0, ALU.add, ALU.add),
                v.reduce_sum(sdtT[0:32, c:c + 1], dtt[0:32, c * 128:(c + 1) * 128],
                             axis=mybir.AxisListType.X),
                v.tensor_scalar(e32t[:], ld32[:], ld32[0:32, 127:128], None,
                                ALU.subtract).then_inc(vM.h, 1)))
            v1 = vM.inc()
            em('s', lambda s, n=v1: (
                s.wait_ge(vM.h, n),
                s.activation(e32t[:], e32t[:], AF.Exp, scale=-1.0),
                s.activation(lamCol[:], ld32[0:32, 127:128], AF.Exp).then_inc(sM.h, 1)))
            s1 = sM.inc()
            em('v', lambda v, n=s1, c=c: (
                v.wait_ge(sM.h, n),
                v.tensor_tensor(w32pre[:], e32t[:],
                                dtt[0:32, c * 128:(c + 1) * 128],
                                ALU.mult).then_inc(vM.h, 1)))
            v2 = vM.inc()
            em('t', lambda t, n=v2, pf=pb4_free, c=c: (
                t.wait_ge(vM.h, n),
                t.wait_ge(vM.h, pf) if pf else None,
                t.transpose(PB[4].ap()[:, 0:32], dtt[0:32, c * 128:(c + 1) * 128],
                            idf[0:32, 0:32]),
                t.transpose(PB[4].ap()[:, 32:64], ld32[:], idf[0:32, 0:32]),
                t.transpose(PB[4].ap()[:, 64:96], w32pre[:], idf[0:32, 0:32]),
                t.transpose(PB[4].ap()[0:1, 96:128], lamCol[:],
                            idf[0:32, 0:32]).then_inc(tM.h, 1)))
            t1 = tM.inc()
            em('v', lambda v, n=t1, c=c: (
                v.wait_ge(tM.h, n),
                v.tensor_copy(dtT_all[:, c * 32:(c + 1) * 32], PB[4].ap()[:, 0:32]),
                v.tensor_copy(ldT_all[:, c * 32:(c + 1) * 32], PB[4].ap()[:, 32:64]),
                v.tensor_copy(W32[:], PB[4].ap()[:, 64:96]),
                v.tensor_copy(lamRowS[:], PB[4].ap()[0:1, 96:128]),
                v.tensor_scalar(ldTneg_all[:, c * 32:(c + 1) * 32],
                                ldT_all[:, c * 32:(c + 1) * 32], -1.0, None,
                                ALU.mult).then_inc(vM.h, 1)))
            pb4_free = v3 = vM.inc()
            em('t', lambda t, n=v3, pf=pb5_free: (
                t.wait_ge(vM.h, n),
                t.wait_ge(sM.h, pf) if pf else None,
                t.matmul(PB[5].ap()[:, 0:32], ones1f[:], lamRowS[:],
                         start=True, stop=True).then_inc(tM.h, 1)))
            t2 = tM.inc()
            em('s', lambda s, n=t2: (
                s.wait_ge(tM.h, n),
                s.activation(lamb[:], PB[5].ap()[:, 0:32], AF.Copy).then_inc(sM.h, 1)))
            pb5_free = s2 = sM.inc()
            em('y', lambda y, n=v1, c=c: (
                y.wait_ge(vM.h, n),
                y.dma_start(out=ldb_dram[32 * c:32 * (c + 1), :],
                            in_=ld32[:]).then_inc(dS.h, 16)))
            ld32_free = dS.inc(16)
            ldb_val.append(ld32_free)
            em('s', lambda s, n=e_done, c=c: (
                s.wait_ge(vM.h, n),
                s.activation(snaps[:, c * 2048:(c + 1) * 2048], hfull[:],
                             AF.Copy).then_inc(sM.h, 1)))
            s3 = sM.inc()
            for rd in range(2):
                bb = 6 + rd
                em('t', lambda t, pf=pbb_free[rd]: (
                    t.wait_ge(vM.h, pf) if pf else None))
                for i in range(8):
                    j = rd * 8 + i
                    em('t', lambda t, c=c, j=j, i=i, bb=bb: (
                        t.transpose(PBb[bb][:, i * 128:(i + 1) * 128],
                                    xc[:, j * TC + c * 128:j * TC + (c + 1) * 128],
                                    idb[:]).then_inc(tM.h, 1) if i == 7 else
                        t.transpose(PBb[bb][:, i * 128:(i + 1) * 128],
                                    xc[:, j * TC + c * 128:j * TC + (c + 1) * 128],
                                    idb[:])))
                ntp = tM.inc()
                em('v', lambda v, n=ntp, c=c, rd=rd, bb=bb: (
                    v.wait_ge(tM.h, n),
                    v.tensor_copy(xcT[:, (c * 16 + rd * 8) * 128:(c * 16 + rd * 8 + 8) * 128],
                                  PBb[bb][:, 0:1024]).then_inc(vM.h, 1)))
                pbb_free[rd] = vM.inc()
            for rd in range(4):
                bb = 6 + rd % 2
                em('t', lambda t, pf=pbb_free[rd % 2], nbc=nbc: (
                    t.wait_ge(dS.h, nbc),
                    t.wait_ge(vM.h, pf) if pf else None))
                for i in range(8):
                    h = rd * 8 + i
                    em('t', lambda t, hb=hb, h=h, i=i, bb=bb: (
                        t.transpose(PBb[bb][:, i * 128:(i + 1) * 128],
                                    bcbuf[:, hb * 4096 + h * 128:hb * 4096 + (h + 1) * 128],
                                    idb[:]).then_inc(tM.h, 1) if i == 7 else
                        t.transpose(PBb[bb][:, i * 128:(i + 1) * 128],
                                    bcbuf[:, hb * 4096 + h * 128:hb * 4096 + (h + 1) * 128],
                                    idb[:])))
                ntp = tM.inc()
                em('v', lambda v, n=ntp, rd=rd, bb=bb: (
                    v.wait_ge(tM.h, n),
                    v.tensor_copy(bcTall[:, rd * 1024:(rd + 1) * 1024],
                                  PBb[bb][:, 0:1024]).then_inc(vM.h, 1)))
                pbb_free[rd % 2] = vM.inc()
            bcT_done = vM.n
            bcb_free[hb] = tM.n
            for g in range(4):
                bk = g % 4
                for i in range(8):
                    h = g * 8 + i
                    dh = h % 2
                    em('v', lambda v, c=c, h=h, dh=dh, df=dtxw_free[dh]: (
                        v.wait_ge(tM.h, df) if df else None,
                        v.tensor_scalar(dtxw[:, dh * 64:(dh + 1) * 64],
                                        xcT[:, (c * 16 + h // 2) * 128 + (h % 2) * 64:
                                            (c * 16 + h // 2) * 128 + (h % 2) * 64 + 64],
                                        W32[:, h:h + 1], None, ALU.mult).then_inc(vM.h, 1)))
                    nv = vM.inc()
                    em('t', lambda t, n=nv, h=h, dh=dh, bk=bk, i=i, dhf=dh_free[bk]: (
                        t.wait_ge(vM.h, n),
                        t.wait_ge(vM.h, dhf) if (i == 0 and dhf) else None,
                        t.matmul(PB[bk].ap()[:, (h % 8) * 64:(h % 8 + 1) * 64],
                                 bcTall[:, h * 128:(h + 1) * 128],
                                 dtxw[:, dh * 64:(dh + 1) * 64],
                                 start=True, stop=True).then_inc(tM.h, 1)))
                    dtxw_free[dh] = tM.inc()
                tdh = tM.n
                em('v', lambda v, n=tdh, s3=s3, g=g, bk=bk: (
                    v.wait_ge(tM.h, n),
                    v.wait_ge(sM.h, s3),
                    *[v.scalar_tensor_tensor(
                        hfull[:, (g * 8 + i2) * 64:(g * 8 + i2 + 1) * 64],
                        hfull[:, (g * 8 + i2) * 64:(g * 8 + i2 + 1) * 64],
                        lamb[:, g * 8 + i2:g * 8 + i2 + 1],
                        PB[bk].ap()[:, i2 * 64:(i2 + 1) * 64],
                        ALU.mult, ALU.add) for i2 in range(7)],
                    v.scalar_tensor_tensor(
                        hfull[:, (g * 8 + 7) * 64:(g * 8 + 8) * 64],
                        hfull[:, (g * 8 + 7) * 64:(g * 8 + 8) * 64],
                        lamb[:, g * 8 + 7:g * 8 + 8],
                        PB[bk].ap()[:, 7 * 64:8 * 64],
                        ALU.mult, ALU.add).then_inc(vM.h, 1)))
                dh_free[bk] = vM.inc()
            e_done = vM.n

        if phases < 3:
            _finish_dev(nc, em, EM, taps, tap_outs, locals())
            p2a.close()
            pxc.close()
            pk2.close()
            pxn.close()
            _run_blocks(st, nc, EM)
            return nc, tap_outs

        p2a.close()
        pxc.close()
        p2c = ExitStack()
        P2C = lambda n, s, d: p2c.enter_context(nc.sbuf_tensor(n, s, d))
        triuS = P2C("triuS", [128, 128], BF16)
        boS = P2C("boS", [128, 512], BF16)
        selS = P2C("selS", [32, 2048], BF16)
        mktS = P2C("mktS", [8, 8], F32)
        cmaskS = P2C("cmaskS", [1, 256], F32)
        sdtc = P2C("sdtc", [32, 1], F32)
        PS8 = P2C("PS8", [32, 8], F32)
        pse = P2C("pse", [32, 8], F32)
        sdtg = P2C("sdtg", [8, 32], F32)
        D8 = P2C("D8", [8, 32], F32)
        eflat = P2C("eflat", [1, 256], F32)
        pcf = P2C("pcf", [1, 256], F32)
        EbS = P2C("EbS", [128, 256], F32)
        PbS = P2C("PbS", [128, 256], F32)
        hin = P2C("hin", [128, 2048], F32)
        gHb = P2C("gHb", [128, 2 * 2048], BF16)
        ldflat = P2C("ldflat", [1, 4096], F32)
        ccbuf = P2C("ccbuf", [128, 2 * 4096], BF16)
        bcb2 = P2C("bcb2", [128, 4096], BF16)
        zbuf = P2C("zbuf", [128, 2048], BF16)
        ybf = P2C("ybf", [128, 2048], BF16)
        ysq = P2C("ysq", [128, 2048], BF16)
        yg = P2C("yg", [128, 2048], BF16)
        mtb = P2C("mtb", [128, 2 * 128], BF16)
        ebcb = P2C("ebcb", [128, 2 * 128], BF16)
        s2b = P2C("s2b", [128, 2 * 128], BF16)
        dtxb = P2C("dtxb", [128, 2 * 64], BF16)
        ccsb = P2C("ccsb", [128, 2 * 128], BF16)
        rstdf = P2C("rstdf", [32, 128], F32)
        rstdf2 = P2C("rstdf2", [32, 128], F32)
        arow = P2C("arow", [1, 32], F32)
        rstdb = P2C("rstdb", [32, 128], BF16)

        t2a, v2a, s2a, d2a = tM.n, vM.n, sM.n, dS.n
        first = True
        for dst, srcn in [(triuS, "TRIU"), (boS, "BO"), (selS, "SEL"),
                          (mktS, "MkT"), (cmaskS, "coremask")]:
            em('g', lambda g, d=dst, s=din[srcn], f=first: (
                (g.wait_ge(tM.h, t2a), g.wait_ge(vM.h, v2a),
                 g.wait_ge(sM.h, s2a), g.wait_ge(dS.h, d2a)) if f else None,
                g.dma_start(out=d[:], in_=s).then_inc(dW.h, 16)))
            dW.inc(16)
            first = False
        c2_consts = dW.n

        em('v', lambda v: (
            v.reduce_sum(sdtc[:], sdtT[0:32, 0:8], axis=mybir.AxisListType.X),
            v.memset(PS8[0:32, 0:1], 0.0),
            *[v.tensor_add(PS8[0:32, cc2:cc2 + 1], PS8[0:32, cc2 - 1:cc2],
                           sdtT[0:32, cc2 - 1:cc2]) for cc2 in range(1, 8)],
            v.tensor_scalar(PS8[:], PS8[:], Acol[:], None, ALU.mult).then_inc(vM.h, 1)))
        v2b = vM.inc()
        em('s', lambda s, n=v2b: (
            s.wait_ge(vM.h, n),
            s.activation(pse[:], PS8[:], AF.Exp).then_inc(sM.h, 1)))
        s2b_ = sM.inc()
        em('y', lambda y, n=s2b_: (
            y.wait_ge(sM.h, n),
            y.dma_start(out=ag_in_s, in_=sdtc[:]).then_inc(dS.h, 16),
            y.dma_start(out=pc_dram, in_=pse[:]).then_inc(dS.h, 16)))
        dS.inc(32)
        ag_s_d = dS.n
        em('g', lambda g, n=e_done: (
            g.wait_ge(vM.h, n),
            g.dma_start(out=ag_in_h, in_=hfull[:]).then_inc(dW.h, 16)))
        ag_h_d = dW.inc(16)
        em('g', lambda g, n1=ag_s_d, n2=ag_h_d: (
            g.wait_ge(dS.h, n1),
            g.wait_ge(dW.h, n2),
            g.collective_compute(
                "AllGather", ALU.bypass,
                replica_groups=[list(range(NCORES))],
                ins=[ag_in_h], outs=[ag_out_h]).then_inc(cS.h),
            g.collective_compute(
                "AllGather", ALU.bypass,
                replica_groups=[list(range(NCORES))],
                ins=[ag_in_s], outs=[ag_out_s]).then_inc(cS.h)))
        cS.inc(2)
        em('y', lambda y: (
            y.wait_ge(cS.h, 2),
            y.dma_start(out=sdtg[:],
                        in_=ag_out_s.rearrange("(r h) c -> r (h c)", h=32)
                        ).then_inc(dS.h, 16),
            y.dma_start(out=pcf[:],
                        in_=pc_dram.rearrange("a b -> (a b)").unsqueeze(0)
                        ).then_inc(dS.h, 16)))
        dS.inc(32)
        sdtg_d = dS.n
        em('t', lambda t, n=sdtg_d: (
            t.wait_ge(dS.h, n),
            t.matmul(PB[6].ap()[0:8, 0:32], mktS[:], sdtg[:],
                     start=True, stop=True).then_inc(tM.h, 1)))
        tD = tM.inc()
        em('s', lambda s, n=tD, nc2=c2_consts: (
            s.wait_ge(tM.h, n),
            s.wait_ge(dW.h, nc2),
            s.activation(D8[:], PB[6].ap()[0:8, 0:32], AF.Copy).then_inc(sM.h, 1)))
        sD8 = sM.inc()
        em('s', lambda s, n=sD8: (
            s.wait_ge(sM.h, n),
            s.dma_start(out=db_dram, in_=D8[:]).then_inc(dC.h, 16)))
        dD8 = dC.inc(16)
        em('y', lambda y, n=dD8: (
            y.wait_ge(dC.h, n),
            y.dma_start(out=eflat[:],
                        in_=db_dram.rearrange("a b -> (a b)").unsqueeze(0)
                        ).then_inc(dS.h, 16)))
        eflat_d = dS.inc(16)
        em('t', lambda t: t.transpose(PB[5].ap()[0:1, 0:32], Acol[:],
                                      idf[0:32, 0:32]).then_inc(tM.h, 1))
        tAr = tM.inc()
        em('v', lambda v, n=tAr, n2=eflat_d: (
            v.wait_ge(tM.h, n),
            v.wait_ge(dS.h, n2),
            v.tensor_copy(arow[:], PB[5].ap()[0:1, 0:32]),
            *[v.tensor_mul(eflat[0:1, 32 * jr:32 * (jr + 1)],
                           eflat[0:1, 32 * jr:32 * (jr + 1)], arow[:])
              for jr in range(7)],
            v.tensor_mul(eflat[0:1, 224:256], eflat[0:1, 224:256],
                         arow[:]).then_inc(vM.h, 1)))
        vAe = vM.inc()
        em('s', lambda s, n=vAe: (
            s.wait_ge(vM.h, n),
            s.activation(eflat[:], eflat[:], AF.Exp).then_inc(sM.h, 1)))
        sEf = sM.inc()
        em('v', lambda v, n=sEf: (
            v.wait_ge(sM.h, n),
            v.tensor_mul(eflat[:], eflat[:], cmaskS[:]).then_inc(vM.h, 1)))
        vEf = vM.inc()
        em('t', lambda t, n=vEf, n2=sdtg_d: (
            t.wait_ge(vM.h, n),
            t.wait_ge(dS.h, n2),
            t.matmul(PB[6].ap()[:, 0:256], ones1f[:], eflat[:], start=True, stop=True),
            t.matmul(PB[7].ap()[:, 0:256], ones1f[:], pcf[:],
                     start=True, stop=True).then_inc(tM.h, 1)))
        tEb = tM.inc()
        em('v', lambda v, n=tEb: (
            v.wait_ge(tM.h, n),
            v.tensor_copy(EbS[:], PB[6].ap()[:, 0:256]),
            v.tensor_copy(PbS[:], PB[7].ap()[:, 0:256]),
            v.memset(hin[:], 0.0).then_inc(vM.h, 1)))
        vEb = vM.inc()
        ghb_free = [vEb, vEb]
        for r in range(NCORES):
            sl = r % 2
            em('g', lambda g, r=r, sl=sl, gf=ghb_free[sl]: (
                g.wait_ge(vM.h, gf),
                g.dma_start(out=gHb[:, sl * 2048:(sl + 1) * 2048],
                            in_=ag_out_h[128 * r:128 * (r + 1), :]
                            ).then_inc(dW.h, 16)))
            ngh = dW.inc(16)
            em('v', lambda v, n=ngh, r=r, sl=sl: (
                v.wait_ge(dW.h, n),
                *[v.scalar_tensor_tensor(
                    hin[:, h2 * 64:(h2 + 1) * 64],
                    gHb[:, sl * 2048 + h2 * 64:sl * 2048 + (h2 + 1) * 64],
                    EbS[:, r * 32 + h2:r * 32 + h2 + 1],
                    hin[:, h2 * 64:(h2 + 1) * 64], ALU.mult, ALU.add)
                    for h2 in range(31)],
                v.scalar_tensor_tensor(
                    hin[:, 31 * 64:32 * 64],
                    gHb[:, sl * 2048 + 31 * 64:sl * 2048 + 32 * 64],
                    EbS[:, r * 32 + 31:r * 32 + 32],
                    hin[:, 31 * 64:32 * 64], ALU.mult, ALU.add).then_inc(vM.h, 1)))
            ghb_free[sl] = vM.inc()
        for c in range(NCH):
            em('v', lambda v, c=c: (
                *[v.scalar_tensor_tensor(
                    snaps[:, c * 2048 + h2 * 64:c * 2048 + (h2 + 1) * 64],
                    hin[:, h2 * 64:(h2 + 1) * 64],
                    PbS[:, h2 * 8 + c:h2 * 8 + c + 1],
                    snaps[:, c * 2048 + h2 * 64:c * 2048 + (h2 + 1) * 64],
                    ALU.mult, ALU.add) for h2 in range(31)],
                v.scalar_tensor_tensor(
                    snaps[:, c * 2048 + 31 * 64:c * 2048 + 32 * 64],
                    hin[:, 31 * 64:32 * 64],
                    PbS[:, 31 * 8 + c:31 * 8 + c + 1],
                    snaps[:, c * 2048 + 31 * 64:c * 2048 + 32 * 64],
                    ALU.mult, ALU.add).then_inc(vM.h, 1)))
            vM.inc()
        corr_done = vM.n

        if phases < 4:
            _finish_dev(nc, em, EM, taps, tap_outs, locals())
            p2c.close()
            pk2.close()
            pxn.close()
            _run_blocks(st, nc, EM)
            return nc, tap_outs

        chunk_done = [{'t': 0, 'v': 0, 's': 0}, {'t': 0, 'v': 0, 's': 0}]
        bcb2_free = 0
        zbuf_free = 0
        ldflat_free = 0
        mp_free = [(None, 0), (None, 0)]
        sa_free = [0, 0]
        y_bank_free = [0, 0, 0, 0]
        rot_free = [0, 0]
        yg_spill = 0
        prev_trms = prev_ygv = 0

        for c in range(NCH):
            hb = c % 2
            cdp = dict(chunk_done[hb])
            em('y', lambda y, c=c, hb=hb, cdp=cdp, cd=corr_done, bf=bcb2_free,
                   zf=zbuf_free, lf=ldflat_free: (
                y.wait_ge(vM.h, cd) if c == 0 else None,
                y.wait_ge(tM.h, cdp['t']) if cdp['t'] else None,
                y.wait_ge(vM.h, cdp['v']) if cdp['v'] else None,
                y.wait_ge(sM.h, cdp['s']) if cdp['s'] else None,
                y.dma_start(out=ccbuf[:, hb * 4096:(hb + 1) * 4096]
                            .rearrange("n (h t) -> n h t", h=32),
                            in_=cc_dram.rearrange("(h n) t -> n h t", n=128)
                            [:, :, c * 128:(c + 1) * 128]).then_inc(dS.h, 16),
                y.wait_ge(tM.h, bf) if bf else None,
                y.dma_start(out=bcb2[:, 0:4096]
                            .rearrange("n (h t) -> n h t", h=32),
                            in_=bc_dram.rearrange("(h n) t -> n h t", n=128)
                            [:, :, c * 128:(c + 1) * 128]).then_inc(dS.h, 16),
                y.wait_ge(vM.h, zf) if zf else None,
                y.dma_start(out=zbuf[:, 0:2048]
                            .rearrange("p (j t) -> p j t", j=16),
                            in_=z_dram.rearrange("(j p) t -> p j t", p=128)
                            [:, :, c * 128:(c + 1) * 128]).then_inc(dS.h, 16),
                y.wait_ge(tM.h, lf) if lf else None,
                y.dma_start(out=ldflat[:],
                            in_=ldb_dram[32 * c:32 * (c + 1), :]
                            .rearrange("a b -> (a b)").unsqueeze(0)
                            ).then_inc(dS.h, 16)))
            dS.inc(64)
            nld = dS.n

            for h in range(32):
                if DEBUG_2C_LVL < 2:
                    break
                rot = h % 2
                bk = h // 8
                pj = h // 2
                em('t', lambda t, h=h, rot=rot, n=nld, mf=mp_free[rot]: (
                    t.wait_ge(dS.h, n) if h == 0 else None,
                    t.wait_ge({'s': sM, 'v': vM}[mf[0]].h, mf[1]) if mf[0] else None,
                    t.matmul(PB[6 + rot].ap()[:, 0:128], ones1f[:],
                             ldflat[0:1, h * 128:(h + 1) * 128],
                             start=True, stop=True).then_inc(tM.h, 1)))
                nmp = tM.inc()
                em('s', lambda s, n=nmp, c=c, h=h, rot=rot: (
                    s.wait_ge(tM.h, n),
                    (s.activation(mtb[:, rot * 128:(rot + 1) * 128],
                                  PB[6 + rot].ap()[:, 0:128], AF.Exp,
                                  bias=ldTneg_all[:, c * 32 + h:c * 32 + h + 1])
                     if DEBUG_2C_LVL >= 2.5 else
                     s.activation(mtb[:, rot * 128:(rot + 1) * 128],
                                  PB[6 + rot].ap()[:, 0:128], AF.Copy)),
                    (s.activation(ebcb[:, rot * 128:(rot + 1) * 128],
                                  PB[6 + rot].ap()[:, 0:128],
                                  AF.Exp) if DEBUG_2C_LVL >= 2.7 else
                     s.activation(ebcb[:, rot * 128:(rot + 1) * 128],
                                  PB[6 + rot].ap()[:, 0:128],
                                  AF.Copy)).then_inc(sM.h, 1)))
                mp_free[rot] = ('s', sM.inc())
                em('v', lambda v, n=mp_free[rot][1], rot=rot: (
                    v.wait_ge(sM.h, n),
                    v.tensor_mul(mtb[:, rot * 128:(rot + 1) * 128],
                                 mtb[:, rot * 128:(rot + 1) * 128],
                                 triuS[:]).then_inc(vM.h, 1)))
                nmt2 = vM.inc()
                if DEBUG_2C_LVL < 3:
                    continue
                em('t', lambda t, h=h, hb=hb, rot=rot, af=sa_free[rot], n=nld: (
                    t.wait_ge(dS.h, n) if h == 0 else None,
                    (t.wait_ge({'s': sM, 'v': vM}[af[0]].h, af[1]) if isinstance(af, tuple)
                     else t.wait_ge(vM.h, af)) if af else None,
                    t.matmul(PB[4 + rot].ap()[:, 0:128],
                             bcb2[:, h * 128:(h + 1) * 128],
                             ccbuf[:, hb * 4096 + h * 128:hb * 4096 + (h + 1) * 128],
                             start=True, stop=True).then_inc(tM.h, 1)))
                nsa = tM.inc()
                em('v', lambda v, n=nsa, n2=nmt2, rot=rot, c=c, h=h, hb=hb: (
                    v.wait_ge(tM.h, n),
                    v.tensor_tensor(s2b[:, rot * 128:(rot + 1) * 128],
                                    PB[4 + rot].ap()[:, 0:128],
                                    mtb[:, rot * 128:(rot + 1) * 128], ALU.mult),
                    v.tensor_scalar(dtxb[:, rot * 64:(rot + 1) * 64],
                                    xcT[:, (c * 16 + h // 2) * 128 + (h % 2) * 64:
                                        (c * 16 + h // 2) * 128 + (h % 2) * 64 + 64],
                                    dtT_all[:, c * 32 + h:c * 32 + h + 1],
                                    None, ALU.mult),
                    v.tensor_tensor(ccsb[:, rot * 128:(rot + 1) * 128],
                                    ccbuf[:, hb * 4096 + h * 128:hb * 4096 + (h + 1) * 128],
                                    ebcb[:, rot * 128:(rot + 1) * 128],
                                    ALU.mult).then_inc(vM.h, 1)))
                sa_free[rot] = vM.inc()
                if DEBUG_2C_LVL < 4:
                    continue
                em('t', lambda t, n=sa_free[rot], h=h, c=c, rot=rot, bk=bk, pj=pj,
                       yf=y_bank_free[bk]: (
                    t.wait_ge(vM.h, n),
                    t.wait_ge(sM.h, yf) if (h % 8 == 0 and yf) else None,
                    t.matmul(PB[bk].ap()[64 * (h % 2):64 * (h % 2) + 64,
                                         (pj % 4) * 128:(pj % 4 + 1) * 128],
                             dtxb[:, rot * 64:(rot + 1) * 64],
                             s2b[:, rot * 128:(rot + 1) * 128],
                             start=True, stop=False),
                    t.matmul(PB[bk].ap()[64 * (h % 2):64 * (h % 2) + 64,
                                         (pj % 4) * 128:(pj % 4 + 1) * 128],
                             snaps[:, c * 2048 + h * 64:c * 2048 + (h + 1) * 64],
                             ccsb[:, rot * 128:(rot + 1) * 128],
                             start=False, stop=True).then_inc(tM.h, 1)))
                tM.inc()
                if h % 8 == 7:
                    tyb = tM.n
                    em('s', lambda s, n=tyb, bk=bk, pt=prev_trms, pv=prev_ygv: (
                        s.wait_ge(tM.h, n),
                        s.wait_ge(tM.h, pt) if (bk == 0 and pt) else None,
                        s.wait_ge(vM.h, pv) if (bk == 0 and pv) else None,
                        s.activation(ybf[:, bk * 512:(bk + 1) * 512],
                                     PB[bk].ap()[:, 0:512], AF.Copy),
                        s.activation(ysq[:, bk * 512:(bk + 1) * 512],
                                     PB[bk].ap()[:, 0:512],
                                     AF.Square).then_inc(sM.h, 1)))
                    y_bank_free[bk] = sM.inc()
            ldflat_free = tM.n
            yev = sM.n
            if DEBUG_NO_EPI or DEBUG_2C_LVL < 9:
                prev_trms, prev_ygv = tM.n, vM.n
                chunk_done[hb] = {'t': tM.n, 'v': vM.n, 's': sM.n}
                bcb2_free = tM.n
                zbuf_free = 0
                continue
            em('t', lambda t, n=yev, af0=sa_free[0], af1=sa_free[1],
                   mf0=mp_free[0], mf1=mp_free[1]: (
                t.wait_ge(sM.h, n),
                ((t.wait_ge({'s': sM, 'v': vM}[af0[0]].h, af0[1]) if isinstance(af0, tuple)
                  else t.wait_ge(vM.h, af0)) if af0 else None),
                ((t.wait_ge({'s': sM, 'v': vM}[af1[0]].h, af1[1]) if isinstance(af1, tuple)
                  else t.wait_ge(vM.h, af1)) if af1 else None),
                t.wait_ge({'s': sM, 'v': vM}[mf0[0]].h, mf0[1]) if mf0[0] else None,
                t.wait_ge({'s': sM, 'v': vM}[mf1[0]].h, mf1[1]) if mf1[0] else None))
            for j in range(16):
                em('t', lambda t, j=j, bk=4 + j // 4, c0=(j % 4) * 128: (
                    t.matmul(PB[bk].ap()[0:32, c0:c0 + 128], boS[:, j * 32:(j + 1) * 32],
                             ysq[:, j * 128:(j + 1) * 128],
                             start=True, stop=True).then_inc(tM.h, 1)
                    if j == 15 else
                    t.matmul(PB[bk].ap()[0:32, c0:c0 + 128], boS[:, j * 32:(j + 1) * 32],
                             ysq[:, j * 128:(j + 1) * 128],
                             start=True, stop=True)))
            trms = tM.inc()
            em('v', lambda v, n=trms: (
                v.wait_ge(tM.h, n),
                v.tensor_copy(rstdf[:], PB[4].ap()[0:32, 0:128]),
                *[v.tensor_add(rstdf[:], rstdf[:],
                               PB[4 + jj // 4].ap()[0:32, (jj % 4) * 128:(jj % 4) * 128 + 128])
                  for jj in range(1, 15)],
                v.tensor_add(rstdf[:], rstdf[:],
                             PB[7].ap()[0:32, 3 * 128:4 * 128]).then_inc(vM.h, 1)))
            vsum = vM.inc()
            em('s', lambda s, n=vsum: (
                s.wait_ge(vM.h, n),
                s.activation(rstdf2[:], rstdf[:], AF.Ln,
                             bias=epsrms[:], scale=1.0 / 64.0),
                s.activation(rstdb[:], rstdf2[:], AF.Exp,
                             scale=-0.5).then_inc(sM.h, 1)))
            vrms = srms = sM.inc()
            sa_free[0] = ('v', vsum)
            sa_free[1] = ('v', vsum)
            em('s', lambda s, n=nld, hb=hb: (
                s.wait_ge(dS.h, n),
                s.activation(zbuf[:, 0:2048], zbuf[:, 0:2048],
                             AF.Silu).then_inc(sM.h, 1)))
            zsil = sM.inc()
            for j in range(16):
                rot = j % 2
                em('t', lambda t, j=j, rot=rot, n=vrms, rf=rot_free[rot],
                       mf0=mp_free[0], mf1=mp_free[1]: (
                    t.wait_ge(sM.h, n) if j == 0 else None,
                    t.wait_ge({'s': sM, 'v': vM}[mf0[0]].h, mf0[1])
                    if (j == 0 and mf0[0]) else None,
                    t.wait_ge({'s': sM, 'v': vM}[mf1[0]].h, mf1[1])
                    if (j == 0 and mf1[0]) else None,
                    t.wait_ge(vM.h, rf) if rf else None,
                    t.matmul(PB[6 + rot].ap()[:, 0:128], selS[:, j * 128:(j + 1) * 128],
                             rstdb[:], start=True, stop=True).then_inc(tM.h, 1)))
                nrb = tM.inc()
                em('v', lambda v, n=nrb, j=j, rot=rot, zs=zsil, hb=hb, ys=yg_spill: (
                    v.wait_ge(tM.h, n),
                    v.wait_ge(sM.h, zs) if j == 0 else None,
                    v.wait_ge(dC.h, ys) if (j == 0 and ys) else None,
                    v.tensor_tensor(yg[:, j * 128:(j + 1) * 128],
                                    ybf[:, j * 128:(j + 1) * 128],
                                    PB[6 + rot].ap()[:, 0:128], ALU.mult),
                    v.tensor_tensor(yg[:, j * 128:(j + 1) * 128],
                                    yg[:, j * 128:(j + 1) * 128],
                                    zbuf[:, j * 128:(j + 1) * 128],
                                    ALU.mult).then_inc(vM.h, 1)))
                rot_free[rot] = vM.inc()
            mp_free = [('v', rot_free[0]), ('v', rot_free[1])]
            ygv = vM.n
            em('s', lambda s, n=ygv, c=c, ys=yg_spill: (
                s.wait_ge(vM.h, n),
                s.wait_ge(dC.h, ys) if ys else None,
                s.dma_start(out=y_dram.rearrange("(j p) t -> p j t", p=128)
                            [:, :, c * 128:(c + 1) * 128],
                            in_=yg[:].rearrange("p (j t) -> p j t", j=16)
                            ).then_inc(dC.h, 16)))
            yg_spill = dC.inc(16)
            prev_trms, prev_ygv = trms, ygv
            bcb2_free = tM.n
            zbuf_free = ygv
            chunk_done[hb] = {'t': tM.n, 'v': vM.n, 's': sM.n}

        if phases < 5:
            _finish_dev(nc, em, EM, taps, tap_outs, locals())
            p2c.close()
            pk2.close()
            pxn.close()
            _run_blocks(st, nc, EM)
            return nc, tap_outs

        p2c.close()
        pk2.close()
        p2d = ExitStack()
        wb2 = p2d.enter_context(nc.sbuf_tensor("wb2", [128, 2 * 4096], BF16))
        yb2 = p2d.enter_context(nc.sbuf_tensor("yb2", [128, 16 * 1024], BF16))
        sob = p2d.enter_context(nc.sbuf_tensor("sob", [128, 2 * 1024], BF16))
        t2c, v2c, s2c, d2c = tM.n, vM.n, sM.n, dS.n
        em('y', lambda y, n=yg_spill: (
            y.wait_ge(dC.h, n), y.wait_ge(tM.h, t2c), y.wait_ge(vM.h, v2c),
            y.wait_ge(sM.h, s2c), y.wait_ge(dS.h, d2c)))
        em('g', lambda g, n=yg_spill: (
            g.wait_ge(dC.h, n), g.wait_ge(tM.h, t2c), g.wait_ge(vM.h, v2c),
            g.wait_ge(sM.h, s2c), g.wait_ge(dS.h, d2c)))
        for k in range(16):
            em('y', lambda y, k=k: y.dma_start(
                out=yb2[:, k * 1024:(k + 1) * 1024],
                in_=y_dram.rearrange("(j p) t -> j p t", p=128)[k]).then_inc(dS.h, 16))
            dS.inc(16)
        yb2_d = dS.n
        wb2_free = [0, 0]
        sb_free = [0, 0]
        bank_free2 = [0, 0, 0, 0]
        sl2 = 0
        for g2 in range(8):
            half = g2 % 2
            cols0 = g2 * 256
            for kg in range(4):
                em('g', lambda g, kg=kg, g2=g2, half=half, wf=wb2_free[half]: (
                    g.wait_ge(tM.h, wf) if (kg == 0 and wf) else None,
                    g.dma_start(
                        out=wb2[:, half * 4096 + kg * 1024:half * 4096 + (kg + 1) * 1024],
                        in_=din["Wp2"][(g2 * 4 + kg) * 128:(g2 * 4 + kg + 1) * 128, :]
                        ).then_inc(dW.h, 16)))
                dW.inc(16)
            nload = dW.n
            for mj2 in range(2):
                m = g2 * 2 + mj2
                sh = m % 2
                for tf in range(2):
                    b = sl2 % 4
                    sl2 += 1
                    em('t', lambda t, n=nload, bf=bank_free2[b], ny=yb2_d: (
                        t.wait_ge(dW.h, n),
                        t.wait_ge(dS.h, ny),
                        t.wait_ge(sM.h, bf) if bf else None))
                    for k in range(16):
                        kg, k2 = divmod(k, 4)
                        base = half * 4096 + kg * 1024 + k2 * 256 + mj2 * 128
                        em('t', lambda t, k=k, base=base, b=b, tf=tf: (
                            t.matmul(PB[b].ap()[:, 0:512], wb2[:, base:base + 128],
                                     yb2[:, k * 1024 + tf * 512:k * 1024 + (tf + 1) * 512],
                                     start=(k == 0), stop=(k == 15)).then_inc(tM.h, 1)
                            if k == 15 else
                            t.matmul(PB[b].ap()[:, 0:512], wb2[:, base:base + 128],
                                     yb2[:, k * 1024 + tf * 512:k * 1024 + (tf + 1) * 512],
                                     start=(k == 0), stop=(k == 15))))
                    ntm = tM.inc()
                    em('s', lambda s, n=ntm, b=b, sh=sh, tf=tf, sf=sb_free[sh]: (
                        s.wait_ge(tM.h, n),
                        s.wait_ge(dC.h, sf) if (tf == 0 and sf) else None,
                        s.activation(sob[:, sh * 1024 + tf * 512:sh * 1024 + (tf + 1) * 512],
                                     PB[b].ap()[:, 0:512], AF.Copy).then_inc(sM.h, 1)))
                    bank_free2[b] = sM.inc()
                em('s', lambda s, n=bank_free2[(sl2 - 1) % 4], m=m, sh=sh: (
                    s.wait_ge(sM.h, n),
                    s.dma_start(out=s_dram.rearrange("(j p) t -> j p t", p=128)[m],
                                in_=sob[:, sh * 1024:(sh + 1) * 1024]).then_inc(dC.h, 16)))
                sb_free[sh] = dC.inc(16)
            wb2_free[half] = tM.n
        s_done_d = dC.n

        p2d.close()
        p3a = ExitStack()
        tmp1f = p3a.enter_context(nc.sbuf_tensor("tmp1f", [128, 16 * TC], F32))
        btmp = p3a.enter_context(nc.sbuf_tensor("btmp", [128, TC], F32))
        t2d, v2d, s2d, d2d = tM.n, vM.n, sM.n, dS.n

        f1_bank_free = [0, 0, 0, 0]
        slc = 0
        em('t', lambda t: (t.wait_ge(sM.h, s2d), t.wait_ge(dC.h, s_done_d)))
        for j in range(DCH):
            for tf in range(2):
                b = slc % 4
                slc += 1
                em('t', lambda t, j=j, tf=tf, b=b, bf=f1_bank_free[b]: (
                    t.wait_ge(vM.h, bf) if bf else None,
                    t.matmul(PB[b].ap()[:, 0:512], h128su[:, j * 128:(j + 1) * 128],
                             xn[:, j * TH + 3 + tf * 512:j * TH + 3 + (tf + 1) * 512],
                             start=True, stop=True).then_inc(tM.h, 1)))
                nf = tM.inc()
                em('v', lambda v, n=nf, j=j, tf=tf, b=b, f=(j == 0 and tf == 0): (
                    v.wait_ge(tM.h, n),
                    v.wait_ge(sM.h, s2d) if f else None,
                    v.tensor_copy(tmp1f[:, j * TC + tf * 512:j * TC + (tf + 1) * 512],
                                  PB[b].ap()[:, 0:512]).then_inc(vM.h, 1)))
                f1_bank_free[b] = vM.inc()
        f1_done = vM.n

        if phases < 5.15:
            _finish_dev(nc, em, EM, taps, tap_outs, locals())
            p3a.close()
            pxn.close()
            _run_blocks(st, nc, EM)
            return nc, tap_outs

        for stg in range(DEBUG_BF_STAGES):
            stepj = 1 << stg
            last = (stg == 3)
            for a in range(DCH):
                if a & stepj:
                    continue
                bpt = a + stepj
                em('v', lambda v, a=a, bpt=bpt: (
                    v.tensor_sub(btmp[:], tmp1f[:, a * TC:(a + 1) * TC],
                                 tmp1f[:, bpt * TC:(bpt + 1) * TC]),
                    v.tensor_add(tmp1f[:, a * TC:(a + 1) * TC],
                                 tmp1f[:, a * TC:(a + 1) * TC],
                                 tmp1f[:, bpt * TC:(bpt + 1) * TC]),
                    v.tensor_copy(tmp1f[:, bpt * TC:(bpt + 1) * TC],
                                  btmp[:]).then_inc(vM.h, 1)))
                nbp = vM.inc()
                if last:
                    em('s', lambda s, a=a, bpt=bpt, n=nbp: (
                        s.wait_ge(vM.h, n),
                        s.activation(xn[:, a * TH:a * TH + TC],
                                     tmp1f[:, a * TC:(a + 1) * TC], AF.Copy),
                        s.activation(xn[:, bpt * TH:bpt * TH + TC],
                                     tmp1f[:, bpt * TC:(bpt + 1) * TC],
                                     AF.Copy).then_inc(sM.h, 1)))
                    sM.inc()
        bf_done = vM.n
        if DEBUG_BF_STAGES != 4:
            for j in range(DCH):
                em('s', lambda s, j=j, n=bf_done: (
                    s.wait_ge(vM.h, n) if j == 0 else None,
                    s.activation(xn[:, j * TH:j * TH + TC],
                                 tmp1f[:, j * TC:(j + 1) * TC],
                                 AF.Copy).then_inc(sM.h, 1)))
                sM.inc()
        fwht1_done = sM.n

        if phases < 5.3:
            _finish_dev(nc, em, EM, taps, tap_outs, locals())
            p3a.close()
            pxn.close()
            _run_blocks(st, nc, EM)
            return nc, tap_outs

        p3a.close()
        p3b = ExitStack()
        x2 = p3b.enter_context(nc.sbuf_tensor("x2", [128, ICH * TC], BF16))
        p3u = ExitStack()
        wbf = p3u.enter_context(nc.sbuf_tensor("wbf", [128, 2 * 4096], BF16))
        gts = p3u.enter_context(nc.sbuf_tensor("gts", [128, 2 * TC], BF16))
        ug_bank_free = [0, 0, 0, 0]
        wbf_free = [0, 0]
        gts_free = [0, 0]
        ug_slc = 0
        for m in range(ICH):
            half = m % 2
            em('g', lambda g, m=m, half=half, wf=wbf_free[half], f=(m == 0): (
                (g.wait_ge(sM.h, fwht1_done), g.wait_ge(dS.h, d2d),
                 g.wait_ge(vM.h, bf_done)) if f else None,
                g.wait_ge(tM.h, wf) if wf else None,
                g.dma_start(out=wbf[:, half * 4096:half * 4096 + 2048],
                            in_=din["WupP"][m * 128:(m + 1) * 128, :]).then_inc(dW.h, 16),
                g.dma_start(out=wbf[:, half * 4096 + 2048:half * 4096 + 4096],
                            in_=din["WgtP"][m * 128:(m + 1) * 128, :]).then_inc(dW.h, 16)))
            nload = dW.inc(32)
            if DEBUG_UG_LVL < 0.8:
                continue
            for tf in range(2):
                b0 = ug_slc % 4
                ug_slc += 1
                em('t', lambda t, n=nload, b0=b0, bf=ug_bank_free[b0], tf=tf, f1=fwht1_done: (
                    (t.wait_ge(dW.h, n), t.wait_ge(sM.h, f1)) if tf == 0 else None,
                    (t.wait_ge({'s': sM, 'v': vM}[bf[0]].h, bf[1]) if isinstance(bf, tuple)
                     else t.wait_ge(vM.h, bf)) if bf else None))
                for k in range(16):
                    em('t', lambda t, k=k, m=m, half=half, tf=tf, b0=b0: (
                        t.matmul(PB[b0].ap()[:, 0:512],
                                 wbf[:, half * 4096 + k * 128:half * 4096 + (k + 1) * 128],
                                 xn[:, k * TH + tf * 512:k * TH + (tf + 1) * 512],
                                 start=(k == 0), stop=(k == 15)).then_inc(tM.h, 1)
                        if k == 15 else
                        t.matmul(PB[b0].ap()[:, 0:512],
                                 wbf[:, half * 4096 + k * 128:half * 4096 + (k + 1) * 128],
                                 xn[:, k * TH + tf * 512:k * TH + (tf + 1) * 512],
                                 start=(k == 0), stop=(k == 15))))
                nup = tM.inc()
                if DEBUG_UG_LVL < 0.9:
                    continue
                em('v', lambda v, n=nup, m=m, tf=tf, b0=b0: (
                    v.wait_ge(tM.h, n),
                    (v.tensor_scalar(x2[:, m * TC + tf * 512:m * TC + (tf + 1) * 512],
                                     PB[b0].ap()[:, 0:512], upscale[:, m:m + 1],
                                     None, ALU.mult) if DEBUG_UG_LVL >= 2 else
                     v.tensor_copy(x2[:, m * TC + tf * 512:m * TC + (tf + 1) * 512],
                                   PB[b0].ap()[:, 0:512])).then_inc(vM.h, 1)))
                ug_bank_free[b0] = ('v', vM.inc())
            for tf in range(2 if DEBUG_UG_LVL >= 0.95 else 0):
                b1 = ug_slc % 4
                ug_slc += 1
                em('t', lambda t, b1=b1, bf=ug_bank_free[b1], tf=tf: (
                    (t.wait_ge({'s': sM, 'v': vM}[bf[0]].h, bf[1]) if isinstance(bf, tuple)
                     else t.wait_ge(vM.h, bf)) if bf else None))
                for k in range(16):
                    em('t', lambda t, k=k, m=m, half=half, tf=tf, b1=b1: (
                        t.matmul(PB[b1].ap()[:, 0:512],
                                 wbf[:, half * 4096 + 2048 + k * 128:
                                     half * 4096 + 2048 + (k + 1) * 128],
                                 xn[:, k * TH + tf * 512:k * TH + (tf + 1) * 512],
                                 start=(k == 0), stop=(k == 15)).then_inc(tM.h, 1)
                        if k == 15 else
                        t.matmul(PB[b1].ap()[:, 0:512],
                                 wbf[:, half * 4096 + 2048 + k * 128:
                                     half * 4096 + 2048 + (k + 1) * 128],
                                 xn[:, k * TH + tf * 512:k * TH + (tf + 1) * 512],
                                 start=(k == 0), stop=(k == 15))))
                ngt = tM.inc()
                em('s', lambda s, n=ngt, m=m, tf=tf, b1=b1, half=half,
                       gf=gts_free[half]: (
                    s.wait_ge(tM.h, n),
                    s.wait_ge(vM.h, gf) if (tf == 0 and gf) else None,
                    (s.activation(gts[:, half * TC + tf * 512:half * TC + (tf + 1) * 512],
                                  PB[b1].ap()[:, 0:512], AF.Silu,
                                  scale=gtscale[:, m:m + 1]) if DEBUG_UG_LVL >= 3 else
                     s.activation(gts[:, half * TC + tf * 512:half * TC + (tf + 1) * 512],
                                  PB[b1].ap()[:, 0:512],
                                  AF.Silu)).then_inc(sM.h, 1)))
                ug_bank_free[b1] = ('s', sM.inc())
            nsil = sM.n
            if DEBUG_UG_LVL >= 0.97:
                em('v', lambda v, n=nsil, m=m, half=half: (
                    v.wait_ge(sM.h, n),
                    v.tensor_tensor(x2[:, m * TC:(m + 1) * TC], x2[:, m * TC:(m + 1) * TC],
                                    gts[:, half * TC:(half + 1) * TC],
                                    ALU.mult).then_inc(vM.h, 1)))
                gts_free[half] = vM.inc()
            wbf_free[half] = tM.n
        ug_done_v = vM.n
        ug_done_s = sM.n
        ug_done_t = tM.n

        if phases < 5.5:
            _finish_dev(nc, em, EM, taps, tap_outs, locals())
            p3u.close()
            p3b.close()
            pxn.close()
            _run_blocks(st, nc, EM)
            return nc, tap_outs

        f2_bank_free = [0, 0, 0, 0]
        slc2 = 0
        em('t', lambda t, n=ug_done_v, n2=ug_done_s: (t.wait_ge(vM.h, n),
                                                      t.wait_ge(sM.h, n2)))
        for i2 in range(ICH):
            for tf in range(2):
                b = slc2 % 4
                slc2 += 1
                em('t', lambda t, i2=i2, tf=tf, b=b, bf=f2_bank_free[b]: (
                    t.wait_ge(sM.h, bf) if bf else None,
                    t.matmul(PB[b].ap()[:, 0:512], h128b[:],
                             x2[:, i2 * TC + tf * 512:i2 * TC + (tf + 1) * 512],
                             start=True, stop=True).then_inc(tM.h, 1)))
                nf = tM.inc()
                em('s', lambda s, n=nf, i2=i2, tf=tf, b=b: (
                    s.wait_ge(tM.h, n),
                    s.activation(x2[:, i2 * TC + tf * 512:i2 * TC + (tf + 1) * 512],
                                 PB[b].ap()[:, 0:512], AF.Copy).then_inc(sM.h, 1)))
                f2_bank_free[b] = sM.inc()
        f2_done = sM.n

        if phases < 5.7:
            _finish_dev(nc, em, EM, taps, tap_outs, locals())
            p3u.close()
            p3b.close()
            pxn.close()
            _run_blocks(st, nc, EM)
            return nc, tap_outs

        p3u.close()
        p3w = ExitStack()
        wdn = p3w.enter_context(nc.sbuf_tensor("wdn", [128, 2 * 8192], BF16))
        wdn_free = [0, 0]
        dn_bank_free = [0, 0, 0, 0]
        slc3 = 0
        for m in range(16):
            half = m % 2
            em('y', lambda y, m=m, half=half, wf=wdn_free[half], f=(m == 0),
                   tU=ug_done_t, vU=ug_done_v: (
                (y.wait_ge(sM.h, f2_done), y.wait_ge(dC.h, wtil_done_d),
                 y.wait_ge(tM.h, tU), y.wait_ge(vM.h, vU)) if f else None,
                y.wait_ge(tM.h, wf) if wf else None,
                *[y.dma_start(out=wdn[:, half * 8192 + kg * 2048:
                                      half * 8192 + (kg + 1) * 2048]
                              .rearrange("p (k c) -> p k c", k=16),
                              in_=wtil_dram[kg * 2048:(kg + 1) * 2048,
                                            m * 128:(m + 1) * 128]
                              .rearrange("(k p) c -> p k c", p=128)
                              ).then_inc(dS.h, 16) for kg in range(4)]))
            nload = dS.inc(64)
            for tf in range(2):
                b = slc3 % 4
                slc3 += 1
                em('t', lambda t, n=nload, b=b, bf=dn_bank_free[b], tf=tf,
                       f2=f2_done, f=(m == 0 and tf == 0): (
                    t.wait_ge(dS.h, n) if tf == 0 else None,
                    t.wait_ge(sM.h, f2) if f else None,
                    t.wait_ge(sM.h, bf) if bf else None))
                for k in range(64):
                    em('t', lambda t, k=k, half=half, tf=tf, b=b: (
                        t.matmul(PB[b].ap()[:, 0:512],
                                 wdn[:, half * 8192 + k * 128:half * 8192 + (k + 1) * 128],
                                 x2[:, k * TC + tf * 512:k * TC + (tf + 1) * 512],
                                 start=(k == 0), stop=(k == 63)).then_inc(tM.h, 1)
                        if k == 63 else
                        t.matmul(PB[b].ap()[:, 0:512],
                                 wdn[:, half * 8192 + k * 128:half * 8192 + (k + 1) * 128],
                                 x2[:, k * TC + tf * 512:k * TC + (tf + 1) * 512],
                                 start=(k == 0), stop=(k == 63))))
                ndn = tM.inc()
                em('s', lambda s, n=ndn, m=m, tf=tf, b=b: (
                    s.wait_ge(tM.h, n),
                    s.activation(xn[:, m * 1024 + tf * 512:m * 1024 + (tf + 1) * 512],
                                 PB[b].ap()[:, 0:512], AF.Silu,
                                 scale=dnscale[:, m:m + 1]).then_inc(sM.h, 1)))
                dn_bank_free[b] = sM.inc()
            wdn_free[half] = tM.n
        g_done = sM.n

        if phases < 5.9:
            _finish_dev(nc, em, EM, taps, tap_outs, locals())
            p3w.close()
            p3b.close()
            pxn.close()
            _run_blocks(st, nc, EM)
            return nc, tap_outs

        dn_t = tM.n
        p3w.close()
        p3b.close()
        p3f = ExitStack()
        xfin = p3f.enter_context(nc.sbuf_tensor("xfin", [128, 2 * TC], F32))
        sfin = p3f.enter_context(nc.sbuf_tensor("sfin", [128, 2 * TC], BF16))
        obuf = p3f.enter_context(nc.sbuf_tensor("obuf", [128, 2 * TC], F32))
        ofin_free = [0, 0]
        xf_free2 = [0, 0]
        for m in range(16):
            half = m % 2
            em('y', lambda y, m=m, half=half, xf=xf_free2[half], f=(m == 0): (
                (y.wait_ge(sM.h, g_done), y.wait_ge(tM.h, dn_t)) if f else None,
                y.wait_ge(vM.h, xf) if xf else None,
                y.dma_start(out=xfin[:, half * TC:(half + 1) * TC],
                            in_=din["xin"].rearrange("(c p) t -> c p t", p=128)
                            [m][:, 3:3 + TC]).then_inc(dS.h, 16),
                y.dma_start(out=sfin[:, half * TC:(half + 1) * TC],
                            in_=s_dram.rearrange("(c p) t -> c p t", p=128)[m]
                            ).then_inc(dS.h, 16)))
            nxl = dS.inc(32)
            em('v', lambda v, n=nxl, m=m, half=half, of=ofin_free[half]: (
                v.wait_ge(dS.h, n),
                v.wait_ge(dS.h, of) if of else None,
                v.tensor_tensor(obuf[:, half * TC:(half + 1) * TC],
                                sfin[:, half * TC:(half + 1) * TC],
                                xn[:, m * 1024:(m + 1) * 1024], ALU.mult),
                v.tensor_add(obuf[:, half * TC:(half + 1) * TC],
                             obuf[:, half * TC:(half + 1) * TC],
                             xfin[:, half * TC:(half + 1) * TC]).then_inc(vM.h, 1)))
            nob = vM.inc()
            xf_free2[half] = nob
            em('y', lambda y, n=nob, m=m, half=half: (
                y.wait_ge(vM.h, n),
                y.dma_start(out=out.rearrange("(c p) t -> c p t", p=128)[m],
                            in_=obuf[:, half * TC:(half + 1) * TC]).then_inc(dS.h, 16)))
            ofin_free[half] = dS.inc(16)
        out_done_d = dS.n
        em('y', lambda y, n=out_done_d: y.wait_ge(dS.h, n))

        _finish_dev(nc, em, EM, taps, tap_outs, locals())
        p3f.close()
        pxn.close()
        _run_blocks(st, nc, EM)
    return nc, tap_outs


def _finish_dev(nc, em, EM, taps, tap_outs, env):
    dW, dS, dC = env["dW"], env["dS"], env["dC"]
    tM, vM, sM = env["tM"], env["vM"], env["sM"]
    fin = [(dW, dW.n), (dS, dS.n), (dC, dC.n), (tM, tM.n), (vM, vM.n), (sM, sM.n)]
    em('g', lambda g, f=fin: [g.wait_ge(s.h, n) for s, n in f if n])
    em('y', lambda y, f=fin: [y.wait_ge(s.h, n) for s, n in f if n])
    srcs = {
        "bc_dram": (env["bc_dram"], [H * 128, TC]),
        "z_dram": (env["z_dram"], [D, TC]),
        "cc_dram": (env["cc_dram"], [H * 128, TC]),
        "y_dram": (env["y_dram"], [D, TC]),
        "s_dram": (env["s_dram"], [D, TC]),
        "wtil_dram": (env["wtil_dram"], [INTER, D]),
    }
    sb_srcs = {"xc": [128, 16 * TC], "dtt": [32, TC], "xn": [128, DCH * TH],
               "hfull": [128, 2048], "snaps": [128, NCH * 2048], "hin": [128, 2048],
               "xcT": [128, NCH * 16 * 128], "sdtT": [32, 8], "ybf": [128, 2048],
               "dtT_all": [128, NCH * 32], "ldT_all": [128, NCH * 32],
               "rstdb": [32, 128], "ysq": [128, 2048], "rstdf": [32, 128]}
    for name in taps:
        if name in srcs:
            srcap, shape = srcs[name]
            t = nc.dram_tensor("tap_" + name, shape, F32, kind="ExternalOutput").ap()
            em('g', lambda g, t=t, srcap=srcap: g.dma_start(out=t, in_=srcap).then_inc(dW.h, 16))
            dW.inc(16)
        elif name in sb_srcs:
            shape = sb_srcs[name]
            t = nc.dram_tensor("tap_" + name, shape, F32, kind="ExternalOutput").ap()
            em('g', lambda g, t=t, srct=env[name], shape=shape: g.dma_start(
                out=t, in_=srct[0:shape[0], 0:shape[1]]).then_inc(dW.h, 16))
            dW.inc(16)
        tap_outs[name] = "tap_" + name
    em('g', lambda g, n=dW.n: g.wait_ge(dW.h, n))


def _run_blocks(st, nc, EM):
    block = st.enter_context(nc.Block())

    @block.tensor
    def _(t):
        for fn in EM['t']:
            fn(t)

    @block.vector
    def _(v):
        for fn in EM['v']:
            fn(v)

    @block.scalar
    def _(s):
        for fn in EM['s']:
            fn(s)

    @block.gpsimd
    def _(g):
        for fn in EM['g']:
            fn(g)

    @block.sync
    def _(y):
        for fn in EM['y']:
            fn(y)


def kernel(**inputs):
    in_maps = _host_prep(inputs)
    nc, _ = _build(phases=DEBUG_PHASES, taps=DEBUG_TAPS)
    res = run_bass_kernel_spmd(nc, in_maps, core_ids=list(range(NCORES)))
    cols = [np.asarray(res.results[k]["out"]) for k in range(NCORES)]
    full = np.concatenate(cols, axis=1)
    return np.ascontiguousarray(full.T).reshape(1, 8192, 2048).astype(np.float32)


# version 12 (final)
# speedup vs baseline: 1.0503x; 1.0362x over previous
import numpy as np
from contextlib import ExitStack
import concourse.bass as bass
import concourse.mybir as mybir
from concourse.bass_utils import run_bass_kernel_spmd

F32 = mybir.dt.float32
BF16 = mybir.dt.bfloat16
AF = mybir.ActivationFunctionType
ALU = mybir.AluOpType

NCORES = 8
TC = 1024
TH = TC + 3
D = 2048
DCH = 16
H = 32
P = 64
NS = 128
INTER = 8192
ICH = 64
L = 128
NCH = 8
PROJ = 12320
XS0, Z0, B0r, C0r, DT0 = 0, 16, 32, 64, 96

DEBUG_PHASES = 99
DEBUG_TAPS = ()
DEBUG_NO_EPI = False
DEBUG_2C_LVL = 9
DEBUG_BF_STAGES = 4
DEBUG_UG_LVL = 9


def _sylvester(n):
    h = np.array([[1.0]], dtype=np.float32)
    while h.shape[0] < n:
        h = np.block([[h, h], [h, -h]])
    return np.ascontiguousarray(h.astype(np.float32))


def _host_prep(inputs):
    x = np.asarray(inputs["L0_attn_x"], dtype=np.float32)[0]
    xT = np.ascontiguousarray(x.T)
    xpad = np.concatenate([np.zeros((D, 3), np.float32), xT], axis=1)

    WinT = np.asarray(inputs["L0_attn_Win"], np.float32).T
    WoutT = np.asarray(inputs["L0_attn_Wout"], np.float32).T
    WupT = np.asarray(inputs["L0_ffn_W_up"], np.float32).T
    WgtT = np.asarray(inputs["L0_ffn_W_gate"], np.float32).T
    Wp1 = np.ascontiguousarray(
        WinT[:, :PROJ - 32].reshape(4, 4, 128, 48, 256)
        .transpose(3, 0, 2, 1, 4).reshape(48 * 4 * 128, 1024))
    WdtP = np.ascontiguousarray(
        WinT[:, PROJ - 32:].reshape(16, 128, 32).transpose(1, 0, 2).reshape(128, 512))
    Wp2 = np.ascontiguousarray(
        WoutT.reshape(4, 4, 128, 8, 256).transpose(3, 0, 2, 1, 4).reshape(4096, 1024))
    WupP = np.ascontiguousarray(
        WupT.reshape(16, 128, 64, 128).transpose(2, 1, 0, 3).reshape(INTER, D))
    WgtP = np.ascontiguousarray(
        WgtT.reshape(16, 128, 64, 128).transpose(2, 1, 0, 3).reshape(INTER, D))
    WdT = np.asarray(inputs["L0_ffn_W_down"], np.float32).T
    orig = (np.arange(64)[None, :] * 128 + np.arange(128)[:, None]).reshape(-1)
    WdP = np.ascontiguousarray(WdT[orig])

    cwx = np.asarray(inputs["L0_attn_conv_w_x"], np.float32).reshape(D, 4)
    cwx_t = np.ascontiguousarray(cwx.reshape(16, 128, 4).transpose(1, 0, 2).reshape(128, 64))
    cwb = np.asarray(inputs["L0_attn_conv_w_b"], np.float32).reshape(4096, 4)
    cwb_t = np.ascontiguousarray(cwb.reshape(32, 128, 4).transpose(1, 0, 2).reshape(128, 128))
    cwc = np.asarray(inputs["L0_attn_conv_w_c"], np.float32).reshape(4096, 4)
    cwc_t = np.ascontiguousarray(cwc.reshape(32, 128, 4).transpose(1, 0, 2).reshape(128, 128))

    A_log = np.ascontiguousarray(np.asarray(inputs["L0_attn_A_log"], np.float32).reshape(H, 1))
    dt_bias = np.ascontiguousarray(np.asarray(inputs["L0_attn_dt_bias"], np.float32).reshape(H, 1))

    suug_t = np.ascontiguousarray(np.asarray(inputs["L0_ffn_SU_ug"], np.float32).reshape(16, 128).T)
    sudp_t = np.ascontiguousarray(np.asarray(inputs["L0_ffn_SU_dp"], np.float32).reshape(64, 128).T)
    wsug = np.asarray(inputs["L0_ffn_Wscale_ug"], np.float32)
    wsup_t = np.ascontiguousarray(wsug[:INTER].reshape(64, 128).T)
    wsgt_t = np.ascontiguousarray(wsug[INTER:].reshape(64, 128).T)
    wsdp_t = np.ascontiguousarray(
        np.asarray(inputs["L0_ffn_Wscale_dp"], np.float32).reshape(16, 128).T)

    bo = np.zeros((128, 16 * 32), np.float32)
    sel = np.zeros((32, 16 * 128), np.float32)
    for j in range(16):
        for p in range(128):
            h = 2 * j + (1 if p >= 64 else 0)
            bo[p, j * 32 + h] = 1.0
            sel[h, j * 128 + p] = 1.0

    shared = {
        "Wp1": Wp1, "WdtP": WdtP, "Wp2": Wp2, "WupP": WupP, "WgtP": WgtP, "WdP": WdP,
        "cwx_t": cwx_t, "cwb_t": cwb_t, "cwc_t": cwc_t,
        "A_log": A_log, "dt_bias": dt_bias,
        "suug_t": suug_t, "sudp_t": sudp_t,
        "wsup_t": wsup_t, "wsgt_t": wsgt_t, "wsdp_t": wsdp_t,
        "H128": _sylvester(128), "H64": _sylvester(64),
        "H64blk": np.ascontiguousarray(np.kron(np.eye(2, dtype=np.float32), _sylvester(64))),
        "IDENT": np.eye(128, dtype=np.float32),
        "TRIU": np.ascontiguousarray(np.triu(np.ones((128, 128), np.float32))),
        "BO": np.ascontiguousarray(bo), "SEL": np.ascontiguousarray(sel),
    }
    in_maps = []
    for k in range(NCORES):
        m = dict(shared)
        m["xin"] = np.ascontiguousarray(xpad[:, TC * k: TC * k + TH])
        MkM = np.zeros((8, 8), np.float32)
        for j in range(8):
            for i in range(8):
                if j < i <= k - 1:
                    MkM[j, i] = 1.0
        m["MkT"] = np.ascontiguousarray(MkM.T)
        cm = np.zeros((1, 256), np.float32)
        for j in range(8):
            if j < k:
                cm[0, j * 32: j * 32 + 32] = 1.0
        m["coremask"] = cm
        in_maps.append(m)
    return in_maps


class Sem:
    def __init__(self, handle):
        self.h = handle
        self.n = 0

    def inc(self, k=1):
        self.n += k
        return self.n


def _build(phases=99, taps=()):
    nc = bass.Bass("TRN2", target_bir_lowering=False)
    taps = list(taps)
    tap_outs = {}

    din = {}
    for name, shape in [
        ("xin", [D, TH]), ("Wp1", [48 * 4 * 128, 1024]), ("WdtP", [128, 512]),
        ("Wp2", [4096, 1024]),
        ("WupP", [INTER, D]), ("WgtP", [INTER, D]), ("WdP", [INTER, D]),
        ("cwx_t", [128, 64]), ("cwb_t", [128, 128]), ("cwc_t", [128, 128]),
        ("A_log", [H, 1]), ("dt_bias", [H, 1]),
        ("suug_t", [128, 16]), ("sudp_t", [128, 64]),
        ("wsup_t", [128, 64]), ("wsgt_t", [128, 64]), ("wsdp_t", [128, 16]),
        ("H128", [128, 128]), ("H64", [64, 64]), ("H64blk", [128, 128]),
        ("IDENT", [128, 128]),
        ("TRIU", [128, 128]), ("BO", [128, 512]), ("SEL", [32, 2048]),
        ("MkT", [8, 8]), ("coremask", [1, 256]),
    ]:
        din[name] = nc.dram_tensor(name, shape, F32, kind="ExternalInput").ap()

    out = nc.dram_tensor("out", [D, TC], F32, kind="ExternalOutput").ap()

    z_dram = nc.dram_tensor("z_dram", [D, TC], BF16).ap()
    bc_dram = nc.dram_tensor("bc_dram", [H * 128, TC], BF16).ap()
    cc_dram = nc.dram_tensor("cc_dram", [H * 128, TC], BF16).ap()
    y_dram = nc.dram_tensor("y_dram", [D, TC], BF16).ap()
    s_dram = nc.dram_tensor("s_dram", [D, TC], BF16).ap()
    wtil_dram = nc.dram_tensor("wtil_dram", [INTER, D], BF16).ap()
    ldb_dram = nc.dram_tensor("ldb_dram", [NCH * 32, 128], F32).ap()
    ag_in_h = nc.dram_tensor("ag_in_h", [128, H * P], F32).ap()
    ag_in_s = nc.dram_tensor("ag_in_s", [32, 1], F32).ap()
    ag_out_h = nc.dram_tensor("ag_out_h", [NCORES * 128, H * P], F32,
                              addr_space="Shared").ap()
    ag_out_s = nc.dram_tensor("ag_out_s", [NCORES * 32, 1], F32,
                              addr_space="Shared").ap()
    db_dram = nc.dram_tensor("db_dram", [8, 32], F32).ap()
    pc_dram = nc.dram_tensor("pc_dram", [H, 8], F32).ap()

    EM = {k: [] for k in "tvsgy"}

    def em(eng, fn):
        EM[eng].append(fn)

    with ExitStack() as st:
        SB = lambda n, s, d: st.enter_context(nc.sbuf_tensor(n, s, d))
        SEM = lambda n: Sem(st.enter_context(nc.semaphore(n)))

        dW = SEM("dW")
        dS = SEM("dS")
        dC = SEM("dC")
        tM = SEM("tM")
        vM = SEM("vM")
        sM = SEM("sM")
        cS = SEM("cS")

        cwxS = SB("cwxS", [128, 64], F32)
        cwbS = SB("cwbS", [128, 128], F32)
        cwcS = SB("cwcS", [128, 128], F32)
        Acol = SB("Acol", [H, 1], F32)
        dtb = SB("dtb", [H, 1], F32)
        suugS = SB("suugS", [128, 16], F32)
        sudpS = SB("sudpS", [128, 64], F32)
        wsupS = SB("wsupS", [128, 64], F32)
        wsgtS = SB("wsgtS", [128, 64], F32)
        wsdpS = SB("wsdpS", [128, 16], F32)
        upscale = SB("upscale", [128, 64], F32)
        gtscale = SB("gtscale", [128, 64], F32)
        dnscale = SB("dnscale", [128, 16], F32)
        h128b = SB("h128b", [128, 128], BF16)
        h128su = SB("h128su", [128, 16 * 128], BF16)
        h64b = SB("h64b", [64, 64], BF16)
        h64blkb = SB("h64blkb", [128, 128], BF16)
        idb = SB("idb", [128, 128], BF16)
        idf = SB("idf", [128, 128], F32)
        ones1b = SB("ones1b", [1, 128], BF16)
        ones1f = SB("ones1f", [1, 128], F32)
        onescolb = SB("onescolb", [128, 1], BF16)
        zeros32 = SB("zeros32", [32, 128], F32)
        epsln = SB("epsln", [1, 1], F32)
        epsrms = SB("epsrms", [32, 1], F32)

        PB = [st.enter_context(nc.psum_tensor(f"pb{i}", [128, 512], F32)) for i in range(8)]
        PBb = [PB[i].ap().bitcast(BF16) for i in range(8)]

        pxn = ExitStack()
        xn = pxn.enter_context(nc.sbuf_tensor("xn", [128, DCH * TH], BF16))
        pk2 = ExitStack()
        xcT = pk2.enter_context(nc.sbuf_tensor("xcT", [128, NCH * 16 * 128], BF16))
        snaps = pk2.enter_context(nc.sbuf_tensor("snaps", [128, NCH * 2048], BF16))
        hfull = pk2.enter_context(nc.sbuf_tensor("hfull", [128, 2048], F32))
        dtT_all = pk2.enter_context(nc.sbuf_tensor("dtT_all", [128, NCH * 32], F32))
        ldT_all = pk2.enter_context(nc.sbuf_tensor("ldT_all", [128, NCH * 32], F32))
        ldTneg_all = pk2.enter_context(nc.sbuf_tensor("ldTneg_all", [128, NCH * 32], F32))
        sdtT = pk2.enter_context(nc.sbuf_tensor("sdtT", [32, 8], F32))
        pxc = ExitStack()
        xc = pxc.enter_context(nc.sbuf_tensor("xc", [128, 16 * TC], BF16))
        dtt = pxc.enter_context(nc.sbuf_tensor("dtt", [32, TC], F32))

        p0 = ExitStack()
        lnrow = p0.enter_context(nc.sbuf_tensor("lnrow", [1, 6 * TH], F32))
        xfs = p0.enter_context(nc.sbuf_tensor("xfs", [128, 2 * TH], F32))
        xbs = p0.enter_context(nc.sbuf_tensor("xbs", [128, 2 * TH], BF16))
        xsq = p0.enter_context(nc.sbuf_tensor("xsq", [128, 2 * TH], BF16))

        const_loads = [
            (cwxS, "cwx_t"), (cwbS, "cwb_t"), (cwcS, "cwc_t"), (dtb, "dt_bias"),
            (suugS, "suug_t"), (sudpS, "sudp_t"), (wsupS, "wsup_t"),
            (wsgtS, "wsgt_t"), (wsdpS, "wsdp_t"), (h128b, "H128"), (h64b, "H64"),
            (h64blkb, "H64blk"), (idb, "IDENT"), (idf, "IDENT"), (Acol, "A_log"),
        ]
        for dst, srcn in const_loads:
            em('g', lambda g, d=dst, s=din[srcn]: g.dma_start(out=d[:], in_=s).then_inc(dW.h, 16))
        dW.inc(16 * len(const_loads))
        n_const = dW.n

        em('s', lambda s: s.wait_ge(dW.h, n_const))
        em('s', lambda s: s.activation(Acol[:], Acol[:], AF.Exp))
        em('s', lambda s: s.activation(Acol[:], Acol[:], AF.Copy, scale=-1.0).then_inc(sM.h, 1))
        sA = sM.inc()

        em('v', lambda v: v.wait_ge(dW.h, n_const))
        em('v', lambda v: v.memset(ones1b[:], 1.0))
        em('v', lambda v: v.memset(ones1f[:], 1.0))
        em('v', lambda v: v.memset(onescolb[:], 1.0))
        em('v', lambda v: v.memset(zeros32[:], 0.0))
        em('v', lambda v: v.memset(epsln[:], 1e-5))
        em('v', lambda v: v.memset(epsrms[:], 1e-6))
        for c in range(16):
            em('v', lambda v, c=c: v.tensor_scalar(
                h128su[:, c * 128:(c + 1) * 128], h128b[:], suugS[:, c:c + 1], None, ALU.mult))
        em('v', lambda v: v.tensor_mul(upscale[:], wsupS[:], sudpS[:]))
        em('v', lambda v: v.tensor_scalar(upscale[:], upscale[:], 1.0 / np.sqrt(2048.0), None, ALU.mult))
        em('v', lambda v: v.tensor_scalar(gtscale[:], wsgtS[:], 1.0 / np.sqrt(2048.0), None, ALU.mult))
        em('v', lambda v: v.tensor_scalar(dnscale[:], wsdpS[:], 1.0 / np.sqrt(8192.0), None,
                                          ALU.mult).then_inc(vM.h, 1))
        vConsts = vM.inc()

        SL = [(0, 343), (343, 342), (685, 342)]
        xb_free = [0, 0]
        for j in range(DCH):
            hb = j % 2
            ld_wait = max(xb_free[hb], 0)
            em('g', lambda g, j=j, hb=hb, w=ld_wait: (
                g.wait_ge(tM.h, w) if w else None,
                g.dma_start(out=xbs[:, hb * TH:hb * TH + TH],
                            in_=din["xin"].rearrange("(c p) t -> c p t", p=128)[j]
                            ).then_inc(dW.h, 16)))
            nl = dW.inc(16)
            em('s', lambda s, hb=hb, nl=nl, w=xb_free[hb]: (
                s.wait_ge(dW.h, nl),
                s.wait_ge(tM.h, w) if w else None,
                s.activation(xsq[:, hb * TH:hb * TH + TH], xbs[:, hb * TH:hb * TH + TH],
                             AF.Square).then_inc(sM.h, 1)))
            nsq = sM.inc()
            em('t', lambda t, nl=nl, nsq=nsq: (t.wait_ge(dW.h, nl), t.wait_ge(sM.h, nsq)))
            for si, (off, w) in enumerate(SL):
                last = (si == len(SL) - 1)
                em('t', lambda t, j=j, hb=hb, si=si, off=off, w=w, last=last: (
                    t.matmul(PB[si].ap()[0:1, 0:w], onescolb[:],
                             xbs[:, hb * TH + off: hb * TH + off + w],
                             start=(j == 0), stop=(j == DCH - 1)),
                    t.matmul(PB[3 + si].ap()[0:1, 0:w], onescolb[:],
                             xsq[:, hb * TH + off: hb * TH + off + w],
                             start=(j == 0), stop=(j == DCH - 1)
                             ).then_inc(tM.h, 1) if last else
                    t.matmul(PB[3 + si].ap()[0:1, 0:w], onescolb[:],
                             xsq[:, hb * TH + off: hb * TH + off + w],
                             start=(j == 0), stop=(j == DCH - 1))))
            xb_free[hb] = tM.inc()
        stats_done = tM.n

        em('v', lambda v: v.wait_ge(tM.h, stats_done))
        for si, (off, w) in enumerate(SL):
            last = (si == len(SL) - 1)
            em('v', lambda v, si=si, off=off, w=w, last=last: (
                v.tensor_scalar(lnrow[0:1, off:off + w], PB[si].ap()[0:1, 0:w],
                                1.0 / 2048.0, None, ALU.mult),
                v.tensor_mul(lnrow[0:1, 2 * TH + off:2 * TH + off + w],
                             lnrow[0:1, off:off + w], lnrow[0:1, off:off + w]),
                (v.scalar_tensor_tensor(lnrow[0:1, 3 * TH + off:3 * TH + off + w],
                                        PB[3 + si].ap()[0:1, 0:w], 1.0 / 2048.0,
                                        lnrow[0:1, 2 * TH + off:2 * TH + off + w],
                                        ALU.mult, ALU.subtract).then_inc(vM.h, 1)
                 if last else
                 v.scalar_tensor_tensor(lnrow[0:1, 3 * TH + off:3 * TH + off + w],
                                        PB[3 + si].ap()[0:1, 0:w], 1.0 / 2048.0,
                                        lnrow[0:1, 2 * TH + off:2 * TH + off + w],
                                        ALU.mult, ALU.subtract))))
        vVar = vM.inc()
        em('s', lambda s: s.wait_ge(vM.h, vVar))
        em('s', lambda s: (
            s.activation(lnrow[0:1, 2 * TH:3 * TH], lnrow[0:1, 3 * TH:4 * TH],
                         AF.Ln, bias=epsln[0:1, 0:1]),
            s.activation(lnrow[0:1, TH:2 * TH], lnrow[0:1, 2 * TH:3 * TH],
                         AF.Exp, scale=-0.5).then_inc(sM.h, 1)))
        vRstd = sM.inc()

        em('t', lambda t: t.wait_ge(sM.h, vRstd))
        for si, (off, w) in enumerate(SL):
            last = (si == len(SL) - 1)
            em('t', lambda t, si=si, off=off, w=w, last=last: (
                t.matmul(PB[si].ap()[:, 0:w], ones1f[:],
                         lnrow[0:1, off:off + w], start=True, stop=True),
                (t.matmul(PB[3 + si].ap()[:, 0:w], ones1f[:],
                          lnrow[0:1, TH + off:TH + off + w], start=True, stop=True
                          ).then_inc(tM.h, 1)
                 if last else
                 t.matmul(PB[3 + si].ap()[:, 0:w], ones1f[:],
                          lnrow[0:1, TH + off:TH + off + w], start=True, stop=True))))
        tBcast = tM.inc()

        xf_free = [0, 0]
        em('v', lambda v: v.wait_ge(tM.h, tBcast))
        for j in range(DCH):
            hb = j % 2
            em('y', lambda y, j=j, hb=hb, w=xf_free[hb]: (
                y.wait_ge(vM.h, w) if w else None,
                y.dma_start(out=xfs[:, hb * TH:hb * TH + TH],
                            in_=din["xin"].rearrange("(c p) t -> c p t", p=128)[j]
                            ).then_inc(dS.h, 16)))
            nl = dS.inc(16)
            em('v', lambda v, nl=nl: v.wait_ge(dS.h, nl))
            for si, (off, w) in enumerate(SL):
                last = (si == len(SL) - 1)
                em('v', lambda v, j=j, hb=hb, si=si, off=off, w=w, last=last: (
                    v.tensor_sub(xfs[:, hb * TH + off:hb * TH + off + w],
                                 xfs[:, hb * TH + off:hb * TH + off + w],
                                 PB[si].ap()[:, 0:w]),
                    (v.tensor_tensor(xn[:, j * TH + off:j * TH + off + w],
                                     xfs[:, hb * TH + off:hb * TH + off + w],
                                     PB[3 + si].ap()[:, 0:w], ALU.mult).then_inc(vM.h, 1)
                     if last else
                     v.tensor_tensor(xn[:, j * TH + off:j * TH + off + w],
                                     xfs[:, hb * TH + off:hb * TH + off + w],
                                     PB[3 + si].ap()[:, 0:w], ALU.mult))))
            xf_free[hb] = vM.inc()
        xn_done_v = vM.n

        p0.close()
        p1 = ExitStack()
        wbuf = p1.enter_context(nc.sbuf_tensor("wbuf", [128, 2 * 4096], BF16))
        wdt = p1.enter_context(nc.sbuf_tensor("wdt", [128, 512], BF16))
        pre = p1.enter_context(nc.sbuf_tensor("pre", [128, 4 * TH], BF16))
        accb = p1.enter_context(nc.sbuf_tensor("accb", [128, 2 * TC], F32))
        spb = p1.enter_context(nc.sbuf_tensor("spb", [128, 3 * TC], BF16))
        dtw = p1.enter_context(nc.sbuf_tensor("dtw", [32, TC], F32))

        mlist = (list(range(XS0, XS0 + 16)) + [DT0] + list(range(B0r, B0r + 32))
                 + list(range(C0r, C0r + 32)) + list(range(Z0, Z0 + 16)))

        bank_free = [0, 0, 0, 0]
        slice_ctr = 0
        wbuf_free = [0, 0]
        pre_free = [0, 0, 0, 0]
        accb_free = [0, 0]
        spb_free = [0, 0, 0]
        pjc = 0
        spill_ctr = 0
        first_w_dma = True
        dt_evac_done = 0

        groups = []
        i = 0
        while i < len(mlist):
            mj = mlist[i]
            if mj == DT0:
                groups.append(("dt", [mj]))
                i += 1
            else:
                groups.append(("n", [mlist[i], mlist[i + 1]]))
                i += 2

        em('t', lambda t, n=xn_done_v: t.wait_ge(vM.h, n))
        gi = 0
        for kind, ms in groups:
            half = gi % 2
            waits = []
            if first_w_dma:
                waits.append(('v', xn_done_v))
            if wbuf_free[half]:
                waits.append(('t', wbuf_free[half]))
            wv = list(waits)
            if kind == "dt":
                em('g', lambda g, wv=wv: (
                    [g.wait_ge({'v': vM, 't': tM}[e].h, n) for e, n in wv],
                    g.dma_start(out=wdt[:],
                                in_=din["WdtP"][:, :]).then_inc(dW.h, 16)))
                dW.inc(16)
            else:
                rb0 = (ms[0] // 2) * 4
                for kg in range(4):
                    wvk = wv if kg == 0 else []
                    em('g', lambda g, kg=kg, rb0=rb0, half=half, wvk=wvk: (
                        [g.wait_ge({'v': vM, 't': tM}[e].h, n) for e, n in wvk],
                        g.dma_start(
                            out=wbuf[:, half * 4096 + kg * 1024: half * 4096 + (kg + 1) * 1024],
                            in_=din["Wp1"][(rb0 + kg) * 128:(rb0 + kg + 1) * 128, :]
                            ).then_inc(dW.h, 16)))
                    dW.inc(16)
            first_w_dma = False
            nload = dW.n

            for mj in ms:
                is_conv = (XS0 <= mj < 16) or (B0r <= mj < 96)
                if kind == "dt":
                    slices = [(3, 512, 0), (515, 512, 512)]
                elif is_conv:
                    slices = [(0, 343, 0), (343, 342, 343), (685, 342, 685)]
                else:
                    slices = [(3, 512, 0), (515, 512, 512)]
                ev_vals = []
                for (off, w, dcol) in slices:
                    b = slice_ctr % 4
                    slice_ctr += 1
                    em('t', lambda t, nload=nload, bf=bank_free[b]: (
                        t.wait_ge(dW.h, nload),
                        t.wait_ge(sM.h, bf) if bf else None))
                    for k in range(16):
                        if kind == "dt":
                            lhsT = lambda k=k: wdt[:, k * 32:(k + 1) * 32]
                            M = 32
                        else:
                            jj = ms.index(mj)
                            kg, k2 = divmod(k, 4)
                            base = half * 4096 + kg * 1024 + k2 * 256 + jj * 128
                            lhsT = lambda base=base: wbuf[:, base:base + 128]
                            M = 128
                        em('t', lambda t, k=k, lhsT=lhsT, M=M, b=b, off=off, w=w: (
                            t.matmul(PB[b].ap()[0:M, 0:w], lhsT(),
                                     xn[:, k * TH + off:k * TH + off + w],
                                     start=(k == 0), stop=(k == 15)
                                     ).then_inc(tM.h, 1) if k == 15 else
                            t.matmul(PB[b].ap()[0:M, 0:w], lhsT(),
                                     xn[:, k * TH + off:k * TH + off + w],
                                     start=(k == 0), stop=(k == 15))))
                    ntm = tM.inc()
                    if kind == "dt":
                        em('s', lambda s, ntm=ntm, b=b, w=w, dcol=dcol: (
                            s.wait_ge(tM.h, ntm),
                            s.activation(dtt[0:32, dcol:dcol + w], PB[b].ap()[0:32, 0:w],
                                         AF.Copy).then_inc(sM.h, 1)))
                    elif is_conv:
                        slot = pjc % 4
                        em('s', lambda s, ntm=ntm, b=b, w=w, off=off, slot=slot,
                               pf=pre_free[slot]: (
                            s.wait_ge(tM.h, ntm),
                            s.wait_ge(vM.h, pf) if pf else None,
                            s.activation(pre[:, slot * TH + off:slot * TH + off + w],
                                         PB[b].ap()[:, 0:w], AF.Copy).then_inc(sM.h, 1)))
                    else:
                        slot = spill_ctr % 3
                        em('s', lambda s, ntm=ntm, b=b, w=w, dcol=dcol, slot=slot,
                               sf=spb_free[slot]: (
                            s.wait_ge(tM.h, ntm),
                            s.wait_ge(dS.h, sf) if sf else None,
                            s.activation(spb[:, slot * TC + dcol:slot * TC + dcol + w],
                                         PB[b].ap()[:, 0:w], AF.Copy).then_inc(sM.h, 1)))
                    bank_free[b] = sM.inc()
                    ev_vals.append(bank_free[b])
                if kind == "dt":
                    dt_evac_done = ev_vals[-1]
                if is_conv:
                    slot = pjc % 4
                    ah = pjc % 2
                    if mj < 16:
                        cw, cwcol0 = cwxS, mj * 4
                        dst = lambda mj=mj: xc[:, mj * TC:(mj + 1) * TC]
                        spill = None
                    elif B0r <= mj < C0r:
                        cw, cwcol0 = cwbS, (mj - B0r) * 4
                        sslot = spill_ctr % 3
                        dst = lambda sslot=sslot: spb[:, sslot * TC:(sslot + 1) * TC]
                        spill = ("bc", mj - B0r, sslot)
                    else:
                        cw, cwcol0 = cwcS, (mj - C0r) * 4
                        sslot = spill_ctr % 3
                        dst = lambda sslot=sslot: spb[:, sslot * TC:(sslot + 1) * TC]
                        spill = ("cc", mj - C0r, sslot)
                    em('v', lambda v, ev=ev_vals[-1], af=accb_free[ah], slot=slot, ah=ah,
                           cw=cw, c0=cwcol0: (
                        v.wait_ge(sM.h, ev),
                        v.wait_ge(sM.h, af) if af else None,
                        v.tensor_scalar(accb[:, ah * TC:(ah + 1) * TC],
                                        pre[:, slot * TH + 3:slot * TH + 3 + TC],
                                        cw[:, c0 + 3:c0 + 4], None, ALU.mult),
                        v.scalar_tensor_tensor(accb[:, ah * TC:(ah + 1) * TC],
                                               pre[:, slot * TH + 2:slot * TH + 2 + TC],
                                               cw[:, c0 + 2:c0 + 3],
                                               accb[:, ah * TC:(ah + 1) * TC],
                                               ALU.mult, ALU.add),
                        v.scalar_tensor_tensor(accb[:, ah * TC:(ah + 1) * TC],
                                               pre[:, slot * TH + 1:slot * TH + 1 + TC],
                                               cw[:, c0 + 1:c0 + 2],
                                               accb[:, ah * TC:(ah + 1) * TC],
                                               ALU.mult, ALU.add),
                        v.scalar_tensor_tensor(accb[:, ah * TC:(ah + 1) * TC],
                                               pre[:, slot * TH:slot * TH + TC],
                                               cw[:, c0:c0 + 1],
                                               accb[:, ah * TC:(ah + 1) * TC],
                                               ALU.mult, ALU.add).then_inc(vM.h, 1)))
                    pre_free[slot] = vM.inc()
                    if spill is None:
                        em('s', lambda s, vv=pre_free[slot], ah=ah, dst=dst: (
                            s.wait_ge(vM.h, vv),
                            s.activation(dst(), accb[:, ah * TC:(ah + 1) * TC],
                                         AF.Silu).then_inc(sM.h, 1)))
                        accb_free[ah] = sM.inc()
                    else:
                        skind, hh, sslot = spill
                        sdram = cc_dram if skind == "cc" else bc_dram
                        em('s', lambda s, vv=pre_free[slot], ah=ah, dst=dst,
                               sf=spb_free[sslot]: (
                            s.wait_ge(vM.h, vv),
                            s.wait_ge(dS.h, sf) if sf else None,
                            s.activation(dst(), accb[:, ah * TC:(ah + 1) * TC],
                                         AF.Silu).then_inc(sM.h, 1)))
                        accb_free[ah] = sM.inc()
                        em('y', lambda y, sv=accb_free[ah], hh=hh, sslot=sslot, sdram=sdram: (
                            y.wait_ge(sM.h, sv),
                            y.dma_start(out=sdram.rearrange("(h p) t -> h p t", p=128)[hh],
                                        in_=spb[:, sslot * TC:(sslot + 1) * TC]
                                        ).then_inc(dS.h, 16)))
                        spb_free[sslot] = dS.inc(16)
                        spill_ctr += 1
                    pjc += 1
                elif kind == "dt":
                    pass
                else:
                    sslot = spill_ctr % 3
                    em('y', lambda y, sv=ev_vals[-1], hh=mj - Z0, sslot=sslot: (
                        y.wait_ge(sM.h, sv),
                        y.dma_start(out=z_dram.rearrange("(h p) t -> h p t", p=128)[hh],
                                    in_=spb[:, sslot * TC:(sslot + 1) * TC]
                                    ).then_inc(dS.h, 16)))
                    spb_free[sslot] = dS.inc(16)
                    spill_ctr += 1
            if kind != "dt":
                wbuf_free[half] = tM.n
            gi += 1

        em('v', lambda v, n=dt_evac_done: (
            v.wait_ge(sM.h, n),
            v.tensor_scalar(dtt[0:32, 0:TC], dtt[0:32, 0:TC], dtb[:], None,
                            ALU.add).then_inc(vM.h, 1)))
        vDtB = vM.inc()
        em('s', lambda s, n=vDtB: (
            s.wait_ge(vM.h, n),
            s.activation(dtw[0:32, 0:TC], dtt[0:32, 0:TC], AF.Abs),
            s.activation(dtw[0:32, 0:TC], dtw[0:32, 0:TC], AF.Exp, scale=-1.0),
            s.activation(dtw[0:32, 0:TC], dtw[0:32, 0:TC], AF.Ln,
                         bias=1.0).then_inc(sM.h, 1)))
        sDt1 = sM.inc()
        em('v', lambda v: (
            v.wait_ge(sM.h, sDt1),
            v.tensor_relu(dtt[0:32, 0:TC], dtt[0:32, 0:TC]),
            v.tensor_add(dtt[0:32, 0:TC], dtt[0:32, 0:TC],
                         dtw[0:32, 0:TC]).then_inc(vM.h, 1)))
        vDt = vM.inc()

        p1w = ExitStack()
        wdb = p1w.enter_context(nc.sbuf_tensor("wdb", [128, 2 * D], BF16))
        wtb = p1w.enter_context(nc.sbuf_tensor("wtb", [128, D], BF16))
        wdb_free = [0, 0]
        wtb_free = 0
        ev_v = ev_s = 0
        for r in range(64):
            hb = r % 2
            em('g', lambda g, r=r, hb=hb, wf=wdb_free[hb]: (
                g.wait_ge(tM.h, wf) if wf else None,
                g.dma_start(out=wdb[:, hb * D:(hb + 1) * D],
                            in_=din["WdP"][r * 128:(r + 1) * 128, :]).then_inc(dW.h, 16)))
            nl = dW.inc(16)
            em('t', lambda t, nl=nl, ev=ev_v, es=ev_s: (
                t.wait_ge(dW.h, nl),
                t.wait_ge(vM.h, ev) if ev else None,
                t.wait_ge(sM.h, es) if es else None))
            for q in range(4):
                em('t', lambda t, hb=hb, q=q: (
                    t.matmul(PB[4 + q].ap()[:, 0:512], h64blkb[:],
                             wdb[:, hb * D + q * 512:hb * D + (q + 1) * 512],
                             start=True, stop=True).then_inc(tM.h, 1)
                    if q == 3 else
                    t.matmul(PB[4 + q].ap()[:, 0:512], h64blkb[:],
                             wdb[:, hb * D + q * 512:hb * D + (q + 1) * 512],
                             start=True, stop=True)))
            wdb_free[hb] = tM.inc()
            em('v', lambda v, ntm=wdb_free[hb], wf=wtb_free: (
                v.wait_ge(tM.h, ntm),
                v.wait_ge(dC.h, wf) if wf else None,
                v.tensor_copy(wtb[:, 0:512], PB[4].ap()[:, 0:512]),
                v.tensor_copy(wtb[:, 512:1024], PB[5].ap()[:, 0:512]).then_inc(vM.h, 1)))
            ev_v = vM.inc()
            em('s', lambda s, ntm=wdb_free[hb], wf=wtb_free: (
                s.wait_ge(tM.h, ntm),
                s.wait_ge(dC.h, wf) if wf else None,
                s.activation(wtb[:, 1024:1536], PB[6].ap()[:, 0:512], AF.Copy),
                s.activation(wtb[:, 1536:2048], PB[7].ap()[:, 0:512],
                             AF.Copy).then_inc(sM.h, 1)))
            ev_s = sM.inc()
            em('s', lambda s, vv=ev_v, r=r: (
                s.wait_ge(vM.h, vv),
                s.dma_start(out=wtil_dram.rearrange("(c p) g -> c p g", p=128)[:, 2 * r, :],
                            in_=wtb[0:64, :]).then_inc(dC.h, 16),
                s.dma_start(out=wtil_dram.rearrange("(c p) g -> c p g", p=128)[:, 2 * r + 1, :],
                            in_=wtb[64:128, :]).then_inc(dC.h, 16)))
            wtb_free = dC.inc(32)
        wtil_done_d = dC.n

        p1w.close()
        p1.close()
        p1_spills_d = dS.n
        p2a = ExitStack()
        bcbuf = p2a.enter_context(nc.sbuf_tensor("bcbuf", [128, 2 * 4096], BF16))
        bcTall = p2a.enter_context(nc.sbuf_tensor("bcTall", [128, 4096], BF16))
        ld32 = p2a.enter_context(nc.sbuf_tensor("ld32", [32, 128], F32))
        dta = p2a.enter_context(nc.sbuf_tensor("dta", [32, 128], F32))
        w32pre = p2a.enter_context(nc.sbuf_tensor("w32pre", [32, 128], F32))
        e32t = p2a.enter_context(nc.sbuf_tensor("e32t", [32, 128], F32))
        W32 = p2a.enter_context(nc.sbuf_tensor("W32", [128, 32], F32))
        lamb = p2a.enter_context(nc.sbuf_tensor("lamb", [128, 32], F32))
        lamRowS = p2a.enter_context(nc.sbuf_tensor("lamRowS", [1, 32], F32))
        lamCol = p2a.enter_context(nc.sbuf_tensor("lamCol", [32, 1], F32))
        dtxw = p2a.enter_context(nc.sbuf_tensor("dtxw", [128, 2 * 64], BF16))

        em('v', lambda v: v.memset(hfull[:], 0.0).then_inc(vM.h, 1))
        hfull_z = vM.inc()

        bcb_free = [0, 0]
        pb4_free = 0
        pb5_free = 0
        pbb_free = [0, 0]
        dh_free = [0, 0, 0, 0]
        ld32_free = 0
        dtxw_free = [0, 0]
        e_done = hfull_z
        ldb_val = []

        for c in range(NCH):
            hb = c % 2
            em('y', lambda y, c=c, hb=hb, bf=bcb_free[hb], pd=p1_spills_d: (
                y.wait_ge(dS.h, pd),
                y.wait_ge(tM.h, bf) if bf else None,
                y.dma_start(out=bcbuf[:, hb * 4096:(hb + 1) * 4096]
                            .rearrange("n (h t) -> n h t", h=32),
                            in_=bc_dram.rearrange("(h n) t -> n h t", n=128)
                            [:, :, c * 128:(c + 1) * 128]).then_inc(dS.h, 16)))
            nbc = dS.inc(16)

            em('v', lambda v, c=c, lf=ld32_free: (
                v.wait_ge(dS.h, lf) if lf else None,
                v.tensor_scalar(dta[:], dtt[0:32, c * 128:(c + 1) * 128], Acol[:],
                                None, ALU.mult),
                v.tensor_tensor_scan(ld32[:], dta[:], zeros32[:], 0.0, ALU.add, ALU.add),
                v.reduce_sum(sdtT[0:32, c:c + 1], dtt[0:32, c * 128:(c + 1) * 128],
                             axis=mybir.AxisListType.X),
                v.tensor_scalar(e32t[:], ld32[:], ld32[0:32, 127:128], None,
                                ALU.subtract).then_inc(vM.h, 1)))
            v1 = vM.inc()
            em('s', lambda s, n=v1: (
                s.wait_ge(vM.h, n),
                s.activation(e32t[:], e32t[:], AF.Exp, scale=-1.0),
                s.activation(lamCol[:], ld32[0:32, 127:128], AF.Exp).then_inc(sM.h, 1)))
            s1 = sM.inc()
            em('v', lambda v, n=s1, c=c: (
                v.wait_ge(sM.h, n),
                v.tensor_tensor(w32pre[:], e32t[:],
                                dtt[0:32, c * 128:(c + 1) * 128],
                                ALU.mult).then_inc(vM.h, 1)))
            v2 = vM.inc()
            em('t', lambda t, n=v2, pf=pb4_free, c=c: (
                t.wait_ge(vM.h, n),
                t.wait_ge(vM.h, pf) if pf else None,
                t.transpose(PB[4].ap()[:, 0:32], dtt[0:32, c * 128:(c + 1) * 128],
                            idf[0:32, 0:32]),
                t.transpose(PB[4].ap()[:, 32:64], ld32[:], idf[0:32, 0:32]),
                t.transpose(PB[4].ap()[:, 64:96], w32pre[:], idf[0:32, 0:32]),
                t.transpose(PB[4].ap()[0:1, 96:128], lamCol[:],
                            idf[0:32, 0:32]).then_inc(tM.h, 1)))
            t1 = tM.inc()
            em('v', lambda v, n=t1, c=c: (
                v.wait_ge(tM.h, n),
                v.tensor_copy(dtT_all[:, c * 32:(c + 1) * 32], PB[4].ap()[:, 0:32]),
                v.tensor_copy(ldT_all[:, c * 32:(c + 1) * 32], PB[4].ap()[:, 32:64]),
                v.tensor_copy(W32[:], PB[4].ap()[:, 64:96]),
                v.tensor_copy(lamRowS[:], PB[4].ap()[0:1, 96:128]),
                v.tensor_scalar(ldTneg_all[:, c * 32:(c + 1) * 32],
                                ldT_all[:, c * 32:(c + 1) * 32], -1.0, None,
                                ALU.mult).then_inc(vM.h, 1)))
            pb4_free = v3 = vM.inc()
            em('t', lambda t, n=v3, pf=pb5_free: (
                t.wait_ge(vM.h, n),
                t.wait_ge(sM.h, pf) if pf else None,
                t.matmul(PB[5].ap()[:, 0:32], ones1f[:], lamRowS[:],
                         start=True, stop=True).then_inc(tM.h, 1)))
            t2 = tM.inc()
            em('s', lambda s, n=t2: (
                s.wait_ge(tM.h, n),
                s.activation(lamb[:], PB[5].ap()[:, 0:32], AF.Copy).then_inc(sM.h, 1)))
            pb5_free = s2 = sM.inc()
            em('y', lambda y, n=v1, c=c: (
                y.wait_ge(vM.h, n),
                y.dma_start(out=ldb_dram[32 * c:32 * (c + 1), :],
                            in_=ld32[:]).then_inc(dS.h, 16)))
            ld32_free = dS.inc(16)
            ldb_val.append(ld32_free)
            em('s', lambda s, n=e_done, c=c: (
                s.wait_ge(vM.h, n),
                s.activation(snaps[:, c * 2048:(c + 1) * 2048], hfull[:],
                             AF.Copy).then_inc(sM.h, 1)))
            s3 = sM.inc()
            for rd in range(2):
                bb = 6 + rd
                em('t', lambda t, pf=pbb_free[rd]: (
                    t.wait_ge(vM.h, pf) if pf else None))
                for i in range(8):
                    j = rd * 8 + i
                    em('t', lambda t, c=c, j=j, i=i, bb=bb: (
                        t.transpose(PBb[bb][:, i * 128:(i + 1) * 128],
                                    xc[:, j * TC + c * 128:j * TC + (c + 1) * 128],
                                    idb[:]).then_inc(tM.h, 1) if i == 7 else
                        t.transpose(PBb[bb][:, i * 128:(i + 1) * 128],
                                    xc[:, j * TC + c * 128:j * TC + (c + 1) * 128],
                                    idb[:])))
                ntp = tM.inc()
                em('v', lambda v, n=ntp, c=c, rd=rd, bb=bb: (
                    v.wait_ge(tM.h, n),
                    v.tensor_copy(xcT[:, (c * 16 + rd * 8) * 128:(c * 16 + rd * 8 + 8) * 128],
                                  PBb[bb][:, 0:1024]).then_inc(vM.h, 1)))
                pbb_free[rd] = vM.inc()
            for rd in range(4):
                bb = 6 + rd % 2
                em('t', lambda t, pf=pbb_free[rd % 2], nbc=nbc: (
                    t.wait_ge(dS.h, nbc),
                    t.wait_ge(vM.h, pf) if pf else None))
                for i in range(8):
                    h = rd * 8 + i
                    em('t', lambda t, hb=hb, h=h, i=i, bb=bb: (
                        t.transpose(PBb[bb][:, i * 128:(i + 1) * 128],
                                    bcbuf[:, hb * 4096 + h * 128:hb * 4096 + (h + 1) * 128],
                                    idb[:]).then_inc(tM.h, 1) if i == 7 else
                        t.transpose(PBb[bb][:, i * 128:(i + 1) * 128],
                                    bcbuf[:, hb * 4096 + h * 128:hb * 4096 + (h + 1) * 128],
                                    idb[:])))
                ntp = tM.inc()
                em('v', lambda v, n=ntp, rd=rd, bb=bb: (
                    v.wait_ge(tM.h, n),
                    v.tensor_copy(bcTall[:, rd * 1024:(rd + 1) * 1024],
                                  PBb[bb][:, 0:1024]).then_inc(vM.h, 1)))
                pbb_free[rd % 2] = vM.inc()
            bcT_done = vM.n
            bcb_free[hb] = tM.n
            for g in range(4):
                bk = g % 4
                for i in range(8):
                    h = g * 8 + i
                    dh = h % 2
                    em('v', lambda v, c=c, h=h, dh=dh, df=dtxw_free[dh]: (
                        v.wait_ge(tM.h, df) if df else None,
                        v.tensor_scalar(dtxw[:, dh * 64:(dh + 1) * 64],
                                        xcT[:, (c * 16 + h // 2) * 128 + (h % 2) * 64:
                                            (c * 16 + h // 2) * 128 + (h % 2) * 64 + 64],
                                        W32[:, h:h + 1], None, ALU.mult).then_inc(vM.h, 1)))
                    nv = vM.inc()
                    em('t', lambda t, n=nv, h=h, dh=dh, bk=bk, i=i, dhf=dh_free[bk]: (
                        t.wait_ge(vM.h, n),
                        t.wait_ge(vM.h, dhf) if (i == 0 and dhf) else None,
                        t.matmul(PB[bk].ap()[:, (h % 8) * 64:(h % 8 + 1) * 64],
                                 bcTall[:, h * 128:(h + 1) * 128],
                                 dtxw[:, dh * 64:(dh + 1) * 64],
                                 start=True, stop=True).then_inc(tM.h, 1)))
                    dtxw_free[dh] = tM.inc()
                tdh = tM.n
                em('v', lambda v, n=tdh, s3=s3, g=g, bk=bk: (
                    v.wait_ge(tM.h, n),
                    v.wait_ge(sM.h, s3),
                    *[v.scalar_tensor_tensor(
                        hfull[:, (g * 8 + i2) * 64:(g * 8 + i2 + 1) * 64],
                        hfull[:, (g * 8 + i2) * 64:(g * 8 + i2 + 1) * 64],
                        lamb[:, g * 8 + i2:g * 8 + i2 + 1],
                        PB[bk].ap()[:, i2 * 64:(i2 + 1) * 64],
                        ALU.mult, ALU.add) for i2 in range(7)],
                    v.scalar_tensor_tensor(
                        hfull[:, (g * 8 + 7) * 64:(g * 8 + 8) * 64],
                        hfull[:, (g * 8 + 7) * 64:(g * 8 + 8) * 64],
                        lamb[:, g * 8 + 7:g * 8 + 8],
                        PB[bk].ap()[:, 7 * 64:8 * 64],
                        ALU.mult, ALU.add).then_inc(vM.h, 1)))
                dh_free[bk] = vM.inc()
            e_done = vM.n

        if phases < 3:
            _finish_dev(nc, em, EM, taps, tap_outs, locals())
            p2a.close()
            pxc.close()
            pk2.close()
            pxn.close()
            _run_blocks(st, nc, EM)
            return nc, tap_outs

        p2a.close()
        pxc.close()
        p2c = ExitStack()
        P2C = lambda n, s, d: p2c.enter_context(nc.sbuf_tensor(n, s, d))
        triuS = P2C("triuS", [128, 128], BF16)
        boS = P2C("boS", [128, 512], BF16)
        selS = P2C("selS", [32, 2048], BF16)
        mktS = P2C("mktS", [8, 8], F32)
        cmaskS = P2C("cmaskS", [1, 256], F32)
        sdtc = P2C("sdtc", [32, 1], F32)
        PS8 = P2C("PS8", [32, 8], F32)
        pse = P2C("pse", [32, 8], F32)
        sdtg = P2C("sdtg", [8, 32], F32)
        D8 = P2C("D8", [8, 32], F32)
        eflat = P2C("eflat", [1, 256], F32)
        pcf = P2C("pcf", [1, 256], F32)
        EbS = P2C("EbS", [128, 256], F32)
        PbS = P2C("PbS", [128, 256], F32)
        hin = P2C("hin", [128, 2048], F32)
        gHb = P2C("gHb", [128, 2 * 2048], BF16)
        ldflat = P2C("ldflat", [1, 4096], F32)
        ccbuf = P2C("ccbuf", [128, 2 * 4096], BF16)
        bcb2 = P2C("bcb2", [128, 4096], BF16)
        zbuf = P2C("zbuf", [128, 2048], BF16)
        ybf = P2C("ybf", [128, 2048], BF16)
        ysq = P2C("ysq", [128, 2048], BF16)
        yg = P2C("yg", [128, 2048], BF16)
        mtb = P2C("mtb", [128, 2 * 128], BF16)
        ebcb = P2C("ebcb", [128, 2 * 128], BF16)
        s2b = P2C("s2b", [128, 2 * 128], BF16)
        dtxb = P2C("dtxb", [128, 2 * 64], BF16)
        ccsb = P2C("ccsb", [128, 2 * 128], BF16)
        rstdf = P2C("rstdf", [32, 128], F32)
        rstdf2 = P2C("rstdf2", [32, 128], F32)
        arow = P2C("arow", [1, 32], F32)
        rstdb = P2C("rstdb", [32, 128], BF16)

        t2a, v2a, s2a, d2a = tM.n, vM.n, sM.n, dS.n
        first = True
        for dst, srcn in [(triuS, "TRIU"), (boS, "BO"), (selS, "SEL"),
                          (mktS, "MkT"), (cmaskS, "coremask")]:
            em('g', lambda g, d=dst, s=din[srcn], f=first: (
                (g.wait_ge(tM.h, t2a), g.wait_ge(vM.h, v2a),
                 g.wait_ge(sM.h, s2a), g.wait_ge(dS.h, d2a)) if f else None,
                g.dma_start(out=d[:], in_=s).then_inc(dW.h, 16)))
            dW.inc(16)
            first = False
        c2_consts = dW.n

        em('v', lambda v: (
            v.reduce_sum(sdtc[:], sdtT[0:32, 0:8], axis=mybir.AxisListType.X),
            v.memset(PS8[0:32, 0:1], 0.0),
            *[v.tensor_add(PS8[0:32, cc2:cc2 + 1], PS8[0:32, cc2 - 1:cc2],
                           sdtT[0:32, cc2 - 1:cc2]) for cc2 in range(1, 8)],
            v.tensor_scalar(PS8[:], PS8[:], Acol[:], None, ALU.mult).then_inc(vM.h, 1)))
        v2b = vM.inc()
        em('s', lambda s, n=v2b: (
            s.wait_ge(vM.h, n),
            s.activation(pse[:], PS8[:], AF.Exp).then_inc(sM.h, 1)))
        s2b_ = sM.inc()
        em('y', lambda y, n=s2b_: (
            y.wait_ge(sM.h, n),
            y.dma_start(out=ag_in_s, in_=sdtc[:]).then_inc(dS.h, 16),
            y.dma_start(out=pc_dram, in_=pse[:]).then_inc(dS.h, 16)))
        dS.inc(32)
        ag_s_d = dS.n
        em('g', lambda g, n=e_done: (
            g.wait_ge(vM.h, n),
            g.dma_start(out=ag_in_h, in_=hfull[:]).then_inc(dW.h, 16)))
        ag_h_d = dW.inc(16)
        em('g', lambda g, n1=ag_s_d, n2=ag_h_d: (
            g.wait_ge(dS.h, n1),
            g.wait_ge(dW.h, n2),
            g.collective_compute(
                "AllGather", ALU.bypass,
                replica_groups=[list(range(NCORES))],
                ins=[ag_in_h], outs=[ag_out_h]).then_inc(cS.h),
            g.collective_compute(
                "AllGather", ALU.bypass,
                replica_groups=[list(range(NCORES))],
                ins=[ag_in_s], outs=[ag_out_s]).then_inc(cS.h)))
        cS.inc(2)
        em('y', lambda y: (
            y.wait_ge(cS.h, 2),
            y.dma_start(out=sdtg[:],
                        in_=ag_out_s.rearrange("(r h) c -> r (h c)", h=32)
                        ).then_inc(dS.h, 16),
            y.dma_start(out=pcf[:],
                        in_=pc_dram.rearrange("a b -> (a b)").unsqueeze(0)
                        ).then_inc(dS.h, 16)))
        dS.inc(32)
        sdtg_d = dS.n
        em('t', lambda t, n=sdtg_d: (
            t.wait_ge(dS.h, n),
            t.matmul(PB[6].ap()[0:8, 0:32], mktS[:], sdtg[:],
                     start=True, stop=True).then_inc(tM.h, 1)))
        tD = tM.inc()
        em('s', lambda s, n=tD, nc2=c2_consts: (
            s.wait_ge(tM.h, n),
            s.wait_ge(dW.h, nc2),
            s.activation(D8[:], PB[6].ap()[0:8, 0:32], AF.Copy).then_inc(sM.h, 1)))
        sD8 = sM.inc()
        em('s', lambda s, n=sD8: (
            s.wait_ge(sM.h, n),
            s.dma_start(out=db_dram, in_=D8[:]).then_inc(dC.h, 16)))
        dD8 = dC.inc(16)
        em('y', lambda y, n=dD8: (
            y.wait_ge(dC.h, n),
            y.dma_start(out=eflat[:],
                        in_=db_dram.rearrange("a b -> (a b)").unsqueeze(0)
                        ).then_inc(dS.h, 16)))
        eflat_d = dS.inc(16)
        em('t', lambda t: t.transpose(PB[5].ap()[0:1, 0:32], Acol[:],
                                      idf[0:32, 0:32]).then_inc(tM.h, 1))
        tAr = tM.inc()
        em('v', lambda v, n=tAr, n2=eflat_d: (
            v.wait_ge(tM.h, n),
            v.wait_ge(dS.h, n2),
            v.tensor_copy(arow[:], PB[5].ap()[0:1, 0:32]),
            *[v.tensor_mul(eflat[0:1, 32 * jr:32 * (jr + 1)],
                           eflat[0:1, 32 * jr:32 * (jr + 1)], arow[:])
              for jr in range(7)],
            v.tensor_mul(eflat[0:1, 224:256], eflat[0:1, 224:256],
                         arow[:]).then_inc(vM.h, 1)))
        vAe = vM.inc()
        em('s', lambda s, n=vAe: (
            s.wait_ge(vM.h, n),
            s.activation(eflat[:], eflat[:], AF.Exp).then_inc(sM.h, 1)))
        sEf = sM.inc()
        em('v', lambda v, n=sEf: (
            v.wait_ge(sM.h, n),
            v.tensor_mul(eflat[:], eflat[:], cmaskS[:]).then_inc(vM.h, 1)))
        vEf = vM.inc()
        em('t', lambda t, n=vEf, n2=sdtg_d: (
            t.wait_ge(vM.h, n),
            t.wait_ge(dS.h, n2),
            t.matmul(PB[6].ap()[:, 0:256], ones1f[:], eflat[:], start=True, stop=True),
            t.matmul(PB[7].ap()[:, 0:256], ones1f[:], pcf[:],
                     start=True, stop=True).then_inc(tM.h, 1)))
        tEb = tM.inc()
        em('v', lambda v, n=tEb: (
            v.wait_ge(tM.h, n),
            v.tensor_copy(EbS[:], PB[6].ap()[:, 0:256]),
            v.tensor_copy(PbS[:], PB[7].ap()[:, 0:256]),
            v.memset(hin[:], 0.0).then_inc(vM.h, 1)))
        vEb = vM.inc()
        ghb_free = [vEb, vEb]
        for r in range(NCORES):
            sl = r % 2
            em('g', lambda g, r=r, sl=sl, gf=ghb_free[sl]: (
                g.wait_ge(vM.h, gf),
                g.dma_start(out=gHb[:, sl * 2048:(sl + 1) * 2048],
                            in_=ag_out_h[128 * r:128 * (r + 1), :]
                            ).then_inc(dW.h, 16)))
            ngh = dW.inc(16)
            em('v', lambda v, n=ngh, r=r, sl=sl: (
                v.wait_ge(dW.h, n),
                *[v.scalar_tensor_tensor(
                    hin[:, h2 * 64:(h2 + 1) * 64],
                    gHb[:, sl * 2048 + h2 * 64:sl * 2048 + (h2 + 1) * 64],
                    EbS[:, r * 32 + h2:r * 32 + h2 + 1],
                    hin[:, h2 * 64:(h2 + 1) * 64], ALU.mult, ALU.add)
                    for h2 in range(31)],
                v.scalar_tensor_tensor(
                    hin[:, 31 * 64:32 * 64],
                    gHb[:, sl * 2048 + 31 * 64:sl * 2048 + 32 * 64],
                    EbS[:, r * 32 + 31:r * 32 + 32],
                    hin[:, 31 * 64:32 * 64], ALU.mult, ALU.add).then_inc(vM.h, 1)))
            ghb_free[sl] = vM.inc()
        for c in range(NCH):
            em('v', lambda v, c=c: (
                *[v.scalar_tensor_tensor(
                    snaps[:, c * 2048 + h2 * 64:c * 2048 + (h2 + 1) * 64],
                    hin[:, h2 * 64:(h2 + 1) * 64],
                    PbS[:, h2 * 8 + c:h2 * 8 + c + 1],
                    snaps[:, c * 2048 + h2 * 64:c * 2048 + (h2 + 1) * 64],
                    ALU.mult, ALU.add) for h2 in range(31)],
                v.scalar_tensor_tensor(
                    snaps[:, c * 2048 + 31 * 64:c * 2048 + 32 * 64],
                    hin[:, 31 * 64:32 * 64],
                    PbS[:, 31 * 8 + c:31 * 8 + c + 1],
                    snaps[:, c * 2048 + 31 * 64:c * 2048 + 32 * 64],
                    ALU.mult, ALU.add).then_inc(vM.h, 1)))
            vM.inc()
        corr_done = vM.n

        if phases < 4:
            _finish_dev(nc, em, EM, taps, tap_outs, locals())
            p2c.close()
            pk2.close()
            pxn.close()
            _run_blocks(st, nc, EM)
            return nc, tap_outs

        chunk_done = [{'t': 0, 'v': 0, 's': 0}, {'t': 0, 'v': 0, 's': 0}]
        bcb2_free = 0
        zbuf_free = 0
        ldflat_free = 0
        mp_free = [(None, 0), (None, 0)]
        sa_free = [0, 0]
        y_bank_free = [0, 0, 0, 0]
        rot_free = [0, 0]
        yg_spill = 0
        prev_trms = prev_ygv = 0

        for c in range(NCH):
            hb = c % 2
            cdp = dict(chunk_done[hb])
            em('y', lambda y, c=c, hb=hb, cdp=cdp, cd=corr_done, bf=bcb2_free,
                   zf=zbuf_free, lf=ldflat_free: (
                y.wait_ge(vM.h, cd) if c == 0 else None,
                y.wait_ge(tM.h, cdp['t']) if cdp['t'] else None,
                y.wait_ge(vM.h, cdp['v']) if cdp['v'] else None,
                y.wait_ge(sM.h, cdp['s']) if cdp['s'] else None,
                y.dma_start(out=ccbuf[:, hb * 4096:(hb + 1) * 4096]
                            .rearrange("n (h t) -> n h t", h=32),
                            in_=cc_dram.rearrange("(h n) t -> n h t", n=128)
                            [:, :, c * 128:(c + 1) * 128]).then_inc(dS.h, 16),
                y.wait_ge(tM.h, bf) if bf else None,
                y.dma_start(out=bcb2[:, 0:4096]
                            .rearrange("n (h t) -> n h t", h=32),
                            in_=bc_dram.rearrange("(h n) t -> n h t", n=128)
                            [:, :, c * 128:(c + 1) * 128]).then_inc(dS.h, 16),
                y.wait_ge(vM.h, zf) if zf else None,
                y.dma_start(out=zbuf[:, 0:2048]
                            .rearrange("p (j t) -> p j t", j=16),
                            in_=z_dram.rearrange("(j p) t -> p j t", p=128)
                            [:, :, c * 128:(c + 1) * 128]).then_inc(dS.h, 16),
                y.wait_ge(tM.h, lf) if lf else None,
                y.dma_start(out=ldflat[:],
                            in_=ldb_dram[32 * c:32 * (c + 1), :]
                            .rearrange("a b -> (a b)").unsqueeze(0)
                            ).then_inc(dS.h, 16)))
            dS.inc(64)
            nld = dS.n

            for h in range(32):
                if DEBUG_2C_LVL < 2:
                    break
                rot = h % 2
                bk = h // 8
                pj = h // 2
                bb = (h // 4) % 2
                q = h % 4
                if q == 0:
                    em('t', lambda t, h=h, bb=bb, n=nld, mf=mp_free[bb]: (
                        t.wait_ge(dS.h, n) if h == 0 else None,
                        t.wait_ge({'s': sM, 'v': vM}[mf[0]].h, mf[1]) if mf[0] else None,
                        t.matmul(PB[6 + bb].ap()[:, 0:512], ones1f[:],
                                 ldflat[0:1, h * 128:(h + 4) * 128],
                                 start=True, stop=True).then_inc(tM.h, 1)))
                    nmp = tM.inc()
                em('s', lambda s, n=nmp, c=c, h=h, rot=rot, bb=bb, q=q,
                       sf=sa_free[rot]: (
                    s.wait_ge(tM.h, n),
                    ((s.wait_ge({'s': sM, 'v': vM}[sf[0]].h, sf[1])
                      if isinstance(sf, tuple) else s.wait_ge(vM.h, sf))
                     if sf else None),
                    (s.activation(mtb[:, rot * 128:(rot + 1) * 128],
                                  PB[6 + bb].ap()[:, q * 128:(q + 1) * 128], AF.Exp,
                                  bias=ldTneg_all[:, c * 32 + h:c * 32 + h + 1])
                     if DEBUG_2C_LVL >= 2.5 else
                     s.activation(mtb[:, rot * 128:(rot + 1) * 128],
                                  PB[6 + bb].ap()[:, q * 128:(q + 1) * 128], AF.Copy)),
                    (s.activation(ebcb[:, rot * 128:(rot + 1) * 128],
                                  PB[6 + bb].ap()[:, q * 128:(q + 1) * 128],
                                  AF.Exp) if DEBUG_2C_LVL >= 2.7 else
                     s.activation(ebcb[:, rot * 128:(rot + 1) * 128],
                                  PB[6 + bb].ap()[:, q * 128:(q + 1) * 128],
                                  AF.Copy)).then_inc(sM.h, 1)))
                nact = sM.inc()
                if q == 3:
                    mp_free[bb] = ('s', nact)
                em('v', lambda v, n=nact, rot=rot: (
                    v.wait_ge(sM.h, n),
                    v.tensor_mul(mtb[:, rot * 128:(rot + 1) * 128],
                                 mtb[:, rot * 128:(rot + 1) * 128],
                                 triuS[:]).then_inc(vM.h, 1)))
                nmt2 = vM.inc()
                if DEBUG_2C_LVL < 3:
                    continue
                em('t', lambda t, h=h, hb=hb, rot=rot, af=sa_free[rot], n=nld: (
                    t.wait_ge(dS.h, n) if h == 0 else None,
                    (t.wait_ge({'s': sM, 'v': vM}[af[0]].h, af[1]) if isinstance(af, tuple)
                     else t.wait_ge(vM.h, af)) if af else None,
                    t.matmul(PB[4 + rot].ap()[:, 0:128],
                             bcb2[:, h * 128:(h + 1) * 128],
                             ccbuf[:, hb * 4096 + h * 128:hb * 4096 + (h + 1) * 128],
                             start=True, stop=True).then_inc(tM.h, 1)))
                nsa = tM.inc()
                em('v', lambda v, n=nsa, n2=nmt2, rot=rot, c=c, h=h, hb=hb: (
                    v.wait_ge(tM.h, n),
                    v.tensor_tensor(s2b[:, rot * 128:(rot + 1) * 128],
                                    PB[4 + rot].ap()[:, 0:128],
                                    mtb[:, rot * 128:(rot + 1) * 128], ALU.mult),
                    v.tensor_scalar(dtxb[:, rot * 64:(rot + 1) * 64],
                                    xcT[:, (c * 16 + h // 2) * 128 + (h % 2) * 64:
                                        (c * 16 + h // 2) * 128 + (h % 2) * 64 + 64],
                                    dtT_all[:, c * 32 + h:c * 32 + h + 1],
                                    None, ALU.mult),
                    v.tensor_tensor(ccsb[:, rot * 128:(rot + 1) * 128],
                                    ccbuf[:, hb * 4096 + h * 128:hb * 4096 + (h + 1) * 128],
                                    ebcb[:, rot * 128:(rot + 1) * 128],
                                    ALU.mult).then_inc(vM.h, 1)))
                sa_free[rot] = vM.inc()
                if DEBUG_2C_LVL < 4:
                    continue
                em('t', lambda t, n=sa_free[rot], h=h, c=c, rot=rot, bk=bk, pj=pj,
                       yf=y_bank_free[bk]: (
                    t.wait_ge(vM.h, n),
                    t.wait_ge(sM.h, yf) if (h % 8 == 0 and yf) else None,
                    t.matmul(PB[bk].ap()[64 * (h % 2):64 * (h % 2) + 64,
                                         (pj % 4) * 128:(pj % 4 + 1) * 128],
                             dtxb[:, rot * 64:(rot + 1) * 64],
                             s2b[:, rot * 128:(rot + 1) * 128],
                             start=True, stop=False),
                    t.matmul(PB[bk].ap()[64 * (h % 2):64 * (h % 2) + 64,
                                         (pj % 4) * 128:(pj % 4 + 1) * 128],
                             snaps[:, c * 2048 + h * 64:c * 2048 + (h + 1) * 64],
                             ccsb[:, rot * 128:(rot + 1) * 128],
                             start=False, stop=True).then_inc(tM.h, 1)))
                tM.inc()
                if h % 8 == 7:
                    tyb = tM.n
                    em('s', lambda s, n=tyb, bk=bk, pt=prev_trms, pv=prev_ygv: (
                        s.wait_ge(tM.h, n),
                        s.wait_ge(tM.h, pt) if (bk == 0 and pt) else None,
                        s.wait_ge(vM.h, pv) if (bk == 0 and pv) else None,
                        s.activation(ybf[:, bk * 512:(bk + 1) * 512],
                                     PB[bk].ap()[:, 0:512], AF.Copy),
                        s.activation(ysq[:, bk * 512:(bk + 1) * 512],
                                     PB[bk].ap()[:, 0:512],
                                     AF.Square).then_inc(sM.h, 1)))
                    y_bank_free[bk] = sM.inc()
            ldflat_free = tM.n
            yev = sM.n
            if DEBUG_NO_EPI or DEBUG_2C_LVL < 9:
                prev_trms, prev_ygv = tM.n, vM.n
                chunk_done[hb] = {'t': tM.n, 'v': vM.n, 's': sM.n}
                bcb2_free = tM.n
                zbuf_free = 0
                continue
            em('t', lambda t, n=yev, af0=sa_free[0], af1=sa_free[1],
                   mf0=mp_free[0], mf1=mp_free[1]: (
                t.wait_ge(sM.h, n),
                ((t.wait_ge({'s': sM, 'v': vM}[af0[0]].h, af0[1]) if isinstance(af0, tuple)
                  else t.wait_ge(vM.h, af0)) if af0 else None),
                ((t.wait_ge({'s': sM, 'v': vM}[af1[0]].h, af1[1]) if isinstance(af1, tuple)
                  else t.wait_ge(vM.h, af1)) if af1 else None),
                t.wait_ge({'s': sM, 'v': vM}[mf0[0]].h, mf0[1]) if mf0[0] else None,
                t.wait_ge({'s': sM, 'v': vM}[mf1[0]].h, mf1[1]) if mf1[0] else None))
            for j in range(16):
                em('t', lambda t, j=j, bk=4 + j // 4, c0=(j % 4) * 128: (
                    t.matmul(PB[bk].ap()[0:32, c0:c0 + 128], boS[:, j * 32:(j + 1) * 32],
                             ysq[:, j * 128:(j + 1) * 128],
                             start=True, stop=True).then_inc(tM.h, 1)
                    if j == 15 else
                    t.matmul(PB[bk].ap()[0:32, c0:c0 + 128], boS[:, j * 32:(j + 1) * 32],
                             ysq[:, j * 128:(j + 1) * 128],
                             start=True, stop=True)))
            trms = tM.inc()
            em('v', lambda v, n=trms: (
                v.wait_ge(tM.h, n),
                v.tensor_copy(rstdf[:], PB[4].ap()[0:32, 0:128]),
                *[v.tensor_add(rstdf[:], rstdf[:],
                               PB[4 + jj // 4].ap()[0:32, (jj % 4) * 128:(jj % 4) * 128 + 128])
                  for jj in range(1, 15)],
                v.tensor_add(rstdf[:], rstdf[:],
                             PB[7].ap()[0:32, 3 * 128:4 * 128]).then_inc(vM.h, 1)))
            vsum = vM.inc()
            em('s', lambda s, n=vsum: (
                s.wait_ge(vM.h, n),
                s.activation(rstdf2[:], rstdf[:], AF.Ln,
                             bias=epsrms[:], scale=1.0 / 64.0),
                s.activation(rstdb[:], rstdf2[:], AF.Exp,
                             scale=-0.5).then_inc(sM.h, 1)))
            vrms = srms = sM.inc()
            sa_free[0] = ('v', vsum)
            sa_free[1] = ('v', vsum)
            em('s', lambda s, n=nld, hb=hb: (
                s.wait_ge(dS.h, n),
                s.activation(zbuf[:, 0:2048], zbuf[:, 0:2048],
                             AF.Silu).then_inc(sM.h, 1)))
            zsil = sM.inc()
            for j in range(16):
                rot = j % 2
                em('t', lambda t, j=j, rot=rot, n=vrms, rf=rot_free[rot],
                       mf0=mp_free[0], mf1=mp_free[1]: (
                    t.wait_ge(sM.h, n) if j == 0 else None,
                    t.wait_ge({'s': sM, 'v': vM}[mf0[0]].h, mf0[1])
                    if (j == 0 and mf0[0]) else None,
                    t.wait_ge({'s': sM, 'v': vM}[mf1[0]].h, mf1[1])
                    if (j == 0 and mf1[0]) else None,
                    t.wait_ge(vM.h, rf) if rf else None,
                    t.matmul(PB[6 + rot].ap()[:, 0:128], selS[:, j * 128:(j + 1) * 128],
                             rstdb[:], start=True, stop=True).then_inc(tM.h, 1)))
                nrb = tM.inc()
                em('v', lambda v, n=nrb, j=j, rot=rot, zs=zsil, hb=hb, ys=yg_spill: (
                    v.wait_ge(tM.h, n),
                    v.wait_ge(sM.h, zs) if j == 0 else None,
                    v.wait_ge(dC.h, ys) if (j == 0 and ys) else None,
                    v.tensor_tensor(yg[:, j * 128:(j + 1) * 128],
                                    ybf[:, j * 128:(j + 1) * 128],
                                    PB[6 + rot].ap()[:, 0:128], ALU.mult),
                    v.tensor_tensor(yg[:, j * 128:(j + 1) * 128],
                                    yg[:, j * 128:(j + 1) * 128],
                                    zbuf[:, j * 128:(j + 1) * 128],
                                    ALU.mult).then_inc(vM.h, 1)))
                rot_free[rot] = vM.inc()
            mp_free = [('v', rot_free[0]), ('v', rot_free[1])]
            ygv = vM.n
            em('s', lambda s, n=ygv, c=c, ys=yg_spill: (
                s.wait_ge(vM.h, n),
                s.wait_ge(dC.h, ys) if ys else None,
                s.dma_start(out=y_dram.rearrange("(j p) t -> p j t", p=128)
                            [:, :, c * 128:(c + 1) * 128],
                            in_=yg[:].rearrange("p (j t) -> p j t", j=16)
                            ).then_inc(dC.h, 16)))
            yg_spill = dC.inc(16)
            prev_trms, prev_ygv = trms, ygv
            bcb2_free = tM.n
            zbuf_free = ygv
            chunk_done[hb] = {'t': tM.n, 'v': vM.n, 's': sM.n}

        if phases < 5:
            _finish_dev(nc, em, EM, taps, tap_outs, locals())
            p2c.close()
            pk2.close()
            pxn.close()
            _run_blocks(st, nc, EM)
            return nc, tap_outs

        p2c.close()
        pk2.close()
        p2d = ExitStack()
        wb2 = p2d.enter_context(nc.sbuf_tensor("wb2", [128, 2 * 4096], BF16))
        yb2 = p2d.enter_context(nc.sbuf_tensor("yb2", [128, 16 * 1024], BF16))
        sob = p2d.enter_context(nc.sbuf_tensor("sob", [128, 2 * 1024], BF16))
        t2c, v2c, s2c, d2c = tM.n, vM.n, sM.n, dS.n
        em('y', lambda y, n=yg_spill: (
            y.wait_ge(dC.h, n), y.wait_ge(tM.h, t2c), y.wait_ge(vM.h, v2c),
            y.wait_ge(sM.h, s2c), y.wait_ge(dS.h, d2c)))
        em('g', lambda g, n=yg_spill: (
            g.wait_ge(dC.h, n), g.wait_ge(tM.h, t2c), g.wait_ge(vM.h, v2c),
            g.wait_ge(sM.h, s2c), g.wait_ge(dS.h, d2c)))
        for k in range(16):
            em('y', lambda y, k=k: y.dma_start(
                out=yb2[:, k * 1024:(k + 1) * 1024],
                in_=y_dram.rearrange("(j p) t -> j p t", p=128)[k]).then_inc(dS.h, 16))
            dS.inc(16)
        yb2_d = dS.n
        wb2_free = [0, 0]
        sb_free = [0, 0]
        bank_free2 = [0, 0, 0, 0]
        sl2 = 0
        for g2 in range(8):
            half = g2 % 2
            cols0 = g2 * 256
            for kg in range(4):
                em('g', lambda g, kg=kg, g2=g2, half=half, wf=wb2_free[half]: (
                    g.wait_ge(tM.h, wf) if (kg == 0 and wf) else None,
                    g.dma_start(
                        out=wb2[:, half * 4096 + kg * 1024:half * 4096 + (kg + 1) * 1024],
                        in_=din["Wp2"][(g2 * 4 + kg) * 128:(g2 * 4 + kg + 1) * 128, :]
                        ).then_inc(dW.h, 16)))
                dW.inc(16)
            nload = dW.n
            for mj2 in range(2):
                m = g2 * 2 + mj2
                sh = m % 2
                for tf in range(2):
                    b = sl2 % 4
                    sl2 += 1
                    em('t', lambda t, n=nload, bf=bank_free2[b], ny=yb2_d: (
                        t.wait_ge(dW.h, n),
                        t.wait_ge(dS.h, ny),
                        t.wait_ge(sM.h, bf) if bf else None))
                    for k in range(16):
                        kg, k2 = divmod(k, 4)
                        base = half * 4096 + kg * 1024 + k2 * 256 + mj2 * 128
                        em('t', lambda t, k=k, base=base, b=b, tf=tf: (
                            t.matmul(PB[b].ap()[:, 0:512], wb2[:, base:base + 128],
                                     yb2[:, k * 1024 + tf * 512:k * 1024 + (tf + 1) * 512],
                                     start=(k == 0), stop=(k == 15)).then_inc(tM.h, 1)
                            if k == 15 else
                            t.matmul(PB[b].ap()[:, 0:512], wb2[:, base:base + 128],
                                     yb2[:, k * 1024 + tf * 512:k * 1024 + (tf + 1) * 512],
                                     start=(k == 0), stop=(k == 15))))
                    ntm = tM.inc()
                    em('s', lambda s, n=ntm, b=b, sh=sh, tf=tf, sf=sb_free[sh]: (
                        s.wait_ge(tM.h, n),
                        s.wait_ge(dC.h, sf) if (tf == 0 and sf) else None,
                        s.activation(sob[:, sh * 1024 + tf * 512:sh * 1024 + (tf + 1) * 512],
                                     PB[b].ap()[:, 0:512], AF.Copy).then_inc(sM.h, 1)))
                    bank_free2[b] = sM.inc()
                em('s', lambda s, n=bank_free2[(sl2 - 1) % 4], m=m, sh=sh: (
                    s.wait_ge(sM.h, n),
                    s.dma_start(out=s_dram.rearrange("(j p) t -> j p t", p=128)[m],
                                in_=sob[:, sh * 1024:(sh + 1) * 1024]).then_inc(dC.h, 16)))
                sb_free[sh] = dC.inc(16)
            wb2_free[half] = tM.n
        s_done_d = dC.n

        p2d.close()
        p3a = ExitStack()
        tmp1f = p3a.enter_context(nc.sbuf_tensor("tmp1f", [128, 16 * TC], F32))
        btmp = p3a.enter_context(nc.sbuf_tensor("btmp", [128, TC], F32))
        t2d, v2d, s2d, d2d = tM.n, vM.n, sM.n, dS.n

        f1_bank_free = [0, 0, 0, 0]
        slc = 0
        em('t', lambda t: (t.wait_ge(sM.h, s2d), t.wait_ge(dC.h, s_done_d)))
        for j in range(DCH):
            for tf in range(2):
                b = slc % 4
                slc += 1
                em('t', lambda t, j=j, tf=tf, b=b, bf=f1_bank_free[b]: (
                    t.wait_ge(vM.h, bf) if bf else None,
                    t.matmul(PB[b].ap()[:, 0:512], h128su[:, j * 128:(j + 1) * 128],
                             xn[:, j * TH + 3 + tf * 512:j * TH + 3 + (tf + 1) * 512],
                             start=True, stop=True).then_inc(tM.h, 1)))
                nf = tM.inc()
                em('v', lambda v, n=nf, j=j, tf=tf, b=b, f=(j == 0 and tf == 0): (
                    v.wait_ge(tM.h, n),
                    v.wait_ge(sM.h, s2d) if f else None,
                    v.tensor_copy(tmp1f[:, j * TC + tf * 512:j * TC + (tf + 1) * 512],
                                  PB[b].ap()[:, 0:512]).then_inc(vM.h, 1)))
                f1_bank_free[b] = vM.inc()
        f1_done = vM.n

        if phases < 5.15:
            _finish_dev(nc, em, EM, taps, tap_outs, locals())
            p3a.close()
            pxn.close()
            _run_blocks(st, nc, EM)
            return nc, tap_outs

        for stg in range(DEBUG_BF_STAGES):
            stepj = 1 << stg
            last = (stg == 3)
            for a in range(DCH):
                if a & stepj:
                    continue
                bpt = a + stepj
                em('v', lambda v, a=a, bpt=bpt: (
                    v.tensor_sub(btmp[:], tmp1f[:, a * TC:(a + 1) * TC],
                                 tmp1f[:, bpt * TC:(bpt + 1) * TC]),
                    v.tensor_add(tmp1f[:, a * TC:(a + 1) * TC],
                                 tmp1f[:, a * TC:(a + 1) * TC],
                                 tmp1f[:, bpt * TC:(bpt + 1) * TC]),
                    v.tensor_copy(tmp1f[:, bpt * TC:(bpt + 1) * TC],
                                  btmp[:]).then_inc(vM.h, 1)))
                nbp = vM.inc()
                if last:
                    em('s', lambda s, a=a, bpt=bpt, n=nbp: (
                        s.wait_ge(vM.h, n),
                        s.activation(xn[:, a * TH:a * TH + TC],
                                     tmp1f[:, a * TC:(a + 1) * TC], AF.Copy),
                        s.activation(xn[:, bpt * TH:bpt * TH + TC],
                                     tmp1f[:, bpt * TC:(bpt + 1) * TC],
                                     AF.Copy).then_inc(sM.h, 1)))
                    sM.inc()
        bf_done = vM.n
        if DEBUG_BF_STAGES != 4:
            for j in range(DCH):
                em('s', lambda s, j=j, n=bf_done: (
                    s.wait_ge(vM.h, n) if j == 0 else None,
                    s.activation(xn[:, j * TH:j * TH + TC],
                                 tmp1f[:, j * TC:(j + 1) * TC],
                                 AF.Copy).then_inc(sM.h, 1)))
                sM.inc()
        fwht1_done = sM.n

        if phases < 5.3:
            _finish_dev(nc, em, EM, taps, tap_outs, locals())
            p3a.close()
            pxn.close()
            _run_blocks(st, nc, EM)
            return nc, tap_outs

        p3a.close()
        p3b = ExitStack()
        x2 = p3b.enter_context(nc.sbuf_tensor("x2", [128, ICH * TC], BF16))
        p3u = ExitStack()
        wbf = p3u.enter_context(nc.sbuf_tensor("wbf", [128, 2 * 4096], BF16))
        gts = p3u.enter_context(nc.sbuf_tensor("gts", [128, 2 * TC], BF16))
        ug_bank_free = [0, 0, 0, 0]
        wbf_free = [0, 0]
        gts_free = [0, 0]
        ug_slc = 0
        for m in range(ICH):
            half = m % 2
            em('g', lambda g, m=m, half=half, wf=wbf_free[half], f=(m == 0): (
                (g.wait_ge(sM.h, fwht1_done), g.wait_ge(dS.h, d2d),
                 g.wait_ge(vM.h, bf_done)) if f else None,
                g.wait_ge(tM.h, wf) if wf else None,
                g.dma_start(out=wbf[:, half * 4096:half * 4096 + 2048],
                            in_=din["WupP"][m * 128:(m + 1) * 128, :]).then_inc(dW.h, 16),
                g.dma_start(out=wbf[:, half * 4096 + 2048:half * 4096 + 4096],
                            in_=din["WgtP"][m * 128:(m + 1) * 128, :]).then_inc(dW.h, 16)))
            nload = dW.inc(32)
            if DEBUG_UG_LVL < 0.8:
                continue
            for tf in range(2):
                b0 = ug_slc % 4
                ug_slc += 1
                em('t', lambda t, n=nload, b0=b0, bf=ug_bank_free[b0], tf=tf, f1=fwht1_done: (
                    (t.wait_ge(dW.h, n), t.wait_ge(sM.h, f1)) if tf == 0 else None,
                    (t.wait_ge({'s': sM, 'v': vM}[bf[0]].h, bf[1]) if isinstance(bf, tuple)
                     else t.wait_ge(vM.h, bf)) if bf else None))
                for k in range(16):
                    em('t', lambda t, k=k, m=m, half=half, tf=tf, b0=b0: (
                        t.matmul(PB[b0].ap()[:, 0:512],
                                 wbf[:, half * 4096 + k * 128:half * 4096 + (k + 1) * 128],
                                 xn[:, k * TH + tf * 512:k * TH + (tf + 1) * 512],
                                 start=(k == 0), stop=(k == 15)).then_inc(tM.h, 1)
                        if k == 15 else
                        t.matmul(PB[b0].ap()[:, 0:512],
                                 wbf[:, half * 4096 + k * 128:half * 4096 + (k + 1) * 128],
                                 xn[:, k * TH + tf * 512:k * TH + (tf + 1) * 512],
                                 start=(k == 0), stop=(k == 15))))
                nup = tM.inc()
                if DEBUG_UG_LVL < 0.9:
                    continue
                em('v', lambda v, n=nup, m=m, tf=tf, b0=b0: (
                    v.wait_ge(tM.h, n),
                    (v.tensor_scalar(x2[:, m * TC + tf * 512:m * TC + (tf + 1) * 512],
                                     PB[b0].ap()[:, 0:512], upscale[:, m:m + 1],
                                     None, ALU.mult) if DEBUG_UG_LVL >= 2 else
                     v.tensor_copy(x2[:, m * TC + tf * 512:m * TC + (tf + 1) * 512],
                                   PB[b0].ap()[:, 0:512])).then_inc(vM.h, 1)))
                ug_bank_free[b0] = ('v', vM.inc())
            for tf in range(2 if DEBUG_UG_LVL >= 0.95 else 0):
                b1 = ug_slc % 4
                ug_slc += 1
                em('t', lambda t, b1=b1, bf=ug_bank_free[b1], tf=tf: (
                    (t.wait_ge({'s': sM, 'v': vM}[bf[0]].h, bf[1]) if isinstance(bf, tuple)
                     else t.wait_ge(vM.h, bf)) if bf else None))
                for k in range(16):
                    em('t', lambda t, k=k, m=m, half=half, tf=tf, b1=b1: (
                        t.matmul(PB[b1].ap()[:, 0:512],
                                 wbf[:, half * 4096 + 2048 + k * 128:
                                     half * 4096 + 2048 + (k + 1) * 128],
                                 xn[:, k * TH + tf * 512:k * TH + (tf + 1) * 512],
                                 start=(k == 0), stop=(k == 15)).then_inc(tM.h, 1)
                        if k == 15 else
                        t.matmul(PB[b1].ap()[:, 0:512],
                                 wbf[:, half * 4096 + 2048 + k * 128:
                                     half * 4096 + 2048 + (k + 1) * 128],
                                 xn[:, k * TH + tf * 512:k * TH + (tf + 1) * 512],
                                 start=(k == 0), stop=(k == 15))))
                ngt = tM.inc()
                em('s', lambda s, n=ngt, m=m, tf=tf, b1=b1, half=half,
                       gf=gts_free[half]: (
                    s.wait_ge(tM.h, n),
                    s.wait_ge(vM.h, gf) if (tf == 0 and gf) else None,
                    (s.activation(gts[:, half * TC + tf * 512:half * TC + (tf + 1) * 512],
                                  PB[b1].ap()[:, 0:512], AF.Silu,
                                  scale=gtscale[:, m:m + 1]) if DEBUG_UG_LVL >= 3 else
                     s.activation(gts[:, half * TC + tf * 512:half * TC + (tf + 1) * 512],
                                  PB[b1].ap()[:, 0:512],
                                  AF.Silu)).then_inc(sM.h, 1)))
                ug_bank_free[b1] = ('s', sM.inc())
            nsil = sM.n
            if DEBUG_UG_LVL >= 0.97:
                em('v', lambda v, n=nsil, m=m, half=half: (
                    v.wait_ge(sM.h, n),
                    v.tensor_tensor(x2[:, m * TC:(m + 1) * TC], x2[:, m * TC:(m + 1) * TC],
                                    gts[:, half * TC:(half + 1) * TC],
                                    ALU.mult).then_inc(vM.h, 1)))
                gts_free[half] = vM.inc()
            wbf_free[half] = tM.n
        ug_done_v = vM.n
        ug_done_s = sM.n
        ug_done_t = tM.n

        if phases < 5.5:
            _finish_dev(nc, em, EM, taps, tap_outs, locals())
            p3u.close()
            p3b.close()
            pxn.close()
            _run_blocks(st, nc, EM)
            return nc, tap_outs

        f2_bank_free = [0, 0, 0, 0]
        slc2 = 0
        em('t', lambda t, n=ug_done_v, n2=ug_done_s: (t.wait_ge(vM.h, n),
                                                      t.wait_ge(sM.h, n2)))
        for i2 in range(ICH):
            for tf in range(2):
                b = slc2 % 4
                slc2 += 1
                em('t', lambda t, i2=i2, tf=tf, b=b, bf=f2_bank_free[b]: (
                    t.wait_ge(sM.h, bf) if bf else None,
                    t.matmul(PB[b].ap()[:, 0:512], h128b[:],
                             x2[:, i2 * TC + tf * 512:i2 * TC + (tf + 1) * 512],
                             start=True, stop=True).then_inc(tM.h, 1)))
                nf = tM.inc()
                em('s', lambda s, n=nf, i2=i2, tf=tf, b=b: (
                    s.wait_ge(tM.h, n),
                    s.activation(x2[:, i2 * TC + tf * 512:i2 * TC + (tf + 1) * 512],
                                 PB[b].ap()[:, 0:512], AF.Copy).then_inc(sM.h, 1)))
                f2_bank_free[b] = sM.inc()
        f2_done = sM.n

        if phases < 5.7:
            _finish_dev(nc, em, EM, taps, tap_outs, locals())
            p3u.close()
            p3b.close()
            pxn.close()
            _run_blocks(st, nc, EM)
            return nc, tap_outs

        p3u.close()
        p3w = ExitStack()
        wdn = p3w.enter_context(nc.sbuf_tensor("wdn", [128, 2 * 8192], BF16))
        wdn_free = [0, 0]
        dn_bank_free = [0, 0, 0, 0]
        slc3 = 0
        for m in range(16):
            half = m % 2
            em('y', lambda y, m=m, half=half, wf=wdn_free[half], f=(m == 0),
                   tU=ug_done_t, vU=ug_done_v: (
                (y.wait_ge(sM.h, f2_done), y.wait_ge(dC.h, wtil_done_d),
                 y.wait_ge(tM.h, tU), y.wait_ge(vM.h, vU)) if f else None,
                y.wait_ge(tM.h, wf) if wf else None,
                *[y.dma_start(out=wdn[:, half * 8192 + kg * 2048:
                                      half * 8192 + (kg + 1) * 2048]
                              .rearrange("p (k c) -> p k c", k=16),
                              in_=wtil_dram[kg * 2048:(kg + 1) * 2048,
                                            m * 128:(m + 1) * 128]
                              .rearrange("(k p) c -> p k c", p=128)
                              ).then_inc(dS.h, 16) for kg in range(4)]))
            nload = dS.inc(64)
            for tf in range(2):
                b = slc3 % 4
                slc3 += 1
                em('t', lambda t, n=nload, b=b, bf=dn_bank_free[b], tf=tf,
                       f2=f2_done, f=(m == 0 and tf == 0): (
                    t.wait_ge(dS.h, n) if tf == 0 else None,
                    t.wait_ge(sM.h, f2) if f else None,
                    t.wait_ge(sM.h, bf) if bf else None))
                for k in range(64):
                    em('t', lambda t, k=k, half=half, tf=tf, b=b: (
                        t.matmul(PB[b].ap()[:, 0:512],
                                 wdn[:, half * 8192 + k * 128:half * 8192 + (k + 1) * 128],
                                 x2[:, k * TC + tf * 512:k * TC + (tf + 1) * 512],
                                 start=(k == 0), stop=(k == 63)).then_inc(tM.h, 1)
                        if k == 63 else
                        t.matmul(PB[b].ap()[:, 0:512],
                                 wdn[:, half * 8192 + k * 128:half * 8192 + (k + 1) * 128],
                                 x2[:, k * TC + tf * 512:k * TC + (tf + 1) * 512],
                                 start=(k == 0), stop=(k == 63))))
                ndn = tM.inc()
                em('s', lambda s, n=ndn, m=m, tf=tf, b=b: (
                    s.wait_ge(tM.h, n),
                    s.activation(xn[:, m * 1024 + tf * 512:m * 1024 + (tf + 1) * 512],
                                 PB[b].ap()[:, 0:512], AF.Silu,
                                 scale=dnscale[:, m:m + 1]).then_inc(sM.h, 1)))
                dn_bank_free[b] = sM.inc()
            wdn_free[half] = tM.n
        g_done = sM.n

        if phases < 5.9:
            _finish_dev(nc, em, EM, taps, tap_outs, locals())
            p3w.close()
            p3b.close()
            pxn.close()
            _run_blocks(st, nc, EM)
            return nc, tap_outs

        dn_t = tM.n
        p3w.close()
        p3b.close()
        p3f = ExitStack()
        xfin = p3f.enter_context(nc.sbuf_tensor("xfin", [128, 2 * TC], F32))
        sfin = p3f.enter_context(nc.sbuf_tensor("sfin", [128, 2 * TC], BF16))
        obuf = p3f.enter_context(nc.sbuf_tensor("obuf", [128, 2 * TC], F32))
        ofin_free = [0, 0]
        xf_free2 = [0, 0]
        for m in range(16):
            half = m % 2
            em('y', lambda y, m=m, half=half, xf=xf_free2[half], f=(m == 0): (
                (y.wait_ge(sM.h, g_done), y.wait_ge(tM.h, dn_t)) if f else None,
                y.wait_ge(vM.h, xf) if xf else None,
                y.dma_start(out=xfin[:, half * TC:(half + 1) * TC],
                            in_=din["xin"].rearrange("(c p) t -> c p t", p=128)
                            [m][:, 3:3 + TC]).then_inc(dS.h, 16),
                y.dma_start(out=sfin[:, half * TC:(half + 1) * TC],
                            in_=s_dram.rearrange("(c p) t -> c p t", p=128)[m]
                            ).then_inc(dS.h, 16)))
            nxl = dS.inc(32)
            em('v', lambda v, n=nxl, m=m, half=half, of=ofin_free[half]: (
                v.wait_ge(dS.h, n),
                v.wait_ge(dS.h, of) if of else None,
                v.tensor_tensor(obuf[:, half * TC:(half + 1) * TC],
                                sfin[:, half * TC:(half + 1) * TC],
                                xn[:, m * 1024:(m + 1) * 1024], ALU.mult),
                v.tensor_add(obuf[:, half * TC:(half + 1) * TC],
                             obuf[:, half * TC:(half + 1) * TC],
                             xfin[:, half * TC:(half + 1) * TC]).then_inc(vM.h, 1)))
            nob = vM.inc()
            xf_free2[half] = nob
            em('y', lambda y, n=nob, m=m, half=half: (
                y.wait_ge(vM.h, n),
                y.dma_start(out=out.rearrange("(c p) t -> c p t", p=128)[m],
                            in_=obuf[:, half * TC:(half + 1) * TC]).then_inc(dS.h, 16)))
            ofin_free[half] = dS.inc(16)
        out_done_d = dS.n
        em('y', lambda y, n=out_done_d: y.wait_ge(dS.h, n))

        _finish_dev(nc, em, EM, taps, tap_outs, locals())
        p3f.close()
        pxn.close()
        _run_blocks(st, nc, EM)
    return nc, tap_outs


def _finish_dev(nc, em, EM, taps, tap_outs, env):
    dW, dS, dC = env["dW"], env["dS"], env["dC"]
    tM, vM, sM = env["tM"], env["vM"], env["sM"]
    fin = [(dW, dW.n), (dS, dS.n), (dC, dC.n), (tM, tM.n), (vM, vM.n), (sM, sM.n)]
    em('g', lambda g, f=fin: [g.wait_ge(s.h, n) for s, n in f if n])
    em('y', lambda y, f=fin: [y.wait_ge(s.h, n) for s, n in f if n])
    srcs = {
        "bc_dram": (env["bc_dram"], [H * 128, TC]),
        "z_dram": (env["z_dram"], [D, TC]),
        "cc_dram": (env["cc_dram"], [H * 128, TC]),
        "y_dram": (env["y_dram"], [D, TC]),
        "s_dram": (env["s_dram"], [D, TC]),
        "wtil_dram": (env["wtil_dram"], [INTER, D]),
    }
    sb_srcs = {"xc": [128, 16 * TC], "dtt": [32, TC], "xn": [128, DCH * TH],
               "hfull": [128, 2048], "snaps": [128, NCH * 2048], "hin": [128, 2048],
               "xcT": [128, NCH * 16 * 128], "sdtT": [32, 8], "ybf": [128, 2048],
               "dtT_all": [128, NCH * 32], "ldT_all": [128, NCH * 32],
               "rstdb": [32, 128], "ysq": [128, 2048], "rstdf": [32, 128]}
    for name in taps:
        if name in srcs:
            srcap, shape = srcs[name]
            t = nc.dram_tensor("tap_" + name, shape, F32, kind="ExternalOutput").ap()
            em('g', lambda g, t=t, srcap=srcap: g.dma_start(out=t, in_=srcap).then_inc(dW.h, 16))
            dW.inc(16)
        elif name in sb_srcs:
            shape = sb_srcs[name]
            t = nc.dram_tensor("tap_" + name, shape, F32, kind="ExternalOutput").ap()
            em('g', lambda g, t=t, srct=env[name], shape=shape: g.dma_start(
                out=t, in_=srct[0:shape[0], 0:shape[1]]).then_inc(dW.h, 16))
            dW.inc(16)
        tap_outs[name] = "tap_" + name
    em('g', lambda g, n=dW.n: g.wait_ge(dW.h, n))


def _run_blocks(st, nc, EM):
    block = st.enter_context(nc.Block())

    @block.tensor
    def _(t):
        for fn in EM['t']:
            fn(t)

    @block.vector
    def _(v):
        for fn in EM['v']:
            fn(v)

    @block.scalar
    def _(s):
        for fn in EM['s']:
            fn(s)

    @block.gpsimd
    def _(g):
        for fn in EM['g']:
            fn(g)

    @block.sync
    def _(y):
        for fn in EM['y']:
            fn(y)


def kernel(**inputs):
    in_maps = _host_prep(inputs)
    nc, _ = _build(phases=DEBUG_PHASES, taps=DEBUG_TAPS)
    res = run_bass_kernel_spmd(nc, in_maps, core_ids=list(range(NCORES)))
    cols = [np.asarray(res.results[k]["out"]) for k in range(NCORES)]
    full = np.concatenate(cols, axis=1)
    return np.ascontiguousarray(full.T).reshape(1, 8192, 2048).astype(np.float32)
